# version 01
# baseline (speedup 1.0000x reference)
import math
import numpy as np
import ml_dtypes
from contextlib import ExitStack

import concourse.bass as bass
import concourse.mybir as mybir
from concourse.bass_utils import run_bass_kernel_spmd

F32 = mybir.dt.float32
BF16 = mybir.dt.bfloat16
I32 = mybir.dt.int32
AF = mybir.ActivationFunctionType
ALU = mybir.AluOpType
AX = mybir.AxisListType
BF = ml_dtypes.bfloat16

NCORE = 8
B, T, D = 2, 8192, 1024
NTOK = B * T
TL = NTOK // NCORE
NCH = D // 128
H, HD = 16, 64
EPS = 1e-5
F_FFN = 4096

ENGINES = ("sync", "vector", "scalar", "tensor", "gpsimd")
SERIAL = False
SER_LO = 0
SER_HI = 10 ** 9
RLX_LO = 0
RLX_HI = 0
RLX_ENG = ""


class Sched:

    def __init__(self, nc, es):
        self.nc = nc
        self.es = es
        self.ops = {e: [] for e in ENGINES}
        self.cnt = {}
        self.sems = {}

    def sem(self, *names):
        for name in names:
            self.sems[name] = self.es.enter_context(self.nc.semaphore(name))
            self.cnt[name] = 0

    def op(self, eng, fn, wait=(), inc=()):
        self.opi = getattr(self, "opi", 0) + 1
        relaxed = (RLX_LO <= self.opi < RLX_HI) and (
            RLX_ENG == "" or eng == RLX_ENG)
        if SERIAL or getattr(self, "serial_mode", False) or (
                SER_LO <= self.opi < SER_HI) or (
                RLX_LO < RLX_HI and not relaxed):
            wait = tuple(wait) + tuple((n, v) for n, v in self.cnt.items())
        self.ops[eng].append((fn, tuple(wait), tuple(inc)))
        for s, a in inc:
            self.cnt[s] += a
        return self.cnt

    def dma(self, eng, out, in_, sem="ld", wait=(), **kw):
        self.op(eng, lambda e, o=out, i=in_, k=kw: e.dma_start(out=o, in_=i, **k),
                wait=wait, inc=((sem, 16),))
        return self.cnt[sem]

    def run_engine(self, name, handle):
        for fn, waits, incs in self.ops[name]:
            for s, v in waits:
                handle.wait_ge(self.sems[s], v)
            r = fn(handle)
            for s, a in incs:
                r = r.then_inc(self.sems[s], a)

    def build_block(self):
        block = self.es.enter_context(self.nc.Block())
        for name in ENGINES:
            getattr(block, name)(lambda h, n=name: self.run_engine(n, h))

    def w(self, *names):
        return tuple((n, self.cnt[n]) for n in names)


def _mk(nc, es, name, shape, dtype):
    return es.enter_context(nc.sbuf_tensor(name, shape, dtype))


def _ps(nc, es, name, shape, dtype=F32):
    return es.enter_context(nc.psum_tensor(name, shape, dtype))



def emit_stats_1024(sch, chunks, onesb, pb, mu, q2, nt, sqs, xbs=None,
                    first_wait=(), nch=None):
    if nch is None:
        nch = len(chunks)
    bf = chunks[0][:, 0:1].dtype == BF16
    tiles = [(i * 512, min(512, nt - i * 512)) for i in range((nt + 511) // 512)]
    first = True
    for t0, tw in tiles:
        for c in range(nch):
            last = c == nch - 1
            if bf:
                sch.op("tensor", lambda e, c=c, t0=t0, tw=tw, last=last:
                       e.matmul(pb[0:1, 0:tw], onesb[:, 0:1],
                                chunks[c][:, t0:t0 + tw], start=(c == 0),
                                stop=last),
                       wait=(first_wait + sch.w("ve") if first
                             else sch.w("ve") if c == 0 else ()),
                       inc=((("mm", 1),) if last else ()))
            else:
                sch.op("vector", lambda e, c=c, t0=t0, tw=tw: e.tensor_copy(
                    xbs[:, 0:tw], chunks[c][:, t0:t0 + tw]),
                    wait=(first_wait + sch.w("mm") if first else sch.w("mm")),
                    inc=(("ve", 1),))
                sch.op("tensor", lambda e, tw=tw, c=c, last=last: e.matmul(
                    pb[0:1, 0:tw], onesb[:, 0:1], xbs[:, 0:tw],
                    start=(c == 0), stop=last),
                    wait=sch.w("ve"), inc=(("mm", 1),))
            first = False
        sch.op("vector", lambda e, t0=t0, tw=tw: e.tensor_scalar_mul(
            mu[0:1, t0:t0 + tw], pb[0:1, 0:tw], 1.0 / (nch * 128)),
            wait=sch.w("mm"), inc=(("ve", 1),))
    for t0, tw in tiles:
        for c in range(nch):
            last = c == nch - 1
            sch.op("scalar", lambda e, c=c, t0=t0, tw=tw: e.activation(
                sqs[:, 0:tw], chunks[c][:, t0:t0 + tw], AF.Square),
                wait=sch.w("mm"), inc=(("ac", 1),))
            sch.op("tensor", lambda e, tw=tw, c=c, last=last: e.matmul(
                pb[0:1, 0:tw], onesb[:, 0:1], sqs[:, 0:tw],
                start=(c == 0), stop=last),
                wait=sch.w("ac", "ve"), inc=(("mm", 1),))
        sch.op("vector", lambda e, t0=t0, tw=tw: e.tensor_scalar_mul(
            q2[0:1, t0:t0 + tw], pb[0:1, 0:tw], 1.0 / (nch * 128)),
            wait=sch.w("mm"), inc=(("ve", 1),))


def emit_rows_rstd(sch, mu, q2, rr, s0, s1, nt, double_norm, epsc=None, r1_out=None):
    sch.op("vector", lambda e: e.tensor_mul(s0[0:1, 0:nt], mu[0:1, 0:nt],
                                            mu[0:1, 0:nt]))
    sch.op("vector", lambda e: e.tensor_sub(s0[0:1, 0:nt], q2[0:1, 0:nt],
                                            s0[0:1, 0:nt]), inc=(("ve", 1),))
    sch.op("scalar", lambda e: e.activation(s1[0:1, 0:nt], s0[0:1, 0:nt],
                                            AF.Sqrt, bias=epsc[0:1, 0:1]),
           wait=sch.w("ve"), inc=(("ac", 1),))
    tgt = r1_out if (double_norm and r1_out is not None) else rr
    sch.op("vector", lambda e, t=tgt: e.reciprocal(t[0:1, 0:nt], s1[0:1, 0:nt]),
           wait=sch.w("ac"), inc=(("ve", 1),))
    if double_norm:
        r1 = r1_out if r1_out is not None else rr
        sch.op("vector", lambda e, r1=r1: e.tensor_mul(
            s1[0:1, 0:nt], r1[0:1, 0:nt], r1[0:1, 0:nt]))
        sch.op("vector", lambda e: e.tensor_mul(s1[0:1, 0:nt], s0[0:1, 0:nt],
                                                s1[0:1, 0:nt]), inc=(("ve", 1),))
        sch.op("scalar", lambda e: e.activation(s1[0:1, 0:nt], s1[0:1, 0:nt],
                                                AF.Sqrt, bias=epsc[0:1, 0:1]),
               wait=sch.w("ve"), inc=(("ac", 1),))
        sch.op("vector", lambda e: e.reciprocal(s0[0:1, 0:nt], s1[0:1, 0:nt]),
               wait=sch.w("ac"))
        sch.op("vector", lambda e, r1=r1: e.tensor_mul(
            rr[0:1, 0:nt], r1[0:1, 0:nt], s0[0:1, 0:nt]), inc=(("ve", 1),))


def emit_ln_apply(sch, src_chunks, dst_chunks, rr, nmr, nt, p1, p2, tmp,
                  onesr, g_row=None, b_row=None, add_chunks=None,
                  dst_is_bf16=True):
    tiles = [(i * 512, min(512, nt - i * 512)) for i in range((nt + 511) // 512)]
    nchunks = len(src_chunks)
    for c in range(nchunks):
        for t0, tw in tiles:
            sch.op("tensor", lambda e, c=c, t0=t0, tw=tw: e.matmul(
                p1[:, 0:tw],
                (g_row[0:1, c * 128:(c + 1) * 128] if g_row is not None
                 else onesr[0:1, 0:128]),
                rr[0:1, t0:t0 + tw], start=True, stop=True),
                wait=sch.w("ve"))
            sch.op("tensor", lambda e, c=c, t0=t0, tw=tw: e.matmul(
                p2[:, 0:tw],
                (g_row[0:1, c * 128:(c + 1) * 128] if g_row is not None
                 else onesr[0:1, 0:128]),
                nmr[0:1, t0:t0 + tw], start=True, stop=(b_row is None)),
                inc=((("mm", 1),) if b_row is None else ()))
            if b_row is not None:
                sch.op("tensor", lambda e, c=c, tw=tw: e.matmul(
                    p2[:, 0:tw], b_row[0:1, c * 128:(c + 1) * 128],
                    onesr[0:1, 0:tw], start=False, stop=True),
                    inc=(("mm", 1),))
            sch.op("vector", lambda e, c=c, t0=t0, tw=tw: e.tensor_mul(
                tmp[:, 0:tw], src_chunks[c][:, t0:t0 + tw], p1[:, 0:tw]),
                wait=sch.w("mm"))
            if add_chunks is None:
                sch.op("vector", lambda e, c=c, t0=t0, tw=tw: e.tensor_add(
                    dst_chunks[c][:, t0:t0 + tw], tmp[:, 0:tw], p2[:, 0:tw]),
                    inc=(("ve", 1),))
            else:
                sch.op("vector", lambda e, c=c, t0=t0, tw=tw: e.tensor_add(
                    tmp[:, 0:tw], tmp[:, 0:tw], p2[:, 0:tw]))
                sch.op("vector", lambda e, c=c, t0=t0, tw=tw: e.tensor_add(
                    dst_chunks[c][:, t0:t0 + tw], add_chunks[c][:, t0:t0 + tw],
                    tmp[:, 0:tw]), inc=(("ve", 1),))


def emit_neg_murr(sch, mu, rr, nmr, nt):
    sch.op("vector", lambda e: e.scalar_tensor_tensor(
        nmr[0:1, 0:nt], mu[0:1, 0:nt], -1.0, rr[0:1, 0:nt],
        ALU.mult, ALU.mult), inc=(("ve", 1),))



def build_launch1():
    nc = bass.Bass("TRN2", target_bir_lowering=False, debug=False)
    xin = nc.dram_tensor("xin", [NCH, 128, TL], F32, kind="ExternalInput").ap()
    wq = nc.dram_tensor("wq", [D, D], F32, kind="ExternalInput").ap()
    wk = nc.dram_tensor("wk", [D, D], F32, kind="ExternalInput").ap()
    wv = nc.dram_tensor("wv", [D, D], F32, kind="ExternalInput").ap()
    bcols = nc.dram_tensor("bcols", [128, 24], F32, kind="ExternalInput").ap()
    g_row = nc.dram_tensor("g_row", [1, D], F32, kind="ExternalInput").ap()
    b_row = nc.dram_tensor("b_row", [1, D], F32, kind="ExternalInput").ap()
    qkv_out = nc.dram_tensor("qkv", [3, NCH, 128, TL], BF16,
                             kind="ExternalOutput").ap()

    with ExitStack() as es:
        sch = Sched(nc, es)
        sch.sem("ld", "lg", "mm", "ve", "ac", "st")
        XB = [_mk(nc, es, f"xb{c}", [128, TL], BF16) for c in range(NCH)]
        XN = [_mk(nc, es, f"xn{c}", [128, TL], BF16) for c in range(NCH)]
        SQS = _mk(nc, es, "sqs", [128, 512], BF16)
        WB = _mk(nc, es, "wb", [128, NCH * D], BF16)
        BIAS = _mk(nc, es, "bias", [128, 24], F32)
        GR = _mk(nc, es, "gr", [1, D], F32)
        BR = _mk(nc, es, "br", [1, D], F32)
        ONESB = _mk(nc, es, "onesb", [128, 1], BF16)
        EPSC = _mk(nc, es, "epsc", [128, 1], F32)
        ONESR = _mk(nc, es, "onesr", [1, 512], F32)
        MU = _mk(nc, es, "mu", [1, TL], F32)
        Q2 = _mk(nc, es, "q2", [1, TL], F32)
        RR = _mk(nc, es, "rr", [1, TL], F32)
        NMR = _mk(nc, es, "nmr", [1, TL], F32)
        RS0 = _mk(nc, es, "rs0", [1, TL], F32)
        RS1 = _mk(nc, es, "rs1", [1, TL], F32)
        TMP = _mk(nc, es, "tmp", [128, 512], F32)
        OUTS = _mk(nc, es, "outs", [128, TL], BF16)
        P0 = _ps(nc, es, "p0", [128, 512])
        P1 = _ps(nc, es, "p1", [128, 512])
        P2 = _ps(nc, es, "p2", [128, 512])

        for c in range(NCH):
            sch.dma("gpsimd", XB[c][:, :], xin[c, :, :], sem="lg")
        ld_x = sch.cnt["lg"]
        sch.dma("sync", BIAS[:, :], bcols[:, :])
        sch.dma("sync", GR[0:1, :], g_row[0:1, :])
        sch.dma("sync", BR[0:1, :], b_row[0:1, :])
        ld_r = sch.cnt["ld"]

        sch.op("vector", lambda e: e.memset(ONESB[:, :], 1.0))
        sch.op("vector", lambda e: e.memset(EPSC[:, :], EPS))
        sch.op("vector", lambda e: e.memset(ONESR[:, :], 1.0),
               inc=(("ve", 1),))
        casts_w = (("lg", ld_x),)

        emit_stats_1024(sch, XB, ONESB, P0, MU, Q2, TL, SQS,
                        first_wait=casts_w)
        emit_rows_rstd(sch, MU, Q2, RR, RS0, RS1, TL, double_norm=True, epsc=EPSC)
        emit_neg_murr(sch, MU, RR, NMR, TL)
        sch.op("tensor", lambda e: e.nop(), wait=(("ld", ld_r),))
        emit_ln_apply(sch, XB, XN, RR, NMR, TL, P1, P2, TMP, ONESR,
                      g_row=GR, b_row=BR)
        xn_done = sch.cnt["ve"]

        for wi, wdram in enumerate((wq, wk, wv)):
            prev_mm = sch.cnt["mm"]
            for kc in range(NCH):
                sch.dma("gpsimd", WB[:, kc * D:(kc + 1) * D],
                        wdram[kc * 128:(kc + 1) * 128, :], sem="lg",
                        wait=((("mm", prev_mm),) if (kc == 0 and wi > 0) else ()))
            ldw = sch.cnt["lg"]
            for dc in range(NCH):
                for ti in range(TL // 512):
                    t0 = ti * 512
                    for kc in range(NCH):
                        first = kc == 0 and dc == 0 and ti == 0
                        sch.op("tensor", lambda e, dc=dc, t0=t0, kc=kc: e.matmul(
                            P0[:, :],
                            WB[:, kc * D + dc * 128: kc * D + (dc + 1) * 128],
                            XN[kc][:, t0:t0 + 512],
                            start=(kc == 0), stop=(kc == NCH - 1)),
                            wait=((("lg", ldw), ("ve", xn_done)) if first
                                  else sch.w("ac") if kc == 0 else ()),
                            inc=((("mm", 1),) if kc == NCH - 1 else ()))
                    sch.op("scalar", lambda e, wi=wi, dc=dc, t0=t0: e.activation(
                        OUTS[:, t0:t0 + 512], P0[:, :], AF.Identity,
                        bias=BIAS[:, wi * NCH + dc: wi * NCH + dc + 1]),
                        wait=sch.w("mm"), inc=(("ac", 1),))
                st_prev = sch.cnt["st"]
                sch.dma("sync", qkv_out[wi, dc, :, :], OUTS[:, :], sem="st",
                        wait=sch.w("ac"))
                sch.op("scalar", lambda e: e.nop(),
                       wait=(("st", st_prev + 16),))

        sch.op("gpsimd", lambda e: e.nop(), wait=(("st", sch.cnt["st"]),))
        sch.build_block()
    return nc



def fm_slices(x_rows):
    out = []
    dx = x_rows.shape[1]
    for c in range(NCORE):
        s = x_rows[c * TL:(c + 1) * TL]
        out.append(np.ascontiguousarray(s.T).reshape(dx // 128, 128, TL))
    return out


def fm_join(slices):
    return np.concatenate([s.reshape(s.shape[0] * 128, TL) for s in slices],
                          axis=1)


_NC_CACHE = {}


def _get_nc(name, builder):
    if name not in _NC_CACHE:
        _NC_CACHE[name] = builder()
    return _NC_CACHE[name]


def kernel(**inputs):
    inp = {k: np.asarray(v) for k, v in inputs.items()}
    cores = list(range(NCORE))

    x = inp["L0_attn_x"].reshape(NTOK, D)
    xfm = fm_slices(x)
    bcols = np.zeros((128, 24), np.float32)
    for wi, b in enumerate((inp["L0_attn_bq"], inp["L0_attn_bk"],
                            inp["L0_attn_bv"])):
        bcols[:, wi * 8:(wi + 1) * 8] = b.reshape(8, 128).T
    nc1 = _get_nc("l1", build_launch1)
    im1 = [{"xin": xfm[c], "wq": inp["L0_attn_Wq"], "wk": inp["L0_attn_Wk"],
            "wv": inp["L0_attn_Wv"], "bcols": bcols,
            "g_row": np.ascontiguousarray(inp["L0_attn_ln_g"].reshape(1, D)),
            "b_row": np.ascontiguousarray(inp["L0_attn_ln_b"].reshape(1, D))}
           for c in cores]
    r1 = run_bass_kernel_spmd(nc1, im1, core_ids=cores)
    qf = fm_join([np.asarray(r1.results[c]["qkv"][0]) for c in cores])
    kf = fm_join([np.asarray(r1.results[c]["qkv"][1]) for c in cores])
    vf = fm_join([np.asarray(r1.results[c]["qkv"][2]) for c in cores])

    im2 = []
    for c in cores:
        q2 = np.stack([qf[c * 128:(c + 1) * 128, b * T:(b + 1) * T]
                       for b in range(2)])
        k2 = np.stack([kf[c * 128:(c + 1) * 128, b * T:(b + 1) * T]
                       for b in range(2)])
        v1 = np.zeros((2, 2, 128, 64, 65), BF)
        for b in range(2):
            for h in range(2):
                vh = vf[(2 * c + h) * 64:(2 * c + h + 1) * 64,
                        b * T:(b + 1) * T]
                vt = np.ascontiguousarray(vh.T)
                v1[b, h, :, :, 0:64] = vt.reshape(64, 128, 64).transpose(1, 0, 2)
                v1[b, h, :, :, 64] = BF(1.0)
        im2.append({"q2in": np.ascontiguousarray(q2),
                    "k2in": np.ascontiguousarray(k2), "v1in": v1})
    nc2 = _get_nc("l2", build_launch2)
    r2 = run_bass_kernel_spmd(nc2, im2, core_ids=cores)
    yfull = np.zeros((D, NTOK), BF)
    for c in cores:
        yo = np.asarray(r2.results[c]["yout"])
        for b in range(2):
            for h in range(2):
                yfull[(2 * c + h) * 64:(2 * c + h + 1) * 64,
                      b * T:(b + 1) * T] = yo[b, h]

    b1c = np.ascontiguousarray(inp["L0_ffn_fc1_bias"].reshape(32, 128).T)
    b2c = np.ascontiguousarray(inp["L0_ffn_fc2_bias"].reshape(8, 128).T)
    nc3 = _get_nc("l3", build_launch3)
    im3 = [{"xin": xfm[c],
            "yin": np.ascontiguousarray(
                yfull[:, c * TL:(c + 1) * TL].reshape(NCH, 128, TL)),
            "lg_row": np.ascontiguousarray(
                inp["L0_ffn_ln_gamma"].reshape(1, D)),
            "lb_row": np.ascontiguousarray(inp["L0_ffn_ln_beta"].reshape(1, D)),
            "fc1q": inp["L0_ffn_fc1_q"], "fc1s": inp["L0_ffn_fc1_scales"],
            "b1c": b1c, "fc2q": inp["L0_ffn_fc2_q"],
            "fc2s": inp["L0_ffn_fc2_scales"], "b2c": b2c} for c in cores]
    r3 = run_bass_kernel_spmd(nc3, im3, core_ids=cores)
    x2f = fm_join([np.asarray(r3.results[c]["x2out"]) for c in cores])
    xs2f = fm_join([np.asarray(r3.results[c]["xs2out"]) for c in cores])

    xs2in = np.stack([
        np.ascontiguousarray(xs2f[:, b * T:(b + 1) * T].reshape(NCH, 128, T))
        for b in range(2)])
    masks4 = np.zeros((4, 128, 512), np.float32)
    for dd in range(4):
        for p in range(128):
            c0 = 128 * dd + p
            if c0 < 512:
                masks4[dd, p, c0:] = 1.0
    sel2 = np.zeros((128, 2), np.float32)
    sel2[0:64, 0] = 1.0
    sel2[64:128, 1] = 1.0
    sel2t = np.ascontiguousarray(sel2.T)
    iden = np.eye(128, dtype=np.float32)
    w0blk = np.zeros((128, 128), np.float32)
    w0blk[0:64, 0:64] = inp["L1_attn_W0"]
    w0blk[64:128, 64:128] = inp["L1_attn_W0"]
    im4 = []
    for c in cores:
        hA, hB = 2 * c, 2 * c + 1
        cols = []
        for oi in range(3):
            for h in (hA, hB):
                cols.append(inp["L1_attn_Wproj"][:, oi * D + h * 64:
                                                 oi * D + (h + 1) * 64])
        wp_s = np.ascontiguousarray(np.concatenate(cols, axis=1))
        bp = np.zeros((128, 3), np.float32)
        lngb = np.zeros((128, 4), np.float32)
        for hh, h in enumerate((hA, hB)):
            lo = hh * 64
            for oi in range(3):
                bp[lo:lo + 64, oi] = inp["L1_attn_bproj"][oi * D + h * 64:
                                                          oi * D + (h + 1) * 64]
            lngb[lo:lo + 64, 0] = inp["L1_attn_ln_g"]
            lngb[lo:lo + 64, 1] = inp["L1_attn_ln_b"]
            lngb[lo:lo + 64, 2] = inp["L1_attn_gn_g"][h * 64:(h + 1) * 64]
            lngb[lo:lo + 64, 3] = inp["L1_attn_gn_b"][h * 64:(h + 1) * 64]
        im4.append({"xs2f": xs2in, "wp": wp_s, "bp": bp, "w0b": w0blk,
                    "lngb": lngb, "masks": masks4, "sel2": sel2,
                    "sel2t": sel2t, "iden": iden})
    nc4 = _get_nc("l4", build_launch4)
    r4 = run_bass_kernel_spmd(nc4, im4, core_ids=cores)
    ctxf = np.zeros((D, NTOK), BF)
    for c in cores:
        co = np.asarray(r4.results[c]["ctxn"])
        for b in range(2):
            ctxf[c * 128:(c + 1) * 128, b * T:(b + 1) * T] = co[b]

    im5 = []
    for c in cores:
        t0 = c * TL
        x2e = np.zeros((D, TE), np.float32)
        cte = np.zeros((D, TE), BF)
        x2e[:, 1:] = x2f[:, t0:t0 + TL]
        cte[:, 1:] = ctxf[:, t0:t0 + TL]
        if c not in (0, NCORE // 2):
            x2e[:, 0] = x2f[:, t0 - 1]
            cte[:, 0] = ctxf[:, t0 - 1]
        bm = (np.zeros((128, 1), np.float32) if c in (0, NCORE // 2)
              else np.ones((128, 1), np.float32))
        im5.append({
            "x2e": np.ascontiguousarray(x2e.reshape(NCH, 128, TE)),
            "ctxe": np.ascontiguousarray(cte.reshape(NCH, 128, TE)),
            "wout": inp["L1_attn_Wout"],
            "wrt": np.ascontiguousarray(inp["L1_ffn_W_r"].T),
            "wkt": np.ascontiguousarray(inp["L1_ffn_W_k"].T),
            "wvt": np.ascontiguousarray(inp["L1_ffn_W_v"].T),
            "bo_c": np.ascontiguousarray(inp["L1_attn_bout"].reshape(8, 128).T),
            "mur_c": np.ascontiguousarray(inp["L1_ffn_mu_r"].reshape(8, 128).T),
            "muk_c": np.ascontiguousarray(inp["L1_ffn_mu_k"].reshape(8, 128).T),
            "bmask": bm,
            "idenf": np.eye(128, dtype=np.float32),
        })
    nc5 = _get_nc("l5", build_launch5)
    r5 = run_bass_kernel_spmd(nc5, im5, core_ids=cores)
    out = np.concatenate([np.asarray(r5.results[c]["out"]) for c in cores],
                         axis=0)
    return out.reshape(B, T, D)



def build_launch2():
    nc = bass.Bass("TRN2", target_bir_lowering=False, debug=False)
    q2in = nc.dram_tensor("q2in", [2, 128, T], BF16, kind="ExternalInput").ap()
    k2in = nc.dram_tensor("k2in", [2, 128, T], BF16, kind="ExternalInput").ap()
    v1in = nc.dram_tensor("v1in", [2, 2, 128, 64, 65], BF16,
                          kind="ExternalInput").ap()
    yout = nc.dram_tensor("yout", [2, 2, 64, T], BF16, kind="ExternalOutput").ap()
    NKT = T // 128
    NTT = T // 512

    with ExitStack() as es:
        sch = Sched(nc, es)
        sch.sem("ld", "lg", "mm", "ve", "ac", "st")
        Q2 = _mk(nc, es, "q2", [128, T], BF16)
        K2 = _mk(nc, es, "k2", [128, T], BF16)
        V1 = [_mk(nc, es, f"v1{h}", [128, 64 * 65], BF16) for h in range(2)]
        SQT = _mk(nc, es, "sqt", [128, T], BF16)
        PA = [_mk(nc, es, f"pa{s}", [128, 512], BF16) for s in range(2)]
        PB = [_mk(nc, es, f"pb{s}", [128, 512], BF16) for s in range(2)]
        NQ = _mk(nc, es, "nq", [1, T], F32)
        SC = _mk(nc, es, "sc", [1, 16], F32)
        NB = _mk(nc, es, "nb", [128, 2], F32)
        OE = _mk(nc, es, "oe", [128, 512], F32)
        RE = _mk(nc, es, "re", [128, 512], F32)
        YH = [_mk(nc, es, f"yh{h}", [64, T], BF16) for h in range(2)]
        ONESB = _mk(nc, es, "onesb", [128, 1], BF16)
        ONESF = _mk(nc, es, "onesf", [128, 128], F32)
        PS_A = [_ps(nc, es, f"psa{s}", [128, 512]) for s in range(2)]
        PS_B = [_ps(nc, es, f"psb{s}", [128, 512]) for s in range(2)]
        PO_A = _ps(nc, es, "poa", [128, 512])
        PO_B = _ps(nc, es, "pob", [128, 512])
        PR = _ps(nc, es, "pr", [128, 512])

        sch.op("vector", lambda e: e.memset(ONESB[:, :], 1.0))
        sch.op("vector", lambda e: e.memset(ONESF[:, :], 1.0), inc=(("ve", 1),))

        for task in range(2):
            w_prev = sch.w("mm", "ac", "st")
            sch.dma("sync", Q2[:, :], q2in[task, :, :], wait=w_prev)
            sch.dma("sync", K2[:, :], k2in[task, :, :])
            for h in range(2):
                sch.dma("sync", V1[h][:, :],
                        v1in[task, h, :, :, :].rearrange("p a b -> p (a b)"))
            ld0 = sch.cnt["ld"]

            for which, SRC in enumerate((Q2, K2)):
                sch.op("scalar", lambda e, SRC=SRC: e.activation(
                    SQT[:, :], SRC[:, :], AF.Square),
                    wait=(("ld", ld0),) + sch.w("mm"), inc=(("ac", 1),))
                for h in range(2):
                    lo = 64 * h
                    for tt in range(NTT):
                        sch.op("tensor",
                               lambda e, lo=lo, tt=tt: e.matmul(
                                   PR[0:1, 0:512], ONESB[lo:lo + 64, 0:1],
                                   SQT[lo:lo + 64, tt * 512:(tt + 1) * 512],
                                   start=True, stop=True),
                               wait=sch.w("ac", "ve"), inc=(("mm", 1),))
                        sch.op("vector", lambda e, tt=tt: e.tensor_copy(
                            NQ[0:1, tt * 512:(tt + 1) * 512], PR[0:1, 0:512]),
                            wait=sch.w("mm"), inc=(("ve", 1),))
                    j = which * 4 + h
                    sch.op("vector", lambda e, j=j: e.reduce_max(
                        SC[0:1, j:j + 1], NQ[0:1, :], axis=AX.X),
                        inc=(("ve", 1),))
            sch.op("scalar", lambda e: e.activation(SC[0:1, 8:14], SC[0:1, 0:6],
                                                    AF.Sqrt),
                   wait=sch.w("ve"), inc=(("ac", 1),))
            for h in range(2):
                sch.op("vector", lambda e, h=h: e.tensor_mul(
                    SC[0:1, 12 + h:13 + h], SC[0:1, 8 + h:9 + h],
                    SC[0:1, 12 + h:13 + h]), wait=sch.w("ac"))
                sch.op("vector", lambda e, h=h: e.tensor_scalar_mul(
                    SC[0:1, 12 + h:13 + h], SC[0:1, 12 + h:13 + h], -0.125),
                    inc=(("ve", 1),))
                sch.op("tensor", lambda e, h=h: e.matmul(
                    PR[:, 0:1], ONESF[0:1, 0:128], SC[0:1, 12 + h:13 + h],
                    start=True, stop=True),
                    wait=sch.w("ve"), inc=(("mm", 1),))
                sch.op("vector", lambda e, h=h: e.tensor_copy(
                    NB[:, h:h + 1], PR[:, 0:1]),
                    wait=sch.w("mm"), inc=(("ve", 1),))
            bnd = sch.w("ve")

            for ti in range(NTT):
                t0 = ti * 512
                for kc in range(NKT):
                    sl = kc % 2
                    sch.op("tensor", lambda e, sl=sl, kc=kc, t0=t0: e.matmul(
                        PS_A[sl][:, :], K2[0:64, kc * 128:(kc + 1) * 128],
                        Q2[0:64, t0:t0 + 512], start=True, stop=True,
                        tile_position=(0, 0)),
                        wait=(bnd if (ti == 0 and kc == 0) else sch.w("ac", "ve")))
                    sch.op("tensor", lambda e, sl=sl, kc=kc, t0=t0: e.matmul(
                        PS_B[sl][:, :], K2[64:128, kc * 128:(kc + 1) * 128],
                        Q2[64:128, t0:t0 + 512], start=True, stop=True,
                        tile_position=(64, 0)),
                        inc=(("mm", 1),))
                    sch.op("scalar", lambda e, sl=sl: e.activation(
                        PA[sl][:, :], PS_A[sl][:, :], AF.Exp,
                        bias=NB[:, 0:1], scale=0.125),
                        wait=sch.w("mm"))
                    sch.op("scalar", lambda e, sl=sl: e.activation(
                        PB[sl][:, :], PS_B[sl][:, :], AF.Exp,
                        bias=NB[:, 1:2], scale=0.125),
                        inc=(("ac", 1),))
                    sch.op("tensor", lambda e, sl=sl, kc=kc: e.matmul(
                        PO_A[0:65, :], V1[0][:, kc * 65:(kc + 1) * 65],
                        PA[sl][:, :], start=(kc == 0), stop=(kc == NKT - 1)),
                        wait=sch.w("ac"))
                    sch.op("tensor", lambda e, sl=sl, kc=kc: e.matmul(
                        PO_B[0:65, :], V1[1][:, kc * 65:(kc + 1) * 65],
                        PB[sl][:, :], start=(kc == 0), stop=(kc == NKT - 1)),
                        inc=(("mm", 1),))
                for h, PO, Y in ((0, PO_A, YH[0]), (1, PO_B, YH[1])):
                    sch.op("vector", lambda e, PO=PO: e.tensor_copy(
                        OE[0:65, :], PO[0:65, :]), wait=sch.w("mm"))
                    sch.op("vector", lambda e: e.reciprocal(
                        RE[64:65, 0:512], OE[64:65, 0:512]), inc=(("ve", 1),))
                    sch.op("tensor", lambda e: e.matmul(
                        PR[0:64, :], ONESF[64:65, 0:64], RE[64:65, 0:512],
                        start=True, stop=True),
                        wait=sch.w("ve"), inc=(("mm", 1),))
                    sch.op("vector", lambda e, Y=Y, t0=t0: e.tensor_mul(
                        Y[0:64, t0:t0 + 512], OE[0:64, :], PR[0:64, :]),
                        wait=sch.w("mm"), inc=(("ve", 1),))
            for h in range(2):
                sch.dma("sync", yout[task, h, :, :], YH[h][0:64, :], sem="st",
                        wait=sch.w("ve"))

        sch.op("gpsimd", lambda e: e.nop(), wait=(("st", sch.cnt["st"]),))
        sch.build_block()
    return nc



def build_launch3():
    nc = bass.Bass("TRN2", target_bir_lowering=False, debug=False)
    xin = nc.dram_tensor("xin", [NCH, 128, TL], F32, kind="ExternalInput").ap()
    yin = nc.dram_tensor("yin", [NCH, 128, TL], BF16, kind="ExternalInput").ap()
    lg_row = nc.dram_tensor("lg_row", [1, D], F32, kind="ExternalInput").ap()
    lb_row = nc.dram_tensor("lb_row", [1, D], F32, kind="ExternalInput").ap()
    fc1q = nc.dram_tensor("fc1q", [D, F_FFN], I32, kind="ExternalInput").ap()
    fc1s = nc.dram_tensor("fc1s", [8, F_FFN], F32, kind="ExternalInput").ap()
    b1c = nc.dram_tensor("b1c", [128, 32], F32, kind="ExternalInput").ap()
    fc2q = nc.dram_tensor("fc2q", [F_FFN, D], I32, kind="ExternalInput").ap()
    fc2s = nc.dram_tensor("fc2s", [32, D], F32, kind="ExternalInput").ap()
    b2c = nc.dram_tensor("b2c", [128, 8], F32, kind="ExternalInput").ap()
    x2out = nc.dram_tensor("x2out", [NCH, 128, TL], F32, kind="ExternalOutput").ap()
    xs2out = nc.dram_tensor("xs2out", [NCH, 128, TL], BF16,
                            kind="ExternalOutput").ap()
    y1dram = nc.dram_tensor("y1dram", [32, 128, TL], BF16, kind="Internal").ap()

    with ExitStack() as es:
        sch = Sched(nc, es)
        sch.sem("ld", "lg", "mm", "ve", "ac", "st")
        X1 = [_mk(nc, es, f"x1{c}", [128, TL], F32) for c in range(NCH)]
        LG = _mk(nc, es, "lg", [1, D], F32)
        LB = _mk(nc, es, "lb", [1, D], F32)
        B1 = _mk(nc, es, "b1", [128, 32], F32)
        B2 = _mk(nc, es, "b2", [128, 8], F32)
        ONESB = _mk(nc, es, "onesb", [128, 1], BF16)
        ONESR = _mk(nc, es, "onesr", [1, 512], F32)
        EPSC = _mk(nc, es, "epsc", [128, 1], F32)
        MU = _mk(nc, es, "mu", [1, TL], F32)
        Q2 = _mk(nc, es, "q2", [1, TL], F32)
        RR = _mk(nc, es, "rr", [1, TL], F32)
        NMR = _mk(nc, es, "nmr", [1, TL], F32)
        R1 = _mk(nc, es, "r1", [1, TL], F32)
        NM1 = _mk(nc, es, "nm1", [1, TL], F32)
        RS0 = _mk(nc, es, "rs0", [1, TL], F32)
        RS1 = _mk(nc, es, "rs1", [1, TL], F32)
        SQS = _mk(nc, es, "sqs", [128, 512], BF16)
        XBS = _mk(nc, es, "xbs", [128, 512], BF16)
        TMP = _mk(nc, es, "tmp", [128, 512], F32)
        TMP2 = _mk(nc, es, "tmp2", [128, 512], F32)
        P0 = _ps(nc, es, "p0", [128, 512])
        P1 = _ps(nc, es, "p1", [128, 512])
        P2 = _ps(nc, es, "p2", [128, 512])

        sch.op("vector", lambda e: e.memset(ONESB[:, :], 1.0))
        sch.op("vector", lambda e: e.memset(EPSC[:, :], EPS))
        sch.op("vector", lambda e: e.memset(ONESR[:, :], 1.0), inc=(("ve", 1),))
        for c in range(NCH):
            sch.dma("sync", X1[c][:, :], xin[c, :, :])
        sch.dma("sync", LG[0:1, :], lg_row[0:1, :])
        sch.dma("sync", LB[0:1, :], lb_row[0:1, :])
        sch.dma("sync", B1[:, :], b1c[:, :])
        sch.dma("sync", B2[:, :], b2c[:, :])
        ld0 = sch.cnt["ld"]

        with ExitStack() as es2:
            Y = [_mk(nc, es2, f"y{c}", [128, TL], BF16) for c in range(NCH)]
            for c in range(NCH):
                sch.dma("gpsimd", Y[c][:, :], yin[c, :, :], sem="lg")
            ldy = sch.cnt["lg"]
            emit_stats_1024(sch, Y, ONESB, P0, MU, Q2, TL, SQS,
                            first_wait=(("lg", ldy), ("ld", ld0)))
            emit_rows_rstd(sch, MU, Q2, RR, RS0, RS1, TL, double_norm=False,
                           epsc=EPSC)
            emit_neg_murr(sch, MU, RR, NMR, TL)
            emit_ln_apply(sch, Y, X1, RR, NMR, TL, P1, P2, TMP, ONESR,
                          add_chunks=X1)

        emit_stats_1024(sch, X1, ONESB, P0, MU, Q2, TL, SQS, xbs=XBS)
        emit_rows_rstd(sch, MU, Q2, RR, RS0, RS1, TL, double_norm=True,
                       epsc=EPSC, r1_out=R1)
        emit_neg_murr(sch, MU, RR, NMR, TL)
        sch.op("vector", lambda e: e.tensor_mul(NM1[0:1, :], MU[0:1, :],
                                                R1[0:1, :]))
        sch.op("vector", lambda e: e.tensor_scalar_mul(NM1[0:1, :], NM1[0:1, :],
                                                       -1.0), inc=(("ve", 1),))

        with ExitStack() as es3:
            HB = [_mk(nc, es3, f"h{c}", [128, TL], BF16) for c in range(NCH)]
            W1 = _mk(nc, es3, "w1", [128, NCH * 1024], BF16)
            QB = _mk(nc, es3, "qb", [128, 1024], BF16)
            SCLF = _mk(nc, es3, "sclf", [128, 1024], F32)
            Y1T = _mk(nc, es3, "y1t", [128, TL], BF16)
            emit_ln_apply(sch, X1, HB, RR, NMR, TL, P1, P2, TMP, ONESR,
                          g_row=LG, b_row=LB)
            h_done = sch.cnt["ve"]
            for fh in range(4):
                f0 = fh * 1024
                for kc in range(NCH):
                    sch.dma("gpsimd", QB[:, :],
                            fc1q[kc * 128:(kc + 1) * 128, f0:f0 + 1024],
                            sem="lg", wait=sch.w("ve"))
                    sch.dma("sync", SCLF[:, :],
                            bass.AP(fc1s.tensor, kc * F_FFN + f0,
                                    [[0, 128], [1, 1024]]), wait=sch.w("ve"))
                    sch.op("vector", lambda e, kc=kc: e.scalar_tensor_tensor(
                        W1[:, kc * 1024:(kc + 1) * 1024], QB[:, :], -128.0,
                        SCLF[:, :], ALU.add, ALU.mult),
                        wait=(("lg", sch.cnt["lg"]), ("ld", sch.cnt["ld"]))
                        + (sch.w("mm") if kc == 0 else ()),
                        inc=(("ve", 1),))
                wdq = sch.cnt["ve"]
                for fc in range(8):
                    gfc = fh * 8 + fc
                    for ti in range(TL // 512):
                        t0 = ti * 512
                        for kc in range(NCH):
                            first = fc == 0 and ti == 0 and kc == 0
                            sch.op("tensor", lambda e, fc=fc, t0=t0, kc=kc:
                                   e.matmul(P0[:, :],
                                            W1[:, kc * 1024 + fc * 128:
                                               kc * 1024 + (fc + 1) * 128],
                                            HB[kc][:, t0:t0 + 512],
                                            start=(kc == 0), stop=(kc == NCH - 1)),
                                   wait=((("ve", max(wdq, h_done)),) if first
                                         else sch.w("ac") if kc == 0 else ()),
                                   inc=((("mm", 1),) if kc == NCH - 1 else ()))
                        sch.op("scalar", lambda e, gfc=gfc, t0=t0: e.activation(
                            Y1T[:, t0:t0 + 512], P0[:, :], AF.Silu,
                            bias=B1[:, gfc:gfc + 1]),
                            wait=sch.w("mm"), inc=(("ac", 1),))
                    stp = sch.cnt["st"]
                    sch.dma("sync", y1dram[gfc, :, :], Y1T[:, :], sem="st",
                            wait=sch.w("ac"))
                    sch.op("scalar", lambda e: e.nop(),
                           wait=(("st", stp + 16),))

        with ExitStack() as es4:
            UB = [_mk(nc, es4, f"u{c}", [128, TL], BF16) for c in range(NCH)]
            W2 = _mk(nc, es4, "w2", [128, 32 * 256], BF16)
            QB2 = _mk(nc, es4, "qb2", [128, 256], BF16)
            SCL2 = _mk(nc, es4, "scl2", [128, 256], F32)
            Y1R = _mk(nc, es4, "y1r", [128, 8 * 512], BF16)
            PZ = [_ps(nc, es4, f"pz{i}", [128, 512]) for i in range(2)]
            for dh in range(4):
                d0 = dh * 256
                for kc in range(32):
                    sch.dma("gpsimd", QB2[:, :],
                            fc2q[kc * 128:(kc + 1) * 128, d0:d0 + 256],
                            sem="lg", wait=sch.w("ve"))
                    sch.dma("sync", SCL2[:, :],
                            bass.AP(fc2s.tensor, kc * D + d0,
                                    [[0, 128], [1, 256]]), wait=sch.w("ve"))
                    sch.op("vector", lambda e, kc=kc: e.scalar_tensor_tensor(
                        W2[:, kc * 256:(kc + 1) * 256], QB2[:, :], -128.0,
                        SCL2[:, :], ALU.add, ALU.mult),
                        wait=(("lg", sch.cnt["lg"]), ("ld", sch.cnt["ld"]))
                        + (sch.w("mm") if kc == 0 else ()),
                        inc=(("ve", 1),))
                wdq2 = sch.cnt["ve"]
                for ti in range(TL // 512):
                    t0 = ti * 512
                    for sub in range(4):
                        for i in range(8):
                            kc = sub * 8 + i
                            sch.dma("sync", Y1R[:, i * 512:(i + 1) * 512],
                                    y1dram[kc, :, t0:t0 + 512], sem="ld",
                                    wait=(sch.w("mm") if i == 0 else ()))
                        ldyr = sch.cnt["ld"]
                        for dc in range(2):
                            for i in range(8):
                                kc = sub * 8 + i
                                sch.op("tensor", lambda e, dc=dc, i=i, sub=sub:
                                       e.matmul(PZ[dc][:, :],
                                                W2[:, (sub * 8 + i) * 256 + dc * 128:
                                                   (sub * 8 + i) * 256 + (dc + 1) * 128],
                                                Y1R[:, i * 512:(i + 1) * 512],
                                                start=(sub == 0 and i == 0),
                                                stop=(sub == 3 and i == 7)),
                                       wait=(((("ld", ldyr), ("ve", wdq2))
                                              + (sch.w("ac") if sub == 0 else ()))
                                             if i == 0 else ()),
                                       inc=((("mm", 1),) if i == 7 else ()))
                    for dc in range(2):
                        gdc = dh * 2 + dc
                        m = sch.w("mm", "ve")
                        sch.op("scalar", lambda e, dc=dc, gdc=gdc: e.activation(
                            TMP[:, :], PZ[dc][:, :], AF.Identity,
                            bias=B2[:, gdc:gdc + 1]),
                            wait=m, inc=(("ac", 1),))
                        sch.op("tensor", lambda e, t0=t0: e.matmul(
                            P1[:, :], ONESR[0:1, 0:128], R1[0:1, t0:t0 + 512],
                            start=True, stop=True), wait=m)
                        sch.op("tensor", lambda e, t0=t0: e.matmul(
                            P2[:, :], ONESR[0:1, 0:128], NM1[0:1, t0:t0 + 512],
                            start=True, stop=True), inc=(("mm", 1),))
                        sch.op("vector", lambda e, gdc=gdc, t0=t0: e.tensor_mul(
                            TMP2[:, :], X1[gdc][:, t0:t0 + 512], P1[:, :]),
                            wait=sch.w("mm", "ac"))
                        sch.op("vector", lambda e: e.tensor_add(
                            TMP2[:, :], TMP2[:, :], P2[:, :]))
                        sch.op("vector", lambda e, gdc=gdc, t0=t0:
                               e.scalar_tensor_tensor(
                                   UB[gdc][:, t0:t0 + 512], TMP[:, :], 0.5,
                                   TMP2[:, :], ALU.mult, ALU.add),
                               inc=(("ve", 1),))

            emit_stats_1024(sch, UB, ONESB, P0, MU, Q2, TL, SQS)
            emit_rows_rstd(sch, MU, Q2, RR, RS0, RS1, TL, double_norm=False,
                           epsc=EPSC)
            emit_neg_murr(sch, MU, RR, NMR, TL)
            emit_ln_apply(sch, UB, X1, RR, NMR, TL, P1, P2, TMP, ONESR,
                          add_chunks=X1)
        for c in range(NCH):
            sch.dma("sync", x2out[c, :, :], X1[c][:, :], sem="st",
                    wait=sch.w("ve"))
        with ExitStack() as es5:
            XS2 = [_mk(nc, es5, f"xs{c}", [128, TL], BF16) for c in range(NCH)]
            emit_stats_1024(sch, X1, ONESB, P0, MU, Q2, TL, SQS, xbs=XBS)
            emit_rows_rstd(sch, MU, Q2, RR, RS0, RS1, TL, double_norm=False,
                           epsc=EPSC)
            emit_neg_murr(sch, MU, RR, NMR, TL)
            emit_ln_apply(sch, X1, XS2, RR, NMR, TL, P1, P2, TMP, ONESR)
            for c in range(NCH):
                sch.dma("sync", xs2out[c, :, :], XS2[c][:, :], sem="st",
                        wait=(sch.w("ve") if c == 0 else ()))

        sch.op("gpsimd", lambda e: e.nop(), wait=(("st", sch.cnt["st"]),))
        sch.build_block()
    return nc



def build_launch4(dbg=False):
    nc = bass.Bass("TRN2", target_bir_lowering=False, debug=False)
    xs2f = nc.dram_tensor("xs2f", [2, NCH, 128, T], BF16, kind="ExternalInput").ap()
    wp = nc.dram_tensor("wp", [D, 384], F32, kind="ExternalInput").ap()
    bp = nc.dram_tensor("bp", [128, 3], F32, kind="ExternalInput").ap()
    w0b = nc.dram_tensor("w0b", [128, 128], F32, kind="ExternalInput").ap()
    lngb = nc.dram_tensor("lngb", [128, 4], F32, kind="ExternalInput").ap()
    masks = nc.dram_tensor("masks", [4, 128, 512], F32, kind="ExternalInput").ap()
    sel2 = nc.dram_tensor("sel2", [128, 2], F32, kind="ExternalInput").ap()
    sel2t = nc.dram_tensor("sel2t", [2, 128], F32, kind="ExternalInput").ap()
    iden = nc.dram_tensor("iden", [128, 128], F32, kind="ExternalInput").ap()
    ctxout = nc.dram_tensor("ctxn", [2, 128, T], BF16, kind="ExternalOutput").ap()
    if dbg:
        dbg_qkv = nc.dram_tensor("dbg_qkv", [3, 128, T], BF16,
                                 kind="ExternalOutput").ap()
        dbg_dz1 = nc.dram_tensor("dbg_dz1", [128, T], BF16,
                                 kind="ExternalOutput").ap()
        dbg_ctx = nc.dram_tensor("dbg_ctx", [128, T], BF16,
                                 kind="ExternalOutput").ap()
        dbg_z2 = nc.dram_tensor("dbg_z2", [128, 512], BF16,
                                kind="ExternalOutput").ap()
        dbg_ctx1 = nc.dram_tensor("dbg_ctx1", [128, T], BF16,
                                  kind="ExternalOutput").ap()
        dbg_gsc = nc.dram_tensor("dbg_gsc", [2, 128, 8], F32,
                                 kind="ExternalOutput").ap()
        dbg_pbs = nc.dram_tensor("dbg_pbs", [2, 128, 2], F32,
                                 kind="ExternalOutput").ap()
        dbg_lng = nc.dram_tensor("dbg_lng", [128, 4], F32,
                                 kind="ExternalOutput").ap()
    NTT = T // 512

    with ExitStack() as es:
        sch = Sched(nc, es)
        sch.sem("ld", "lg", "mm", "ve", "ac", "st")
        Q2 = _mk(nc, es, "q2", [128, T], BF16)
        K2 = _mk(nc, es, "k2", [128, T], BF16)
        V2 = _mk(nc, es, "v2", [128, T], BF16)
        DZ1 = _mk(nc, es, "dz1", [128, T], BF16)
        D1TA = _mk(nc, es, "d1ta", [128, (T // 128) * 128], BF16)
        D1TB = _mk(nc, es, "d1tb", [128, (T // 128) * 128], BF16)
        DSTA = _mk(nc, es, "dsta", [128, (T // 128) * 128], BF16)
        DSTB = _mk(nc, es, "dstb", [128, (T // 128) * 128], BF16)
        CTX = _mk(nc, es, "ctx", [128, T], BF16)
        XT = _mk(nc, es, "xt", [128, 8 * 512], BF16)
        WPB = _mk(nc, es, "wpb", [128, 8 * 384], BF16)
        W0B = _mk(nc, es, "w0bs", [128, 128], BF16)
        BP = _mk(nc, es, "bps", [128, 3], F32)
        LNG = _mk(nc, es, "lng", [128, 4], F32)
        MASK = _mk(nc, es, "masks_sb", [128, 4 * 512], F32)
        SEL2 = _mk(nc, es, "sel2s", [128, 2], BF16)
        S2TF = _mk(nc, es, "s2tf", [2, 128], F32)
        S2TB = _mk(nc, es, "s2tb", [2, 128], BF16)
        IDEN = _mk(nc, es, "idens", [128, 128], BF16)
        EPSC = _mk(nc, es, "epsc", [128, 1], F32)
        KKE = _mk(nc, es, "kke", [128, 2 * 512], BF16)
        ZT = _mk(nc, es, "zt", [128, 512], BF16)
        Z2T = _mk(nc, es, "z2t", [128, 512], BF16)
        ZHT = _mk(nc, es, "zht", [128, 512], BF16)
        DZHT = _mk(nc, es, "dzht", [128, 512], BF16)
        PRB = _mk(nc, es, "prb", [128, 512], BF16)
        SQB = _mk(nc, es, "sqb", [128, 512], BF16)
        DZNT = _mk(nc, es, "dznt", [128, 512], BF16)
        TS1 = _mk(nc, es, "ts1", [128, 512], F32)
        TS2 = _mk(nc, es, "ts2", [128, 512], F32)
        ROWA = _mk(nc, es, "rowa", [2, 512], F32)
        ROWB = _mk(nc, es, "rowb", [2, 512], F32)
        ROWC = _mk(nc, es, "rowc", [2, 512], F32)
        ROWD = _mk(nc, es, "rowd", [2, 512], F32)
        GSC = _mk(nc, es, "gsc", [128, 8], F32)
        PB_S = _ps(nc, es, "pbs", [128, 512])
        PB_A = _ps(nc, es, "pba", [128, 512])
        PB_B = _ps(nc, es, "pbb", [128, 512])
        PB_C = _ps(nc, es, "pbc", [128, 512])
        PKA = _ps(nc, es, "pka", [128, 512])
        PKB = _ps(nc, es, "pkb", [128, 512])
        PU = _ps(nc, es, "pu", [128, 512])
        PT = _ps(nc, es, "pt", [128, 128], BF16)

        sch.dma("gpsimd", WPB[:, :].rearrange("p (c n) -> p c n", c=NCH),
                wp[:, :].rearrange("(c p) n -> p c n", p=128), sem="lg")
        sch.dma("gpsimd", W0B[:, :], w0b[:, :], sem="lg")
        sch.dma("gpsimd", SEL2[:, :], sel2[:, :], sem="lg")
        sch.dma("gpsimd", S2TB[0:2, :], sel2t[0:2, :], sem="lg")
        sch.dma("gpsimd", IDEN[:, :], iden[:, :], sem="lg")
        sch.dma("sync", BP[:, :], bp[:, :])
        sch.dma("sync", LNG[:, :], lngb[:, :])
        sch.dma("sync", S2TF[0:2, :], sel2t[0:2, :])
        sch.dma("sync", MASK[:, :].rearrange("p (a b) -> p a b", a=4),
                masks[:, :, :].rearrange("a p b -> p a b"))
        ldc = sch.w("ld", "lg")
        sch.op("vector", lambda e: e.memset(EPSC[:, :], EPS))
        for Tn in (D1TA, D1TB, DSTA, DSTB):
            sch.op("vector", lambda e, Tn=Tn: e.memset(Tn[:, :], 0.0))
        sch.op("vector", lambda e: e.nop(), inc=(("ve", 1),))
        init_w = sch.w("ve")

        g_col = LNG[:, 0:1]
        b_col = LNG[:, 1:2]

        def bcast2(prow, psum, dt=F32, wait=()):
            sel = S2TF if dt == F32 else S2TB
            sch.op("tensor", lambda e, prow=prow, psum=psum, sel=sel: e.matmul(
                psum[:, 0:prow.shape[1]], sel[0:2, :], prow,
                start=True, stop=True), wait=wait, inc=(("mm", 1),))

        def stats2(src_ap, psum, wait=()):
            sch.op("tensor", lambda e, src_ap=src_ap, psum=psum: e.matmul(
                psum[0:2, 0:src_ap.shape[1]], SEL2[:, :], src_ap,
                start=True, stop=True), wait=wait, inc=(("mm", 1),))

        def ln_rows_tile(zsrc, w=512):
            stats2(zsrc, PB_S, wait=sch.w("ve", "ac"))
            sch.op("vector", lambda e, w=w: e.tensor_scalar_mul(
                ROWA[0:2, 0:w], PB_S[0:2, 0:w], 1.0 / 64),
                wait=sch.w("mm"), inc=(("ve", 1),))
            sch.op("scalar", lambda e, zsrc=zsrc: e.activation(
                SQB[:, 0:w], zsrc, AF.Square),
                wait=sch.w("ve"), inc=(("ac", 1),))
            stats2(SQB[:, 0:w], PB_S, wait=sch.w("ac", "ve"))
            sch.op("vector", lambda e, w=w: e.tensor_scalar_mul(
                ROWB[0:2, 0:w], PB_S[0:2, 0:w], 1.0 / 64),
                wait=sch.w("mm"))
            sch.op("vector", lambda e, w=w: e.tensor_mul(
                ROWC[0:2, 0:w], ROWA[0:2, 0:w], ROWA[0:2, 0:w]))
            sch.op("vector", lambda e, w=w: e.tensor_sub(
                ROWB[0:2, 0:w], ROWB[0:2, 0:w], ROWC[0:2, 0:w]),
                inc=(("ve", 1),))
            sch.op("scalar", lambda e, w=w: e.activation(
                ROWB[0:2, 0:w], ROWB[0:2, 0:w], AF.Sqrt, bias=EPSC[0:2, 0:1]),
                wait=sch.w("ve"), inc=(("ac", 1),))
            sch.op("vector", lambda e, w=w: e.reciprocal(
                ROWC[0:2, 0:w], ROWB[0:2, 0:w]), wait=sch.w("ac"))
            sch.op("vector", lambda e, w=w: e.scalar_tensor_tensor(
                ROWD[0:2, 0:w], ROWA[0:2, 0:w], -1.0, ROWC[0:2, 0:w],
                ALU.mult, ALU.mult), inc=(("ve", 1),))

        def dz_chain(zsrc, dz_dst_tile, dzn_tile, t0):
            ln_rows_tile(zsrc)
            bcast2(ROWC[0:2, 0:512], PB_C, wait=sch.w("ve"))
            bcast2(ROWD[0:2, 0:512], PB_A)
            sch.op("vector", lambda e, zsrc=zsrc: e.tensor_mul(
                TS1[:, :], zsrc, PB_C[:, :]), wait=sch.w("mm"))
            sch.op("vector", lambda e: e.tensor_add(
                ZHT[:, :], TS1[:, :], PB_A[:, :]), inc=(("ve", 1),))
            sch.op("vector", lambda e: e.tensor_scalar(
                TS1[:, :], ZHT[:, :], g_col, b_col, ALU.mult, ALU.add))
            sch.op("vector", lambda e, t0=t0: e.tensor_sub(
                TS1[:, :], TS1[:, :], V2[:, t0:t0 + 512]))
            sch.op("vector", lambda e: e.tensor_scalar(
                DZHT[:, :], TS1[:, :], g_col, None, ALU.mult),
                inc=(("ve", 1),))
            stats2(DZHT[:, :], PB_S, wait=sch.w("ve"))
            sch.op("vector", lambda e: e.tensor_scalar_mul(
                ROWA[0:2, :], PB_S[0:2, :], 1.0 / 64), wait=sch.w("mm"))
            sch.op("vector", lambda e: e.tensor_mul(
                PRB[:, :], DZHT[:, :], ZHT[:, :]), inc=(("ve", 1),))
            stats2(PRB[:, :], PB_S, wait=sch.w("ve"))
            sch.op("vector", lambda e: e.tensor_scalar_mul(
                ROWB[0:2, :], PB_S[0:2, :], 1.0 / 64),
                wait=sch.w("mm"), inc=(("ve", 1),))
            bcast2(ROWA[0:2, :], PB_A, wait=sch.w("ve"))
            bcast2(ROWB[0:2, :], PB_B)
            sch.op("vector", lambda e: e.tensor_mul(
                TS1[:, :], ZHT[:, :], PB_B[:, :]), wait=sch.w("mm"))
            sch.op("vector", lambda e: e.tensor_sub(
                TS2[:, :], DZHT[:, :], PB_A[:, :]))
            sch.op("vector", lambda e: e.tensor_sub(
                TS2[:, :], TS2[:, :], TS1[:, :]))
            if dz_dst_tile is not None:
                sch.op("vector", lambda e, d=dz_dst_tile: e.tensor_mul(
                    d, TS2[:, :], PB_C[:, :]))
            sch.op("vector", lambda e: e.tensor_mul(
                TS2[:, :], TS2[:, :], PB_C[:, :]))
            sch.op("vector", lambda e, d=dzn_tile: e.tensor_scalar_mul(
                d, TS2[:, :], -1.0), inc=(("ve", 1),))

        def transposes(src_tile, dstA, dstB, i):
            for jj in range(4):
                j = 4 * i + jj
                sch.op("tensor", lambda e, jj=jj: e.transpose(
                    PT[:, 0:64], src_tile[0:64, jj * 128:(jj + 1) * 128],
                    IDEN[0:64, 0:64]),
                    wait=sch.w("ve"), inc=(("mm", 1),))
                sch.op("vector", lambda e, j=j: e.tensor_copy(
                    dstA[:, j * 128:j * 128 + 64], PT[:, 0:64]),
                    wait=sch.w("mm"), inc=(("ve", 1),))
                sch.op("tensor", lambda e, jj=jj: e.transpose(
                    PT[:, 0:64], src_tile[64:128, jj * 128:(jj + 1) * 128],
                    IDEN[64:128, 64:128]),
                    wait=sch.w("ve"), inc=(("mm", 1),))
                sch.op("vector", lambda e, j=j: e.tensor_copy(
                    dstB[:, j * 128 + 64:j * 128 + 128], PT[:, 0:64]),
                    wait=sch.w("mm"), inc=(("ve", 1),))

        for task in range(2):
            prev = sch.w("mm", "ve", "ac", "st")
            for tt in range(NTT):
                t0 = tt * 512
                for kc in range(NCH):
                    sch.dma("sync", XT[:, kc * 512:(kc + 1) * 512],
                            xs2f[task, kc, :, t0:t0 + 512],
                            wait=(prev + ldc if (tt == 0 and kc == 0)
                                  else sch.w("mm") if kc == 0 else ()))
                ldt = sch.cnt["ld"]
                for oi, DST in ((0, Q2), (1, K2), (2, V2)):
                    for kc in range(NCH):
                        sch.op("tensor", lambda e, oi=oi, kc=kc: e.matmul(
                            PU[:, :], WPB[:, kc * 384 + oi * 128:
                                          kc * 384 + (oi + 1) * 128],
                            XT[:, kc * 512:(kc + 1) * 512],
                            start=(kc == 0), stop=(kc == NCH - 1)),
                            wait=((("ld", ldt),) + (init_w if tt == 0 else ())
                                  + sch.w("ac") if kc == 0 else ()),
                            inc=((("mm", 1),) if kc == NCH - 1 else ()))
                    sch.op("scalar", lambda e, oi=oi, DST=DST, t0=t0:
                           e.activation(DST[:, t0:t0 + 512], PU[:, :],
                                        AF.Identity, bias=BP[:, oi:oi + 1]),
                           wait=sch.w("mm"), inc=(("ac", 1),))
            proj_done = sch.w("ac")
            if dbg and task == 0:
                for oi, SRC in ((0, Q2), (1, K2), (2, V2)):
                    sch.dma("sync", dbg_qkv[oi, :, :], SRC[:, :], sem="st",
                            wait=proj_done)

            for i in range(NTT):
                t0 = i * 512
                sch.op("tensor", lambda e, t0=t0: e.matmul(
                    PU[:, :], W0B[:, :], K2[:, t0:t0 + 512],
                    start=True, stop=True),
                    wait=(proj_done if i == 0 else sch.w("ac")) + sch.w("ve"),
                    inc=(("mm", 1),))
                sch.op("scalar", lambda e: e.copy(ZT[:, :], PU[:, :]),
                       wait=sch.w("mm"), inc=(("ac", 1),))
                dz_chain(ZT[:, :], DZ1[:, t0:t0 + 512], DZNT[:, :], t0)
                transposes(DZNT, D1TA, D1TB, i)
                sch.op("tensor", lambda e, t0=t0: e.matmul(
                    PU[:, :], W0B[:, :], K2[:, t0:t0 + 512],
                    start=True, stop=False), wait=sch.w("ve"))
                for j in range(4 * i + 4):
                    d4 = j - 4 * i
                    for hh, PK, lo in ((0, PKA, 0), (1, PKB, 64)):
                        sch.op("tensor", lambda e, j=j, t0=t0, lo=lo, PK=PK:
                               e.matmul(PK[:, :],
                                        K2[lo:lo + 64, j * 128:(j + 1) * 128],
                                        K2[lo:lo + 64, t0:t0 + 512],
                                        start=True, stop=True,
                                        tile_position=(lo, 0)),
                               wait=(sch.w("ve") if hh == 0 else ()),
                               inc=((("mm", 1),) if hh == 1 else ()))
                    for hh, PK, DT in ((0, PKA, D1TA), (1, PKB, D1TB)):
                        sl = hh * 512
                        if d4 >= 0:
                            sch.op("vector", lambda e, sl=sl, PK=PK, d4=d4:
                                   e.tensor_mul(KKE[:, sl:sl + 512], PK[:, :],
                                                MASK[:, d4 * 512:(d4 + 1) * 512]),
                                   wait=sch.w("mm"),
                                   inc=((("ve", 1),) if hh == 1 else ()))
                        else:
                            sch.op("vector", lambda e, sl=sl, PK=PK:
                                   e.tensor_copy(KKE[:, sl:sl + 512], PK[:, :]),
                                   wait=sch.w("mm"),
                                   inc=((("ve", 1),) if hh == 1 else ()))
                    for hh, DT in ((0, D1TA), (1, D1TB)):
                        sl = hh * 512
                        sch.op("tensor", lambda e, j=j, DT=DT, sl=sl: e.matmul(
                            PU[:, :], DT[:, j * 128:(j + 1) * 128],
                            KKE[:, sl:sl + 512],
                            start=False, stop=(j == 4 * i + 3 and hh == 1)),
                            wait=(sch.w("ve") if hh == 0 else ()),
                            inc=((("mm", 1),) if hh == 1 else ()))
                sch.op("scalar", lambda e: e.copy(Z2T[:, :], PU[:, :]),
                       wait=sch.w("mm"), inc=(("ac", 1),))
                if dbg and task == 0 and i == 1:
                    sch.dma("sync", dbg_z2[:, :], Z2T[:, :], sem="st",
                            wait=sch.w("ac"))
                dz_chain(Z2T[:, :], None, DZNT[:, :], t0)
                sch.op("vector", lambda e, t0=t0: e.tensor_sub(
                    PRB[:, :], DZNT[:, :], DZ1[:, t0:t0 + 512]),
                    inc=(("ve", 1),))
                transposes(PRB, DSTA, DSTB, i)
                sch.op("tensor", lambda e, t0=t0: e.matmul(
                    PU[:, :], W0B[:, :], Q2[:, t0:t0 + 512],
                    start=True, stop=False), wait=sch.w("ve"))
                for j in range(4 * i + 4):
                    d4 = j - 4 * i
                    for hh, PK, lo in ((0, PKA, 0), (1, PKB, 64)):
                        sch.op("tensor", lambda e, j=j, t0=t0, lo=lo, PK=PK:
                               e.matmul(PK[:, :],
                                        K2[lo:lo + 64, j * 128:(j + 1) * 128],
                                        Q2[lo:lo + 64, t0:t0 + 512],
                                        start=True, stop=True,
                                        tile_position=(lo, 0)),
                               wait=(sch.w("ve") if hh == 0 else ()),
                               inc=((("mm", 1),) if hh == 1 else ()))
                    for hh, PK in ((0, PKA), (1, PKB)):
                        sl = hh * 512
                        if d4 >= 0:
                            sch.op("vector", lambda e, sl=sl, PK=PK, d4=d4:
                                   e.tensor_mul(KKE[:, sl:sl + 512], PK[:, :],
                                                MASK[:, d4 * 512:(d4 + 1) * 512]),
                                   wait=sch.w("mm"),
                                   inc=((("ve", 1),) if hh == 1 else ()))
                        else:
                            sch.op("vector", lambda e, sl=sl, PK=PK:
                                   e.tensor_copy(KKE[:, sl:sl + 512], PK[:, :]),
                                   wait=sch.w("mm"),
                                   inc=((("ve", 1),) if hh == 1 else ()))
                    for hh, DT in ((0, DSTA), (1, DSTB)):
                        sl = hh * 512
                        sch.op("tensor", lambda e, j=j, DT=DT, sl=sl: e.matmul(
                            PU[:, :], DT[:, j * 128:(j + 1) * 128],
                            KKE[:, sl:sl + 512],
                            start=False, stop=(j == 4 * i + 3 and hh == 1)),
                            wait=(sch.w("ve") if hh == 0 else ()),
                            inc=((("mm", 1),) if hh == 1 else ()))
                sch.op("scalar", lambda e, t0=t0: e.copy(
                    CTX[:, t0:t0 + 512], PU[:, :]),
                    wait=sch.w("mm"), inc=(("ac", 1),))

            if dbg and task == 0:
                sch.dma("sync", dbg_dz1[:, :], DZ1[:, :], sem="st",
                        wait=sch.w("ve", "ac"))
                sch.dma("sync", dbg_ctx[:, :], CTX[:, :], sem="st",
                        wait=sch.w("ve", "ac"))
            if dbg and task == 1:
                sch.dma("sync", dbg_ctx1[:, :], CTX[:, :], sem="st",
                        wait=sch.w("ve", "ac"))
            sch.serial_mode = True
            for i in range(NTT):
                t0 = i * 512
                sch.op("tensor", lambda e, t0=t0, i=i: e.matmul(
                    PB_A[0:2, 0:512], SEL2[:, :], CTX[:, t0:t0 + 512],
                    start=(i == 0), stop=(i == NTT - 1)),
                    wait=(sch.w("ac", "ve") if i == 0 else ()),
                    inc=((("mm", 1),) if i == NTT - 1 else ()))
            for i in range(NTT):
                t0 = i * 512
                sch.op("scalar", lambda e, t0=t0: e.activation(
                    SQB[:, :], CTX[:, t0:t0 + 512], AF.Square),
                    wait=sch.w("mm"), inc=(("ac", 1),))
                sch.op("tensor", lambda e, i=i: e.matmul(
                    PB_B[0:2, 0:512], SEL2[:, :], SQB[:, :],
                    start=(i == 0), stop=(i == NTT - 1)),
                    wait=sch.w("ac"), inc=(("mm", 1),))
            m = sch.w("mm")
            sch.op("vector", lambda e: e.reduce_sum(
                GSC[0:2, 0:1], PB_A[0:2, 0:512], axis=AX.X), wait=m)
            sch.op("vector", lambda e: e.reduce_sum(
                GSC[0:2, 1:2], PB_B[0:2, 0:512], axis=AX.X))
            NT64 = 1.0 / (64.0 * T)
            sch.op("vector", lambda e: e.tensor_scalar_mul(
                GSC[0:2, 0:1], GSC[0:2, 0:1], NT64))
            sch.op("vector", lambda e: e.tensor_scalar_mul(
                GSC[0:2, 1:2], GSC[0:2, 1:2], NT64))
            sch.op("vector", lambda e: e.tensor_mul(
                GSC[0:2, 2:3], GSC[0:2, 0:1], GSC[0:2, 0:1]))
            sch.op("vector", lambda e: e.tensor_sub(
                GSC[0:2, 1:2], GSC[0:2, 1:2], GSC[0:2, 2:3]),
                inc=(("ve", 1),))
            sch.op("scalar", lambda e: e.activation(
                GSC[0:2, 1:2], GSC[0:2, 1:2], AF.Sqrt, bias=EPSC[0:2, 0:1]),
                wait=sch.w("ve"), inc=(("ac", 1),))
            sch.op("vector", lambda e: e.reciprocal(
                GSC[0:2, 1:2], GSC[0:2, 1:2]), wait=sch.w("ac"),
                inc=(("ve", 1),))
            sch.op("tensor", lambda e: e.matmul(
                PB_S[:, 0:1], S2TF[0:2, :], GSC[0:2, 1:2],
                start=True, stop=True), wait=sch.w("ve"))
            sch.op("tensor", lambda e: e.matmul(
                PB_S[:, 1:2], S2TF[0:2, :], GSC[0:2, 0:1],
                start=False, stop=True), inc=(("mm", 1),))
            if dbg:
                sch.op("vector", lambda e: e.tensor_copy(
                    TS1[:, 0:2], PB_S[:, 0:2]), wait=sch.w("mm"),
                    inc=(("ve", 1),))
                sch.dma("sync", dbg_pbs[task, :, :], TS1[:, 0:2], sem="st",
                        wait=sch.w("ve"))
                if task == 0:
                    sch.dma("sync", dbg_lng[:, :], LNG[:, :], sem="st")
            sch.op("vector", lambda e: e.tensor_mul(
                GSC[:, 2:3], PB_S[:, 0:1], LNG[:, 2:3]), wait=sch.w("mm"))
            sch.op("vector", lambda e: e.tensor_mul(
                GSC[:, 3:4], PB_S[:, 1:2], GSC[:, 2:3]))
            sch.op("vector", lambda e: e.tensor_sub(
                GSC[:, 4:5], LNG[:, 3:4], GSC[:, 3:4]), inc=(("ve", 1),))
            gsc_done = sch.w("ve")
            if dbg:
                sch.dma("sync", dbg_gsc[task, :, 0:5], GSC[:, 0:5], sem="st",
                        wait=gsc_done)
            for i in range(NTT):
                t0 = i * 512
                sl = (i % 2) * 512
                stw = sch.cnt["st"] - 16 if i >= 2 else 0
                sch.op("scalar", lambda e, t0=t0, sl=sl: e.activation(
                    KKE[:, sl:sl + 512], CTX[:, t0:t0 + 512], AF.Identity,
                    bias=GSC[:, 4:5], scale=GSC[:, 2:3]),
                    wait=(gsc_done if i < 2 else (("st", stw),)),
                    inc=(("ac", 1),))
                sch.dma("sync", ctxout[task, :, t0:t0 + 512],
                        KKE[:, sl:sl + 512], sem="st", wait=sch.w("ac"))
            sch.serial_mode = False

        sch.op("gpsimd", lambda e: e.nop(), wait=(("st", sch.cnt["st"]),))
        sch.build_block()
    return nc



TE = TL + 1
T5TILES = [(0, 1), (1, 512), (513, 512), (1025, 512), (1537, 512)]


def build_launch5():
    nc = bass.Bass("TRN2", target_bir_lowering=False, debug=False)
    x2e = nc.dram_tensor("x2e", [NCH, 128, TE], F32, kind="ExternalInput").ap()
    ctxe = nc.dram_tensor("ctxe", [NCH, 128, TE], BF16, kind="ExternalInput").ap()
    wout = nc.dram_tensor("wout", [D, D], F32, kind="ExternalInput").ap()
    wrt = nc.dram_tensor("wrt", [D, D], F32, kind="ExternalInput").ap()
    wkt = nc.dram_tensor("wkt", [D, D], F32, kind="ExternalInput").ap()
    wvt = nc.dram_tensor("wvt", [D, D], F32, kind="ExternalInput").ap()
    bo_c = nc.dram_tensor("bo_c", [128, 8], F32, kind="ExternalInput").ap()
    mur_c = nc.dram_tensor("mur_c", [128, 8], F32, kind="ExternalInput").ap()
    muk_c = nc.dram_tensor("muk_c", [128, 8], F32, kind="ExternalInput").ap()
    bmask = nc.dram_tensor("bmask", [128, 1], F32, kind="ExternalInput").ap()
    idenf = nc.dram_tensor("idenf", [128, 128], F32, kind="ExternalInput").ap()
    out = nc.dram_tensor("out", [TL, D], F32, kind="ExternalOutput").ap()

    with ExitStack() as es:
        sch = Sched(nc, es)
        sch.sem("ld", "lg", "mm", "ve", "ac", "st")
        X3 = [_mk(nc, es, f"x3{c}", [128, TE], F32) for c in range(NCH)]
        WB = _mk(nc, es, "wb", [128, NCH * 128], BF16)
        BO = _mk(nc, es, "bo", [128, 8], F32)
        MUR = _mk(nc, es, "mur", [128, 8], F32)
        MUK = _mk(nc, es, "muk", [128, 8], F32)
        OMR = _mk(nc, es, "omr", [128, 8], F32)
        OMK = _mk(nc, es, "omk", [128, 8], F32)
        BM = _mk(nc, es, "bm", [128, 1], F32)
        ONESB = _mk(nc, es, "onesb", [128, 1], BF16)
        ONESR = _mk(nc, es, "onesr", [1, 512], F32)
        EPSC = _mk(nc, es, "epsc", [128, 1], F32)
        MU = _mk(nc, es, "mu", [1, TE], F32)
        Q2 = _mk(nc, es, "q2", [1, TE], F32)
        RR = _mk(nc, es, "rr", [1, TE], F32)
        NMR = _mk(nc, es, "nmr", [1, TE], F32)
        SQS = _mk(nc, es, "sqs", [128, 512], BF16)
        XBS = _mk(nc, es, "xbs", [128, 512], BF16)
        TMP = _mk(nc, es, "tmp", [128, 512], F32)
        P0 = _ps(nc, es, "p0", [128, 512])
        P1 = _ps(nc, es, "p1", [128, 512])
        P2 = _ps(nc, es, "p2", [128, 512])
        PT = _ps(nc, es, "pt", [128, 128])

        sch.op("vector", lambda e: e.memset(ONESB[:, :], 1.0))
        sch.op("vector", lambda e: e.memset(EPSC[:, :], EPS))
        sch.op("vector", lambda e: e.memset(ONESR[:, :], 1.0), inc=(("ve", 1),))
        for c in range(NCH):
            sch.dma("sync", X3[c][:, :], x2e[c, :, :])
        for name, dst, srcd in (("bo", BO, bo_c), ("mur", MUR, mur_c),
                                ("muk", MUK, muk_c), ("bm", BM, bmask)):
            sch.dma("sync", dst[:, :], srcd[:, :])
        ld0 = sch.cnt["ld"]
        sch.op("vector", lambda e: e.tensor_scalar(
            OMR[:, :], MUR[:, :], -1.0, 1.0, ALU.mult, ALU.add),
            wait=(("ld", ld0),))
        sch.op("vector", lambda e: e.tensor_scalar(
            OMK[:, :], MUK[:, :], -1.0, 1.0, ALU.mult, ALU.add),
            inc=(("ve", 1),))

        def proj(wdram, rhs_chunks, evac, nt_tiles, extra_wait=()):
            for dc in range(NCH):
                prev_mm = sch.cnt["mm"]
                for kc in range(NCH):
                    sch.dma("gpsimd", WB[:, kc * 128:(kc + 1) * 128],
                            wdram[kc * 128:(kc + 1) * 128,
                                  dc * 128:(dc + 1) * 128], sem="lg",
                            wait=((("mm", prev_mm),) if kc == 0 else ()))
                ldw = sch.cnt["lg"]
                for t0, tw in nt_tiles:
                    for kc in range(NCH):
                        first = t0 == nt_tiles[0][0] and kc == 0
                        sch.op("tensor", lambda e, t0=t0, tw=tw, kc=kc:
                               e.matmul(P0[:, 0:tw],
                                        WB[:, kc * 128:(kc + 1) * 128],
                                        rhs_chunks[kc][:, t0:t0 + tw],
                                        start=(kc == 0), stop=(kc == NCH - 1)),
                               wait=(((("lg", ldw),) + extra_wait + sch.w("ve"))
                                     if first else sch.w("ac") if kc == 0
                                     else ()),
                               inc=((("mm", 1),) if kc == NCH - 1 else ()))
                    evac(dc, t0, tw)

        with ExitStack() as es2:
            CTXS = [_mk(nc, es2, f"ctxs{c}", [128, TE], BF16) for c in range(NCH)]
            OB = [_mk(nc, es2, f"ob{c}", [128, TE], BF16) for c in range(NCH)]
            for c in range(NCH):
                sch.dma("gpsimd", CTXS[c][:, :], ctxe[c, :, :], sem="lg")
            ldc = sch.cnt["lg"]

            def evac_o(dc, t0, tw):
                sch.op("scalar", lambda e, dc=dc, t0=t0, tw=tw: e.activation(
                    OB[dc][:, t0:t0 + tw], P0[:, 0:tw], AF.Identity,
                    bias=BO[:, dc:dc + 1]),
                    wait=sch.w("mm"), inc=(("ac", 1),))

            proj(wout, CTXS, evac_o, T5TILES,
                 extra_wait=(("lg", ldc), ("ld", ld0)))
            emit_stats_1024(sch, OB, ONESB, P0, MU, Q2, TE, SQS,
                            first_wait=sch.w("ac"))
            emit_rows_rstd(sch, MU, Q2, RR, NMR, Q2, TE, double_norm=False,
                           epsc=EPSC)
            emit_neg_murr(sch, MU, RR, NMR, TE)
            emit_ln_apply(sch, OB, X3, RR, NMR, TE, P1, P2, TMP, ONESR,
                          add_chunks=X3)

        with ExitStack() as es3:
            XRK = [_mk(nc, es3, f"xrk{c}", [128, TL], BF16) for c in range(NCH)]
            R = [_mk(nc, es3, f"r{c}", [128, TL], BF16) for c in range(NCH)]
            with ExitStack() as es4:
                XS3 = [_mk(nc, es4, f"xs3{c}", [128, TE], BF16)
                       for c in range(NCH)]
                emit_stats_1024(sch, X3, ONESB, P0, MU, Q2, TE, SQS, xbs=XBS)
                emit_rows_rstd(sch, MU, Q2, RR, NMR, Q2, TE, double_norm=False,
                               epsc=EPSC)
                emit_neg_murr(sch, MU, RR, NMR, TE)
                emit_ln_apply(sch, X3, XS3, RR, NMR, TE, P1, P2, TMP, ONESR)
                for c in range(NCH):
                    sch.op("vector", lambda e, c=c: e.tensor_scalar(
                        SQS[:, 0:1], XS3[c][:, 0:1], BM[:, 0:1], None,
                        ALU.mult),
                        wait=(sch.w("ve") if c == 0 else ()))
                    sch.op("vector", lambda e, c=c: e.tensor_copy(
                        XS3[c][:, 0:1], SQS[:, 0:1]),
                        inc=((("ve", 1),) if c == NCH - 1 else ()))

                def mix(dst, mcol, ocol):
                    for c in range(NCH):
                        for t0 in range(0, TL, 512):
                            sch.op("vector", lambda e, c=c, t0=t0:
                                   e.tensor_scalar(
                                       dst[c][:, t0:t0 + 512],
                                       XS3[c][:, 1 + t0:1 + t0 + 512],
                                       mcol[:, c:c + 1], None, ALU.mult),
                                   wait=(sch.w("ve") if t0 == 0 and c == 0
                                         else ()))
                            sch.op("vector", lambda e, c=c, t0=t0:
                                   e.tensor_scalar(
                                       SQS[:, :], XS3[c][:, t0:t0 + 512],
                                       ocol[:, c:c + 1], None, ALU.mult))
                            sch.op("vector", lambda e, c=c, t0=t0:
                                   e.tensor_add(dst[c][:, t0:t0 + 512],
                                                dst[c][:, t0:t0 + 512],
                                                SQS[:, :]),
                                   inc=(("ve", 1),))

                mix(XRK, MUR, OMR)

                def evac_r(dc, t0, tw):
                    sch.op("scalar", lambda e, dc=dc, t0=t0, tw=tw:
                           e.activation(R[dc][:, t0:t0 + tw], P0[:, 0:tw],
                                        AF.Sigmoid),
                           wait=sch.w("mm"), inc=(("ac", 1),))

                tiles4 = [(i * 512, 512) for i in range(4)]
                proj(wrt, XRK, evac_r, tiles4)
                mix(XRK, MUK, OMK)
            with ExitStack() as es5:
                KA = [_mk(nc, es5, f"ka{c}", [128, TL], BF16) for c in range(NCH)]

                def evac_k(dc, t0, tw):
                    sch.op("scalar", lambda e, dc=dc, t0=t0, tw=tw:
                           e.activation(TMP[:, 0:tw], P0[:, 0:tw], AF.Relu),
                           wait=sch.w("mm"), inc=(("ac", 1),))
                    sch.op("scalar", lambda e, dc=dc, t0=t0, tw=tw:
                           e.activation(KA[dc][:, t0:t0 + tw], TMP[:, 0:tw],
                                        AF.Square),
                           inc=(("ac", 1),))

                proj(wkt, XRK, evac_k, tiles4)

                def evac_v(dc, t0, tw):
                    sch.op("vector", lambda e, dc=dc, t0=t0, tw=tw:
                           e.tensor_mul(XRK[dc][:, t0:t0 + tw],
                                        R[dc][:, t0:t0 + tw], P0[:, 0:tw]),
                           wait=sch.w("mm"), inc=(("ac", 1),))

                proj(wvt, KA, evac_v, tiles4)
                emit_stats_1024(sch, XRK, ONESB, P0, MU, Q2, TL, SQS)
                emit_rows_rstd(sch, MU, Q2, RR, NMR, Q2, TL, double_norm=False,
                               epsc=EPSC)
                emit_neg_murr(sch, MU, RR, NMR, TL)
                X3s = [X3[c][:, 1:TE] for c in range(NCH)]
                emit_ln_apply(sch, XRK, X3s, RR, NMR, TL, P1, P2, TMP, ONESR,
                              add_chunks=X3s)

        OSTG = _mk(nc, es, "ostg", [128, D], F32)
        IDF = _mk(nc, es, "idf", [128, 128], F32)
        sch.dma("sync", IDF[:, :], idenf[:, :])
        ldi = sch.cnt["ld"]
        sch.op("tensor", lambda e: e.nop(), wait=(("ld", ldi),))
        for j in range(TL // 128):
            for c in range(NCH):
                sch.op("tensor", lambda e, j=j, c=c: e.transpose(
                    PT[:, :], X3[c][:, 1 + j * 128:1 + (j + 1) * 128],
                    IDF[:, :]),
                    wait=sch.w("ve"), inc=(("mm", 1),))
                sch.op("vector", lambda e, c=c: e.tensor_copy(
                    OSTG[:, c * 128:(c + 1) * 128], PT[:, :]),
                    wait=sch.w("mm"), inc=(("ve", 1),))
            stp = sch.cnt["st"]
            sch.dma("sync", out[j * 128:(j + 1) * 128, :], OSTG[:, :],
                    sem="st", wait=sch.w("ve"))
            sch.op("vector", lambda e: e.nop(), wait=(("st", stp + 16),))

        sch.op("gpsimd", lambda e: e.nop(), wait=(("st", sch.cnt["st"]),))
        sch.build_block()
    return nc


# version 03
# speedup vs baseline: 2.1625x; 2.1625x over previous
import math
import numpy as np
import ml_dtypes
from contextlib import ExitStack

import concourse.bass as bass
import concourse.mybir as mybir
from concourse.bass_utils import run_bass_kernel_spmd

F32 = mybir.dt.float32
BF16 = mybir.dt.bfloat16
I32 = mybir.dt.int32
AF = mybir.ActivationFunctionType
ALU = mybir.AluOpType
AX = mybir.AxisListType
BF = ml_dtypes.bfloat16

NCORE = 8
B, T, D = 2, 8192, 1024
NTOK = B * T
TL = NTOK // NCORE
NCH = D // 128
H, HD = 16, 64
EPS = 1e-5
F_FFN = 4096

ENGINES = ("sync", "vector", "scalar", "tensor", "gpsimd")
SERIAL = False
T4DVE = False
T4LAG1 = False
T4TSER = False
T4TSER1 = False
T4K1DVE = False
T4SLK = False
T4DBG2 = False
T4SPL1 = False
T4TSER2 = False
SER_LO = 0
SER_HI = 0
RLX_LO = 0
RLX_HI = 0
RLX_ENG = ""


class Sched:

    def __init__(self, nc, es):
        self.nc = nc
        self.es = es
        self.ops = {e: [] for e in ENGINES}
        self.cnt = {}
        self.sems = {}

    def sem(self, *names):
        for name in names:
            self.sems[name] = self.es.enter_context(self.nc.semaphore(name))
            self.cnt[name] = 0

    def op(self, eng, fn, wait=(), inc=()):
        self.opi = getattr(self, "opi", 0) + 1
        relaxed = (RLX_LO <= self.opi < RLX_HI) and (
            RLX_ENG == "" or eng == RLX_ENG)
        if SERIAL or getattr(self, "serial_mode", False) or (
                SER_LO <= self.opi < SER_HI) or (
                RLX_LO < RLX_HI and not relaxed):
            wait = tuple(wait) + tuple((n, v) for n, v in self.cnt.items())
        self.ops[eng].append((fn, tuple(wait), tuple(inc)))
        for s, a in inc:
            self.cnt[s] += a
        return self.cnt

    def dma(self, eng, out, in_, sem="ld", wait=(), **kw):
        self.op(eng, lambda e, o=out, i=in_, k=kw: e.dma_start(out=o, in_=i, **k),
                wait=wait, inc=((sem, 16),))
        return self.cnt[sem]

    def run_engine(self, name, handle):
        for fn, waits, incs in self.ops[name]:
            for s, v in waits:
                handle.wait_ge(self.sems[s], v)
            r = fn(handle)
            for s, a in incs:
                r = r.then_inc(self.sems[s], a)

    def build_block(self):
        block = self.es.enter_context(self.nc.Block())
        for name in ENGINES:
            getattr(block, name)(lambda h, n=name: self.run_engine(n, h))

    def w(self, *names):
        return tuple((n, self.cnt[n]) for n in names)


def _mk(nc, es, name, shape, dtype):
    return es.enter_context(nc.sbuf_tensor(name, shape, dtype))


def _ps(nc, es, name, shape, dtype=F32):
    return es.enter_context(nc.psum_tensor(name, shape, dtype))



def emit_stats_1024(sch, chunks, onesb, pb, mu, q2, nt, sqs, xbs=None,
                    first_wait=(), nch=None):
    if nch is None:
        nch = len(chunks)
    bf = chunks[0][:, 0:1].dtype == BF16
    tiles = [(i * 512, min(512, nt - i * 512)) for i in range((nt + 511) // 512)]
    first = True
    for t0, tw in tiles:
        for c in range(nch):
            last = c == nch - 1
            if bf:
                sch.op("tensor", lambda e, c=c, t0=t0, tw=tw, last=last:
                       e.matmul(pb[0:1, 0:tw], onesb[:, 0:1],
                                chunks[c][:, t0:t0 + tw], start=(c == 0),
                                stop=last),
                       wait=(first_wait + sch.w("ve") if first
                             else sch.w("ve") if c == 0 else ()),
                       inc=((("mm", 1),) if last else ()))
            else:
                sch.op("vector", lambda e, c=c, t0=t0, tw=tw: e.tensor_copy(
                    xbs[:, 0:tw], chunks[c][:, t0:t0 + tw]),
                    wait=(first_wait + sch.w("mm") if first else sch.w("mm")),
                    inc=(("ve", 1),))
                sch.op("tensor", lambda e, tw=tw, c=c, last=last: e.matmul(
                    pb[0:1, 0:tw], onesb[:, 0:1], xbs[:, 0:tw],
                    start=(c == 0), stop=last),
                    wait=sch.w("ve"), inc=(("mm", 1),))
            first = False
        sch.op("vector", lambda e, t0=t0, tw=tw: e.tensor_scalar_mul(
            mu[0:1, t0:t0 + tw], pb[0:1, 0:tw], 1.0 / (nch * 128)),
            wait=sch.w("mm"), inc=(("ve", 1),))
    for t0, tw in tiles:
        for c in range(nch):
            last = c == nch - 1
            sch.op("scalar", lambda e, c=c, t0=t0, tw=tw: e.activation(
                sqs[:, 0:tw], chunks[c][:, t0:t0 + tw], AF.Square),
                wait=sch.w("mm"), inc=(("ac", 1),))
            sch.op("tensor", lambda e, tw=tw, c=c, last=last: e.matmul(
                pb[0:1, 0:tw], onesb[:, 0:1], sqs[:, 0:tw],
                start=(c == 0), stop=last),
                wait=sch.w("ac", "ve"), inc=(("mm", 1),))
        sch.op("vector", lambda e, t0=t0, tw=tw: e.tensor_scalar_mul(
            q2[0:1, t0:t0 + tw], pb[0:1, 0:tw], 1.0 / (nch * 128)),
            wait=sch.w("mm"), inc=(("ve", 1),))


def emit_rows_rstd(sch, mu, q2, rr, s0, s1, nt, double_norm, epsc=None, r1_out=None):
    sch.op("vector", lambda e: e.tensor_mul(s0[0:1, 0:nt], mu[0:1, 0:nt],
                                            mu[0:1, 0:nt]))
    sch.op("vector", lambda e: e.tensor_sub(s0[0:1, 0:nt], q2[0:1, 0:nt],
                                            s0[0:1, 0:nt]), inc=(("ve", 1),))
    sch.op("scalar", lambda e: e.activation(s1[0:1, 0:nt], s0[0:1, 0:nt],
                                            AF.Sqrt, bias=epsc[0:1, 0:1]),
           wait=sch.w("ve"), inc=(("ac", 1),))
    tgt = r1_out if (double_norm and r1_out is not None) else rr
    sch.op("vector", lambda e, t=tgt: e.reciprocal(t[0:1, 0:nt], s1[0:1, 0:nt]),
           wait=sch.w("ac"), inc=(("ve", 1),))
    if double_norm:
        r1 = r1_out if r1_out is not None else rr
        sch.op("vector", lambda e, r1=r1: e.tensor_mul(
            s1[0:1, 0:nt], r1[0:1, 0:nt], r1[0:1, 0:nt]))
        sch.op("vector", lambda e: e.tensor_mul(s1[0:1, 0:nt], s0[0:1, 0:nt],
                                                s1[0:1, 0:nt]), inc=(("ve", 1),))
        sch.op("scalar", lambda e: e.activation(s1[0:1, 0:nt], s1[0:1, 0:nt],
                                                AF.Sqrt, bias=epsc[0:1, 0:1]),
               wait=sch.w("ve"), inc=(("ac", 1),))
        sch.op("vector", lambda e: e.reciprocal(s0[0:1, 0:nt], s1[0:1, 0:nt]),
               wait=sch.w("ac"))
        sch.op("vector", lambda e, r1=r1: e.tensor_mul(
            rr[0:1, 0:nt], r1[0:1, 0:nt], s0[0:1, 0:nt]), inc=(("ve", 1),))


def emit_ln_apply(sch, src_chunks, dst_chunks, rr, nmr, nt, p1, p2, tmp,
                  onesr, g_row=None, b_row=None, add_chunks=None,
                  dst_is_bf16=True):
    tiles = [(i * 512, min(512, nt - i * 512)) for i in range((nt + 511) // 512)]
    nchunks = len(src_chunks)
    for c in range(nchunks):
        for t0, tw in tiles:
            sch.op("tensor", lambda e, c=c, t0=t0, tw=tw: e.matmul(
                p1[:, 0:tw],
                (g_row[0:1, c * 128:(c + 1) * 128] if g_row is not None
                 else onesr[0:1, 0:128]),
                rr[0:1, t0:t0 + tw], start=True, stop=True),
                wait=sch.w("ve"))
            sch.op("tensor", lambda e, c=c, t0=t0, tw=tw: e.matmul(
                p2[:, 0:tw],
                (g_row[0:1, c * 128:(c + 1) * 128] if g_row is not None
                 else onesr[0:1, 0:128]),
                nmr[0:1, t0:t0 + tw], start=True, stop=(b_row is None)),
                inc=((("mm", 1),) if b_row is None else ()))
            if b_row is not None:
                sch.op("tensor", lambda e, c=c, tw=tw: e.matmul(
                    p2[:, 0:tw], b_row[0:1, c * 128:(c + 1) * 128],
                    onesr[0:1, 0:tw], start=False, stop=True),
                    inc=(("mm", 1),))
            sch.op("vector", lambda e, c=c, t0=t0, tw=tw: e.tensor_mul(
                tmp[:, 0:tw], src_chunks[c][:, t0:t0 + tw], p1[:, 0:tw]),
                wait=sch.w("mm"))
            if add_chunks is None:
                sch.op("vector", lambda e, c=c, t0=t0, tw=tw: e.tensor_add(
                    dst_chunks[c][:, t0:t0 + tw], tmp[:, 0:tw], p2[:, 0:tw]),
                    inc=(("ve", 1),))
            else:
                sch.op("vector", lambda e, c=c, t0=t0, tw=tw: e.tensor_add(
                    tmp[:, 0:tw], tmp[:, 0:tw], p2[:, 0:tw]))
                sch.op("vector", lambda e, c=c, t0=t0, tw=tw: e.tensor_add(
                    dst_chunks[c][:, t0:t0 + tw], add_chunks[c][:, t0:t0 + tw],
                    tmp[:, 0:tw]), inc=(("ve", 1),))


def emit_neg_murr(sch, mu, rr, nmr, nt):
    sch.op("vector", lambda e: e.scalar_tensor_tensor(
        nmr[0:1, 0:nt], mu[0:1, 0:nt], -1.0, rr[0:1, 0:nt],
        ALU.mult, ALU.mult), inc=(("ve", 1),))



def build_launch1():
    nc = bass.Bass("TRN2", target_bir_lowering=False, debug=False)
    xin = nc.dram_tensor("xin", [NCH, 128, TL], F32, kind="ExternalInput").ap()
    wq = nc.dram_tensor("wq", [D, D], F32, kind="ExternalInput").ap()
    wk = nc.dram_tensor("wk", [D, D], F32, kind="ExternalInput").ap()
    wv = nc.dram_tensor("wv", [D, D], F32, kind="ExternalInput").ap()
    bcols = nc.dram_tensor("bcols", [128, 24], F32, kind="ExternalInput").ap()
    g_row = nc.dram_tensor("g_row", [1, D], F32, kind="ExternalInput").ap()
    b_row = nc.dram_tensor("b_row", [1, D], F32, kind="ExternalInput").ap()
    qkv_out = nc.dram_tensor("qkv", [3, NCH, 128, TL], BF16,
                             kind="ExternalOutput").ap()

    with ExitStack() as es:
        sch = Sched(nc, es)
        sch.sem("ld", "lg", "mm", "ve", "ac", "st")
        XB = [_mk(nc, es, f"xb{c}", [128, TL], BF16) for c in range(NCH)]
        XN = [_mk(nc, es, f"xn{c}", [128, TL], BF16) for c in range(NCH)]
        SQS = _mk(nc, es, "sqs", [128, 512], BF16)
        WB = _mk(nc, es, "wb", [128, NCH * D], BF16)
        BIAS = _mk(nc, es, "bias", [128, 24], F32)
        GR = _mk(nc, es, "gr", [1, D], F32)
        BR = _mk(nc, es, "br", [1, D], F32)
        ONESB = _mk(nc, es, "onesb", [128, 1], BF16)
        EPSC = _mk(nc, es, "epsc", [128, 1], F32)
        ONESR = _mk(nc, es, "onesr", [1, 512], F32)
        MU = _mk(nc, es, "mu", [1, TL], F32)
        Q2 = _mk(nc, es, "q2", [1, TL], F32)
        RR = _mk(nc, es, "rr", [1, TL], F32)
        NMR = _mk(nc, es, "nmr", [1, TL], F32)
        RS0 = _mk(nc, es, "rs0", [1, TL], F32)
        RS1 = _mk(nc, es, "rs1", [1, TL], F32)
        TMP = _mk(nc, es, "tmp", [128, 512], F32)
        OUTS = _mk(nc, es, "outs", [128, TL], BF16)
        P0 = _ps(nc, es, "p0", [128, 512])
        P1 = _ps(nc, es, "p1", [128, 512])
        P2 = _ps(nc, es, "p2", [128, 512])

        for c in range(NCH):
            sch.dma("gpsimd", XB[c][:, :], xin[c, :, :], sem="lg")
        ld_x = sch.cnt["lg"]
        sch.dma("sync", BIAS[:, :], bcols[:, :])
        sch.dma("sync", GR[0:1, :], g_row[0:1, :])
        sch.dma("sync", BR[0:1, :], b_row[0:1, :])
        ld_r = sch.cnt["ld"]

        sch.op("vector", lambda e: e.memset(ONESB[:, :], 1.0))
        sch.op("vector", lambda e: e.memset(EPSC[:, :], EPS))
        sch.op("vector", lambda e: e.memset(ONESR[:, :], 1.0),
               inc=(("ve", 1),))
        casts_w = (("lg", ld_x),)

        emit_stats_1024(sch, XB, ONESB, P0, MU, Q2, TL, SQS,
                        first_wait=casts_w)
        emit_rows_rstd(sch, MU, Q2, RR, RS0, RS1, TL, double_norm=True, epsc=EPSC)
        emit_neg_murr(sch, MU, RR, NMR, TL)
        sch.op("tensor", lambda e: e.nop(), wait=(("ld", ld_r),))
        emit_ln_apply(sch, XB, XN, RR, NMR, TL, P1, P2, TMP, ONESR,
                      g_row=GR, b_row=BR)
        xn_done = sch.cnt["ve"]

        for wi, wdram in enumerate((wq, wk, wv)):
            prev_mm = sch.cnt["mm"]
            for kc in range(NCH):
                sch.dma("gpsimd", WB[:, kc * D:(kc + 1) * D],
                        wdram[kc * 128:(kc + 1) * 128, :], sem="lg",
                        wait=((("mm", prev_mm),) if (kc == 0 and wi > 0) else ()))
            ldw = sch.cnt["lg"]
            for dc in range(NCH):
                for ti in range(TL // 512):
                    t0 = ti * 512
                    for kc in range(NCH):
                        first = kc == 0 and dc == 0 and ti == 0
                        sch.op("tensor", lambda e, dc=dc, t0=t0, kc=kc: e.matmul(
                            P0[:, :],
                            WB[:, kc * D + dc * 128: kc * D + (dc + 1) * 128],
                            XN[kc][:, t0:t0 + 512],
                            start=(kc == 0), stop=(kc == NCH - 1)),
                            wait=((("lg", ldw), ("ve", xn_done)) if first
                                  else sch.w("ac") if kc == 0 else ()),
                            inc=((("mm", 1),) if kc == NCH - 1 else ()))
                    sch.op("scalar", lambda e, wi=wi, dc=dc, t0=t0: e.activation(
                        OUTS[:, t0:t0 + 512], P0[:, :], AF.Identity,
                        bias=BIAS[:, wi * NCH + dc: wi * NCH + dc + 1]),
                        wait=sch.w("mm"), inc=(("ac", 1),))
                st_prev = sch.cnt["st"]
                sch.dma("sync", qkv_out[wi, dc, :, :], OUTS[:, :], sem="st",
                        wait=sch.w("ac"))
                sch.op("scalar", lambda e: e.nop(),
                       wait=(("st", st_prev + 16),))

        sch.op("gpsimd", lambda e: e.nop(), wait=(("st", sch.cnt["st"]),))
        sch.build_block()
    return nc



def fm_slices(x_rows):
    out = []
    dx = x_rows.shape[1]
    for c in range(NCORE):
        s = x_rows[c * TL:(c + 1) * TL]
        out.append(np.ascontiguousarray(s.T).reshape(dx // 128, 128, TL))
    return out


def fm_join(slices):
    return np.concatenate([s.reshape(s.shape[0] * 128, TL) for s in slices],
                          axis=1)


_NC_CACHE = {}


def _get_nc(name, builder):
    if name not in _NC_CACHE:
        _NC_CACHE[name] = builder()
    return _NC_CACHE[name]


def kernel(**inputs):
    inp = {k: np.asarray(v) for k, v in inputs.items()}
    cores = list(range(NCORE))

    x = inp["L0_attn_x"].reshape(NTOK, D)
    xfm = fm_slices(x)
    bcols = np.zeros((128, 24), np.float32)
    for wi, b in enumerate((inp["L0_attn_bq"], inp["L0_attn_bk"],
                            inp["L0_attn_bv"])):
        bcols[:, wi * 8:(wi + 1) * 8] = b.reshape(8, 128).T
    nc1 = _get_nc("l1", build_launch1)
    im1 = [{"xin": xfm[c], "wq": inp["L0_attn_Wq"], "wk": inp["L0_attn_Wk"],
            "wv": inp["L0_attn_Wv"], "bcols": bcols,
            "g_row": np.ascontiguousarray(inp["L0_attn_ln_g"].reshape(1, D)),
            "b_row": np.ascontiguousarray(inp["L0_attn_ln_b"].reshape(1, D))}
           for c in cores]
    r1 = run_bass_kernel_spmd(nc1, im1, core_ids=cores)
    qf = fm_join([np.asarray(r1.results[c]["qkv"][0]) for c in cores])
    kf = fm_join([np.asarray(r1.results[c]["qkv"][1]) for c in cores])
    vf = fm_join([np.asarray(r1.results[c]["qkv"][2]) for c in cores])

    im2 = []
    for c in cores:
        q2 = np.stack([qf[c * 128:(c + 1) * 128, b * T:(b + 1) * T]
                       for b in range(2)])
        k2 = np.stack([kf[c * 128:(c + 1) * 128, b * T:(b + 1) * T]
                       for b in range(2)])
        v1 = np.zeros((2, 2, 128, 64, 65), BF)
        for b in range(2):
            for h in range(2):
                vh = vf[(2 * c + h) * 64:(2 * c + h + 1) * 64,
                        b * T:(b + 1) * T]
                vt = np.ascontiguousarray(vh.T)
                v1[b, h, :, :, 0:64] = vt.reshape(64, 128, 64).transpose(1, 0, 2)
                v1[b, h, :, :, 64] = BF(1.0)
        im2.append({"q2in": np.ascontiguousarray(q2),
                    "k2in": np.ascontiguousarray(k2), "v1in": v1})
    nc2 = _get_nc("l2", build_launch2)
    r2 = run_bass_kernel_spmd(nc2, im2, core_ids=cores)
    yfull = np.zeros((D, NTOK), BF)
    for c in cores:
        yo = np.asarray(r2.results[c]["yout"])
        for b in range(2):
            for h in range(2):
                yfull[(2 * c + h) * 64:(2 * c + h + 1) * 64,
                      b * T:(b + 1) * T] = yo[b, h]

    b1c = np.ascontiguousarray(inp["L0_ffn_fc1_bias"].reshape(32, 128).T)
    b2c = np.ascontiguousarray(inp["L0_ffn_fc2_bias"].reshape(8, 128).T)
    nc3 = _get_nc("l3", build_launch3)
    im3 = [{"xin": xfm[c],
            "yin": np.ascontiguousarray(
                yfull[:, c * TL:(c + 1) * TL].reshape(NCH, 128, TL)),
            "lg_row": np.ascontiguousarray(
                inp["L0_ffn_ln_gamma"].reshape(1, D)),
            "lb_row": np.ascontiguousarray(inp["L0_ffn_ln_beta"].reshape(1, D)),
            "fc1q": inp["L0_ffn_fc1_q"], "fc1s": inp["L0_ffn_fc1_scales"],
            "b1c": b1c, "fc2q": inp["L0_ffn_fc2_q"],
            "fc2s": inp["L0_ffn_fc2_scales"], "b2c": b2c} for c in cores]
    r3 = run_bass_kernel_spmd(nc3, im3, core_ids=cores)
    x2f = fm_join([np.asarray(r3.results[c]["x2out"]) for c in cores])
    xs2f = fm_join([np.asarray(r3.results[c]["xs2out"]) for c in cores])

    xs2in = np.stack([
        np.ascontiguousarray(xs2f[:, b * T:(b + 1) * T].reshape(NCH, 128, T))
        for b in range(2)])
    masks4 = np.zeros((4, 128, 512), np.float32)
    for dd in range(4):
        for p in range(128):
            c0 = 128 * dd + p
            if c0 < 512:
                masks4[dd, p, c0:] = 1.0
    sel2 = np.zeros((128, 2), np.float32)
    sel2[0:64, 0] = 1.0
    sel2[64:128, 1] = 1.0
    sel2t = np.ascontiguousarray(sel2.T)
    iden = np.eye(128, dtype=np.float32)
    w0blk = np.zeros((128, 128), np.float32)
    w0blk[0:64, 0:64] = inp["L1_attn_W0"]
    w0blk[64:128, 64:128] = inp["L1_attn_W0"]
    im4 = []
    for c in cores:
        hA, hB = 2 * c, 2 * c + 1
        cols = []
        for oi in range(3):
            for h in (hA, hB):
                cols.append(inp["L1_attn_Wproj"][:, oi * D + h * 64:
                                                 oi * D + (h + 1) * 64])
        wp_s = np.ascontiguousarray(np.concatenate(cols, axis=1))
        bp = np.zeros((128, 3), np.float32)
        lngb = np.zeros((128, 4), np.float32)
        for hh, h in enumerate((hA, hB)):
            lo = hh * 64
            for oi in range(3):
                bp[lo:lo + 64, oi] = inp["L1_attn_bproj"][oi * D + h * 64:
                                                          oi * D + (h + 1) * 64]
            lngb[lo:lo + 64, 0] = inp["L1_attn_ln_g"]
            lngb[lo:lo + 64, 1] = inp["L1_attn_ln_b"]
            lngb[lo:lo + 64, 2] = inp["L1_attn_gn_g"][h * 64:(h + 1) * 64]
            lngb[lo:lo + 64, 3] = inp["L1_attn_gn_b"][h * 64:(h + 1) * 64]
        im4.append({"xs2f": xs2in, "wp": wp_s, "bp": bp, "w0b": w0blk,
                    "lngb": lngb, "masks": masks4, "sel2": sel2,
                    "sel2t": sel2t, "iden": iden})
    nc4 = _get_nc("l4", build_launch4)
    r4 = run_bass_kernel_spmd(nc4, im4, core_ids=cores)
    ctxf = np.zeros((D, NTOK), BF)
    for c in cores:
        co = np.asarray(r4.results[c]["ctxn"])
        for b in range(2):
            ctxf[c * 128:(c + 1) * 128, b * T:(b + 1) * T] = co[b]

    im5 = []
    for c in cores:
        t0 = c * TL
        x2e = np.zeros((D, TE), np.float32)
        cte = np.zeros((D, TE), BF)
        x2e[:, 1:] = x2f[:, t0:t0 + TL]
        cte[:, 1:] = ctxf[:, t0:t0 + TL]
        if c not in (0, NCORE // 2):
            x2e[:, 0] = x2f[:, t0 - 1]
            cte[:, 0] = ctxf[:, t0 - 1]
        bm = (np.zeros((128, 1), np.float32) if c in (0, NCORE // 2)
              else np.ones((128, 1), np.float32))
        im5.append({
            "x2e": np.ascontiguousarray(x2e.reshape(NCH, 128, TE)),
            "ctxe": np.ascontiguousarray(cte.reshape(NCH, 128, TE)),
            "wout": inp["L1_attn_Wout"],
            "wrt": np.ascontiguousarray(inp["L1_ffn_W_r"].T),
            "wkt": np.ascontiguousarray(inp["L1_ffn_W_k"].T),
            "wvt": np.ascontiguousarray(inp["L1_ffn_W_v"].T),
            "bo_c": np.ascontiguousarray(inp["L1_attn_bout"].reshape(8, 128).T),
            "mur_c": np.ascontiguousarray(inp["L1_ffn_mu_r"].reshape(8, 128).T),
            "muk_c": np.ascontiguousarray(inp["L1_ffn_mu_k"].reshape(8, 128).T),
            "bmask": bm,
            "idenf": np.eye(128, dtype=np.float32),
        })
    nc5 = _get_nc("l5", build_launch5)
    r5 = run_bass_kernel_spmd(nc5, im5, core_ids=cores)
    out = np.concatenate([np.asarray(r5.results[c]["out"]) for c in cores],
                         axis=0)
    return out.reshape(B, T, D)



def build_launch2():
    nc = bass.Bass("TRN2", target_bir_lowering=False, debug=False)
    q2in = nc.dram_tensor("q2in", [2, 128, T], BF16, kind="ExternalInput").ap()
    k2in = nc.dram_tensor("k2in", [2, 128, T], BF16, kind="ExternalInput").ap()
    v1in = nc.dram_tensor("v1in", [2, 2, 128, 64, 65], BF16,
                          kind="ExternalInput").ap()
    yout = nc.dram_tensor("yout", [2, 2, 64, T], BF16, kind="ExternalOutput").ap()
    NKT = T // 128
    NTT = T // 512

    with ExitStack() as es:
        sch = Sched(nc, es)
        sch.sem("ld", "lg", "mm", "ve", "ac", "st")
        Q2 = _mk(nc, es, "q2", [128, T], BF16)
        K2 = _mk(nc, es, "k2", [128, T], BF16)
        V1 = [_mk(nc, es, f"v1{h}", [128, 64 * 65], BF16) for h in range(2)]
        SQT = _mk(nc, es, "sqt", [128, T], BF16)
        PA = [_mk(nc, es, f"pa{s}", [128, 1024], BF16) for s in range(8)]
        NQ = _mk(nc, es, "nq", [1, T], F32)
        SC = _mk(nc, es, "sc", [1, 16], F32)
        NB = _mk(nc, es, "nb", [128, 2], F32)
        OE = _mk(nc, es, "oe", [128, 512], F32)
        RE = _mk(nc, es, "re", [128, 512], F32)
        YH = [_mk(nc, es, f"yh{h}", [64, T], BF16) for h in range(2)]
        ONESB = _mk(nc, es, "onesb", [128, 1], BF16)
        ONESF = _mk(nc, es, "onesf", [128, 128], F32)
        PS = [_ps(nc, es, f"ps{s}", [128, 1024]) for s in range(3)]
        PO_A = _ps(nc, es, "poa", [128, 512])
        PO_B = _ps(nc, es, "pob", [128, 512])
        PR = PS[0]

        sch.op("vector", lambda e: e.memset(ONESB[:, :], 1.0))
        sch.op("vector", lambda e: e.memset(ONESF[:, :], 1.0), inc=(("ve", 1),))

        for task in range(2):
            w_prev = sch.w("mm", "ac", "st")
            sch.dma("sync", Q2[:, :], q2in[task, :, :], wait=w_prev)
            sch.dma("sync", K2[:, :], k2in[task, :, :])
            for h in range(2):
                sch.dma("sync", V1[h][:, :],
                        v1in[task, h, :, :, :].rearrange("p a b -> p (a b)"))
            ld0 = sch.cnt["ld"]

            for which, SRC in enumerate((Q2, K2)):
                sch.op("scalar", lambda e, SRC=SRC: e.activation(
                    SQT[:, :], SRC[:, :], AF.Square),
                    wait=(("ld", ld0),) + sch.w("mm"), inc=(("ac", 1),))
                for h in range(2):
                    lo = 64 * h
                    nsn = {}
                    for tt in range(NTT):
                        bank = PS[tt % 2][:, 0:512]
                        w = sch.w("ac", "ve") if tt == 0 else (
                            ("ve", nsn[tt - 2]),) if tt >= 2 else ()
                        sch.op("tensor",
                               lambda e, lo=lo, tt=tt, bank=bank: e.matmul(
                                   bank[0:1, :], ONESB[lo:lo + 64, 0:1],
                                   SQT[lo:lo + 64, tt * 512:(tt + 1) * 512],
                                   start=True, stop=True),
                               wait=w, inc=(("mm", 1),))
                        msn = sch.cnt["mm"]
                        sch.op("vector", lambda e, tt=tt, bank=bank:
                               e.tensor_copy(NQ[0:1, tt * 512:(tt + 1) * 512],
                                             bank[0:1, :]),
                               wait=(("mm", msn),), inc=(("ve", 1),))
                        nsn[tt] = sch.cnt["ve"]
                    j = which * 4 + h
                    sch.op("vector", lambda e, j=j: e.reduce_max(
                        SC[0:1, j:j + 1], NQ[0:1, 0:T], axis=AX.X),
                        inc=(("ve", 1),))
            sch.op("scalar", lambda e: e.activation(SC[0:1, 8:14], SC[0:1, 0:6],
                                                    AF.Sqrt),
                   wait=sch.w("ve"), inc=(("ac", 1),))
            for h in range(2):
                sch.op("vector", lambda e, h=h: e.tensor_mul(
                    SC[0:1, 12 + h:13 + h], SC[0:1, 8 + h:9 + h],
                    SC[0:1, 12 + h:13 + h]), wait=sch.w("ac"),
                    inc=(("ve", 1),))
            sch.op("vector", lambda e: e.tensor_max(
                SC[0:1, 14:15], SC[0:1, 12:13], SC[0:1, 13:14]))
            sch.op("vector", lambda e: e.tensor_scalar_mul(
                SC[0:1, 15:16], SC[0:1, 14:15], -0.125), inc=(("ve", 1),))
            sch.op("tensor", lambda e: e.matmul(
                PR[:, 0:1], ONESF[0:1, 0:128], SC[0:1, 15:16],
                start=True, stop=True),
                wait=sch.w("ve"), inc=(("mm", 1),))
            sch.op("vector", lambda e: e.tensor_copy(
                NB[:, 0:1], PR[:, 0:1]),
                wait=sch.w("mm"), inc=(("ve", 1),))
            bnd = sch.w("ve")

            snap = {}
            KG = [list(range(g, min(g + 2, NKT))) for g in range(0, NKT, 2)]
            macros = [(h, kcs) for kcs in KG for h in (0, 1)]
            NM = len(macros)
            for ti in range(NTT):
                t0 = ti * 512
                for step in range(NM + 3):
                    if NM - 1 + 2 >= step >= 2:
                        mi = ti * NM + step - 2
                        sl = mi % 3
                        pl = mi % 8
                        nk = len(macros[step - 2][1]) * 512
                        v = snap[("S", mi)]
                        if mi >= 8:
                            v = max(v, snap[("PV", mi - 8)])
                        sch.op("scalar", lambda e, sl=sl, pl=pl, nk=nk:
                               e.activation(PA[pl][:, 0:nk], PS[sl][:, 0:nk],
                                            AF.Exp, bias=NB[:, 0:1],
                                            scale=0.125),
                               wait=(("mm", v),), inc=(("ac", 1),))
                        snap[("exp", mi)] = sch.cnt["ac"]
                    if step <= NM - 1:
                        mi = ti * NM + step
                        h, kcs = macros[step]
                        lo = 64 * h
                        sl = mi % 3
                        w = []
                        if mi >= 3:
                            w.append(("ac", snap[("exp", mi - 3)]))
                        if step == 0 and ti > 0:
                            w.append(("ve", snap[("fin", ti - 1)]))
                        if ti == 0 and step == 0:
                            w = list(bnd)
                        for q, kc in enumerate(kcs):
                            sch.op("tensor",
                                   lambda e, sl=sl, kc=kc, t0=t0, q=q, lo=lo:
                                   e.matmul(PS[sl][:, q * 512:(q + 1) * 512],
                                            K2[lo:lo + 64,
                                               kc * 128:(kc + 1) * 128],
                                            Q2[lo:lo + 64, t0:t0 + 512],
                                            start=True, stop=True,
                                            tile_position=(lo, 0)),
                                   wait=(w if q == 0 else ()),
                                   inc=((("mm", 1),) if q == len(kcs) - 1
                                        else ()))
                        snap[("S", mi)] = sch.cnt["mm"]
                    if step >= 3:
                        mi = ti * NM + step - 3
                        h, kcs = macros[step - 3]
                        pl = mi % 8
                        PO = PO_A if h == 0 else PO_B
                        for q, kc in enumerate(kcs):
                            sch.op("tensor", lambda e, pl=pl, kc=kc, q=q, PO=PO:
                                   e.matmul(PO[0:65, :],
                                            V1[0 if PO is PO_A else 1]
                                            [:, kc * 65:(kc + 1) * 65],
                                            PA[pl][:, q * 512:(q + 1) * 512],
                                            start=(kc == 0),
                                            stop=(kc == NKT - 1)),
                                   wait=((("ac", snap[("exp", mi)]),)
                                         if q == 0 else ()),
                                   inc=((("mm", 1),) if q == len(kcs) - 1
                                        else ()))
                        snap[("PV", mi)] = sch.cnt["mm"]
                for h, PO, Y in ((0, PO_A, YH[0]), (1, PO_B, YH[1])):
                    sch.op("vector", lambda e, PO=PO: e.tensor_copy(
                        OE[0:65, :], PO[0:65, :]), wait=sch.w("mm"))
                    sch.op("vector", lambda e: e.reciprocal(
                        RE[64:65, 0:512], OE[64:65, 0:512]), inc=(("ve", 1),))
                    sch.op("tensor", lambda e, h=h: e.matmul(
                        PR[0:64, h * 512:h * 512 + 512],
                        ONESF[64:65, 0:64], RE[64:65, 0:512],
                        start=True, stop=True),
                        wait=sch.w("ve"), inc=(("mm", 1),))
                    sch.op("vector", lambda e, h=h, Y=Y, t0=t0: e.tensor_mul(
                        Y[0:64, t0:t0 + 512], OE[0:64, :],
                        PR[0:64, h * 512:h * 512 + 512]),
                        wait=sch.w("mm"), inc=(("ve", 1),))
                snap[("fin", ti)] = sch.cnt["ve"]
            for h in range(2):
                sch.dma("sync", yout[task, h, :, :], YH[h][0:64, :], sem="st",
                        wait=sch.w("ve"))

        sch.op("gpsimd", lambda e: e.nop(), wait=(("st", sch.cnt["st"]),))
        sch.build_block()
    return nc



def build_launch3():
    nc = bass.Bass("TRN2", target_bir_lowering=False, debug=False)
    xin = nc.dram_tensor("xin", [NCH, 128, TL], F32, kind="ExternalInput").ap()
    yin = nc.dram_tensor("yin", [NCH, 128, TL], BF16, kind="ExternalInput").ap()
    lg_row = nc.dram_tensor("lg_row", [1, D], F32, kind="ExternalInput").ap()
    lb_row = nc.dram_tensor("lb_row", [1, D], F32, kind="ExternalInput").ap()
    fc1q = nc.dram_tensor("fc1q", [D, F_FFN], I32, kind="ExternalInput").ap()
    fc1s = nc.dram_tensor("fc1s", [8, F_FFN], F32, kind="ExternalInput").ap()
    b1c = nc.dram_tensor("b1c", [128, 32], F32, kind="ExternalInput").ap()
    fc2q = nc.dram_tensor("fc2q", [F_FFN, D], I32, kind="ExternalInput").ap()
    fc2s = nc.dram_tensor("fc2s", [32, D], F32, kind="ExternalInput").ap()
    b2c = nc.dram_tensor("b2c", [128, 8], F32, kind="ExternalInput").ap()
    x2out = nc.dram_tensor("x2out", [NCH, 128, TL], F32, kind="ExternalOutput").ap()
    xs2out = nc.dram_tensor("xs2out", [NCH, 128, TL], BF16,
                            kind="ExternalOutput").ap()
    y1dram = nc.dram_tensor("y1dram", [32, 128, TL], BF16, kind="Internal").ap()

    with ExitStack() as es:
        sch = Sched(nc, es)
        sch.sem("ld", "lg", "mm", "ve", "ac", "st")
        X1 = [_mk(nc, es, f"x1{c}", [128, TL], F32) for c in range(NCH)]
        LG = _mk(nc, es, "lg", [1, D], F32)
        LB = _mk(nc, es, "lb", [1, D], F32)
        B1 = _mk(nc, es, "b1", [128, 32], F32)
        B2 = _mk(nc, es, "b2", [128, 8], F32)
        ONESB = _mk(nc, es, "onesb", [128, 1], BF16)
        ONESR = _mk(nc, es, "onesr", [1, 512], F32)
        EPSC = _mk(nc, es, "epsc", [128, 1], F32)
        MU = _mk(nc, es, "mu", [1, TL], F32)
        Q2 = _mk(nc, es, "q2", [1, TL], F32)
        RR = _mk(nc, es, "rr", [1, TL], F32)
        NMR = _mk(nc, es, "nmr", [1, TL], F32)
        R1 = _mk(nc, es, "r1", [1, TL], F32)
        NM1 = _mk(nc, es, "nm1", [1, TL], F32)
        RS0 = _mk(nc, es, "rs0", [1, TL], F32)
        RS1 = _mk(nc, es, "rs1", [1, TL], F32)
        SQS = _mk(nc, es, "sqs", [128, 512], BF16)
        XBS = _mk(nc, es, "xbs", [128, 512], BF16)
        TMP = _mk(nc, es, "tmp", [128, 512], F32)
        TMP2 = _mk(nc, es, "tmp2", [128, 512], F32)
        P0 = _ps(nc, es, "p0", [128, 512])
        P1 = _ps(nc, es, "p1", [128, 512])
        P2 = _ps(nc, es, "p2", [128, 512])

        sch.op("vector", lambda e: e.memset(ONESB[:, :], 1.0))
        sch.op("vector", lambda e: e.memset(EPSC[:, :], EPS))
        sch.op("vector", lambda e: e.memset(ONESR[:, :], 1.0), inc=(("ve", 1),))
        for c in range(NCH):
            sch.dma("sync", X1[c][:, :], xin[c, :, :])
        sch.dma("sync", LG[0:1, :], lg_row[0:1, :])
        sch.dma("sync", LB[0:1, :], lb_row[0:1, :])
        sch.dma("sync", B1[:, :], b1c[:, :])
        sch.dma("sync", B2[:, :], b2c[:, :])
        ld0 = sch.cnt["ld"]

        with ExitStack() as es2:
            Y = [_mk(nc, es2, f"y{c}", [128, TL], BF16) for c in range(NCH)]
            for c in range(NCH):
                sch.dma("gpsimd", Y[c][:, :], yin[c, :, :], sem="lg")
            ldy = sch.cnt["lg"]
            emit_stats_1024(sch, Y, ONESB, P0, MU, Q2, TL, SQS,
                            first_wait=(("lg", ldy), ("ld", ld0)))
            emit_rows_rstd(sch, MU, Q2, RR, RS0, RS1, TL, double_norm=False,
                           epsc=EPSC)
            emit_neg_murr(sch, MU, RR, NMR, TL)
            emit_ln_apply(sch, Y, X1, RR, NMR, TL, P1, P2, TMP, ONESR,
                          add_chunks=X1)

        emit_stats_1024(sch, X1, ONESB, P0, MU, Q2, TL, SQS, xbs=XBS)
        emit_rows_rstd(sch, MU, Q2, RR, RS0, RS1, TL, double_norm=True,
                       epsc=EPSC, r1_out=R1)
        emit_neg_murr(sch, MU, RR, NMR, TL)
        sch.op("vector", lambda e: e.tensor_mul(NM1[0:1, :], MU[0:1, :],
                                                R1[0:1, :]))
        sch.op("vector", lambda e: e.tensor_scalar_mul(NM1[0:1, :], NM1[0:1, :],
                                                       -1.0), inc=(("ve", 1),))

        with ExitStack() as es3:
            HB = [_mk(nc, es3, f"h{c}", [128, TL], BF16) for c in range(NCH)]
            W1 = _mk(nc, es3, "w1", [128, NCH * 1024], BF16)
            QB = _mk(nc, es3, "qb", [128, 1024], BF16)
            SCLF = _mk(nc, es3, "sclf", [128, 1024], F32)
            Y1T = _mk(nc, es3, "y1t", [128, TL], BF16)
            emit_ln_apply(sch, X1, HB, RR, NMR, TL, P1, P2, TMP, ONESR,
                          g_row=LG, b_row=LB)
            h_done = sch.cnt["ve"]
            for fh in range(4):
                f0 = fh * 1024
                for kc in range(NCH):
                    sch.dma("gpsimd", QB[:, :],
                            fc1q[kc * 128:(kc + 1) * 128, f0:f0 + 1024],
                            sem="lg", wait=sch.w("ve"))
                    sch.dma("sync", SCLF[:, :],
                            bass.AP(fc1s.tensor, kc * F_FFN + f0,
                                    [[0, 128], [1, 1024]]), wait=sch.w("ve"))
                    sch.op("vector", lambda e, kc=kc: e.scalar_tensor_tensor(
                        W1[:, kc * 1024:(kc + 1) * 1024], QB[:, :], -128.0,
                        SCLF[:, :], ALU.add, ALU.mult),
                        wait=(("lg", sch.cnt["lg"]), ("ld", sch.cnt["ld"]))
                        + (sch.w("mm") if kc == 0 else ()),
                        inc=(("ve", 1),))
                wdq = sch.cnt["ve"]
                for fc in range(8):
                    gfc = fh * 8 + fc
                    for ti in range(TL // 512):
                        t0 = ti * 512
                        for kc in range(NCH):
                            first = fc == 0 and ti == 0 and kc == 0
                            sch.op("tensor", lambda e, fc=fc, t0=t0, kc=kc:
                                   e.matmul(P0[:, :],
                                            W1[:, kc * 1024 + fc * 128:
                                               kc * 1024 + (fc + 1) * 128],
                                            HB[kc][:, t0:t0 + 512],
                                            start=(kc == 0), stop=(kc == NCH - 1)),
                                   wait=((("ve", max(wdq, h_done)),) if first
                                         else sch.w("ac") if kc == 0 else ()),
                                   inc=((("mm", 1),) if kc == NCH - 1 else ()))
                        sch.op("scalar", lambda e, gfc=gfc, t0=t0: e.activation(
                            Y1T[:, t0:t0 + 512], P0[:, :], AF.Silu,
                            bias=B1[:, gfc:gfc + 1]),
                            wait=sch.w("mm"), inc=(("ac", 1),))
                    stp = sch.cnt["st"]
                    sch.dma("sync", y1dram[gfc, :, :], Y1T[:, :], sem="st",
                            wait=sch.w("ac"))
                    sch.op("scalar", lambda e: e.nop(),
                           wait=(("st", stp + 16),))

        with ExitStack() as es4:
            UB = [_mk(nc, es4, f"u{c}", [128, TL], BF16) for c in range(NCH)]
            W2 = _mk(nc, es4, "w2", [128, 32 * 256], BF16)
            QB2 = _mk(nc, es4, "qb2", [128, 256], BF16)
            SCL2 = _mk(nc, es4, "scl2", [128, 256], F32)
            Y1R = _mk(nc, es4, "y1r", [128, 8 * 512], BF16)
            PZ = [_ps(nc, es4, f"pz{i}", [128, 512]) for i in range(2)]
            for dh in range(4):
                d0 = dh * 256
                for kc in range(32):
                    sch.dma("gpsimd", QB2[:, :],
                            fc2q[kc * 128:(kc + 1) * 128, d0:d0 + 256],
                            sem="lg", wait=sch.w("ve"))
                    sch.dma("sync", SCL2[:, :],
                            bass.AP(fc2s.tensor, kc * D + d0,
                                    [[0, 128], [1, 256]]), wait=sch.w("ve"))
                    sch.op("vector", lambda e, kc=kc: e.scalar_tensor_tensor(
                        W2[:, kc * 256:(kc + 1) * 256], QB2[:, :], -128.0,
                        SCL2[:, :], ALU.add, ALU.mult),
                        wait=(("lg", sch.cnt["lg"]), ("ld", sch.cnt["ld"]))
                        + (sch.w("mm") if kc == 0 else ()),
                        inc=(("ve", 1),))
                wdq2 = sch.cnt["ve"]
                for ti in range(TL // 512):
                    t0 = ti * 512
                    for sub in range(4):
                        for i in range(8):
                            kc = sub * 8 + i
                            sch.dma("sync", Y1R[:, i * 512:(i + 1) * 512],
                                    y1dram[kc, :, t0:t0 + 512], sem="ld",
                                    wait=(sch.w("mm") if i == 0 else ()))
                        ldyr = sch.cnt["ld"]
                        for dc in range(2):
                            for i in range(8):
                                kc = sub * 8 + i
                                sch.op("tensor", lambda e, dc=dc, i=i, sub=sub:
                                       e.matmul(PZ[dc][:, :],
                                                W2[:, (sub * 8 + i) * 256 + dc * 128:
                                                   (sub * 8 + i) * 256 + (dc + 1) * 128],
                                                Y1R[:, i * 512:(i + 1) * 512],
                                                start=(sub == 0 and i == 0),
                                                stop=(sub == 3 and i == 7)),
                                       wait=(((("ld", ldyr), ("ve", wdq2))
                                              + (sch.w("ac") if sub == 0 else ()))
                                             if i == 0 else ()),
                                       inc=((("mm", 1),) if i == 7 else ()))
                    for dc in range(2):
                        gdc = dh * 2 + dc
                        m = sch.w("mm", "ve")
                        sch.op("scalar", lambda e, dc=dc, gdc=gdc: e.activation(
                            TMP[:, :], PZ[dc][:, :], AF.Identity,
                            bias=B2[:, gdc:gdc + 1]),
                            wait=m, inc=(("ac", 1),))
                        sch.op("tensor", lambda e, t0=t0: e.matmul(
                            P1[:, :], ONESR[0:1, 0:128], R1[0:1, t0:t0 + 512],
                            start=True, stop=True), wait=m)
                        sch.op("tensor", lambda e, t0=t0: e.matmul(
                            P2[:, :], ONESR[0:1, 0:128], NM1[0:1, t0:t0 + 512],
                            start=True, stop=True), inc=(("mm", 1),))
                        sch.op("vector", lambda e, gdc=gdc, t0=t0: e.tensor_mul(
                            TMP2[:, :], X1[gdc][:, t0:t0 + 512], P1[:, :]),
                            wait=sch.w("mm", "ac"))
                        sch.op("vector", lambda e: e.tensor_add(
                            TMP2[:, :], TMP2[:, :], P2[:, :]))
                        sch.op("vector", lambda e, gdc=gdc, t0=t0:
                               e.scalar_tensor_tensor(
                                   UB[gdc][:, t0:t0 + 512], TMP[:, :], 0.5,
                                   TMP2[:, :], ALU.mult, ALU.add),
                               inc=(("ve", 1),))

            emit_stats_1024(sch, UB, ONESB, P0, MU, Q2, TL, SQS)
            emit_rows_rstd(sch, MU, Q2, RR, RS0, RS1, TL, double_norm=False,
                           epsc=EPSC)
            emit_neg_murr(sch, MU, RR, NMR, TL)
            emit_ln_apply(sch, UB, X1, RR, NMR, TL, P1, P2, TMP, ONESR,
                          add_chunks=X1)
        for c in range(NCH):
            sch.dma("sync", x2out[c, :, :], X1[c][:, :], sem="st",
                    wait=sch.w("ve"))
        with ExitStack() as es5:
            XS2 = [_mk(nc, es5, f"xs{c}", [128, TL], BF16) for c in range(NCH)]
            emit_stats_1024(sch, X1, ONESB, P0, MU, Q2, TL, SQS, xbs=XBS)
            emit_rows_rstd(sch, MU, Q2, RR, RS0, RS1, TL, double_norm=False,
                           epsc=EPSC)
            emit_neg_murr(sch, MU, RR, NMR, TL)
            emit_ln_apply(sch, X1, XS2, RR, NMR, TL, P1, P2, TMP, ONESR)
            for c in range(NCH):
                sch.dma("sync", xs2out[c, :, :], XS2[c][:, :], sem="st",
                        wait=(sch.w("ve") if c == 0 else ()))

        sch.op("gpsimd", lambda e: e.nop(), wait=(("st", sch.cnt["st"]),))
        sch.build_block()
    return nc



def build_launch4(dbg=False):
    nc = bass.Bass("TRN2", target_bir_lowering=False, debug=False)
    xs2f = nc.dram_tensor("xs2f", [2, NCH, 128, T], BF16, kind="ExternalInput").ap()
    wp = nc.dram_tensor("wp", [D, 384], F32, kind="ExternalInput").ap()
    bp = nc.dram_tensor("bp", [128, 3], F32, kind="ExternalInput").ap()
    w0b = nc.dram_tensor("w0b", [128, 128], F32, kind="ExternalInput").ap()
    lngb = nc.dram_tensor("lngb", [128, 4], F32, kind="ExternalInput").ap()
    masks = nc.dram_tensor("masks", [4, 128, 512], F32, kind="ExternalInput").ap()
    sel2 = nc.dram_tensor("sel2", [128, 2], F32, kind="ExternalInput").ap()
    sel2t = nc.dram_tensor("sel2t", [2, 128], F32, kind="ExternalInput").ap()
    iden = nc.dram_tensor("iden", [128, 128], F32, kind="ExternalInput").ap()
    ctxout = nc.dram_tensor("ctxn", [2, 128, T], BF16, kind="ExternalOutput").ap()
    if dbg or T4DBG2:
        dbg_qkv = nc.dram_tensor("dbg_qkv", [3, 128, T], BF16,
                                 kind="ExternalOutput").ap()
        dbg_dz1 = nc.dram_tensor("dbg_dz1", [128, T], BF16,
                                 kind="ExternalOutput").ap()
        dbg_ctx = nc.dram_tensor("dbg_ctx", [128, T], BF16,
                                 kind="ExternalOutput").ap()
        dbg_z2 = nc.dram_tensor("dbg_z2", [128, 512], BF16,
                                kind="ExternalOutput").ap()
        dbg_ctx1 = nc.dram_tensor("dbg_ctx1", [128, T], BF16,
                                  kind="ExternalOutput").ap()
        dbg_gsc = nc.dram_tensor("dbg_gsc", [2, 128, 8], F32,
                                 kind="ExternalOutput").ap()
        dbg_pbs = nc.dram_tensor("dbg_pbs", [2, 128, 2], F32,
                                 kind="ExternalOutput").ap()
        dbg_lng = nc.dram_tensor("dbg_lng", [128, 4], F32,
                                 kind="ExternalOutput").ap()
    NTT = T // 512

    with ExitStack() as es:
        sch = Sched(nc, es)
        sch.sem("ld", "lg", "mm", "ve", "ac", "st")
        Q2 = _mk(nc, es, "q2", [128, T], BF16)
        K2 = _mk(nc, es, "k2", [128, T], BF16)
        V2 = _mk(nc, es, "v2", [128, T], BF16)
        DZ1 = _mk(nc, es, "dz1", [128, T], BF16)
        D1TA = _mk(nc, es, "d1ta", [128, (T // 128) * 128], BF16)
        D1TB = _mk(nc, es, "d1tb", [128, (T // 128) * 128], BF16)
        DSTA = _mk(nc, es, "dsta", [128, (T // 128) * 128], BF16)
        DSTB = _mk(nc, es, "dstb", [128, (T // 128) * 128], BF16)
        CTX = _mk(nc, es, "ctx", [128, T], BF16)
        XT = _mk(nc, es, "xt", [128, 8 * 512], BF16)
        WPB = _mk(nc, es, "wpb", [128, 8 * 384], BF16)
        W0B = _mk(nc, es, "w0bs", [128, 128], BF16)
        BP = _mk(nc, es, "bps", [128, 3], F32)
        LNG = _mk(nc, es, "lng", [128, 4], F32)
        MASK = _mk(nc, es, "masks_sb", [128, 4 * 512], F32)
        SEL2 = _mk(nc, es, "sel2s", [128, 2], BF16)
        S2TF = _mk(nc, es, "s2tf", [2, 128], F32)
        S2TB = _mk(nc, es, "s2tb", [2, 128], BF16)
        IDEN = _mk(nc, es, "idens", [128, 128], BF16)
        EPSC = _mk(nc, es, "epsc", [128, 1], F32)
        KKE = _mk(nc, es, "kke", [128, 4 * 512], BF16)
        ZT = _mk(nc, es, "zt", [128, 512], BF16)
        Z2T = _mk(nc, es, "z2t", [128, 512], BF16)
        ZHT = _mk(nc, es, "zht", [128, 512], BF16)
        DZHT = _mk(nc, es, "dzht", [128, 512], BF16)
        PRB = _mk(nc, es, "prb", [128, 512], BF16)
        SQB = _mk(nc, es, "sqb", [128, 512], BF16)
        DZNT = _mk(nc, es, "dznt", [128, 512], BF16)
        TS1 = _mk(nc, es, "ts1", [128, 512], F32)
        TS2 = _mk(nc, es, "ts2", [128, 512], F32)
        ROWA = _mk(nc, es, "rowa", [2, 512], F32)
        ROWB = _mk(nc, es, "rowb", [2, 512], F32)
        ROWC = _mk(nc, es, "rowc", [2, 512], F32)
        ROWD = _mk(nc, es, "rowd", [2, 512], F32)
        GSC = _mk(nc, es, "gsc", [128, 8], F32)
        PB_S = _ps(nc, es, "pbs", [128, 512])
        PB_A = _ps(nc, es, "pba", [128, 512])
        PB_B = _ps(nc, es, "pbb", [128, 512])
        PB_C = _ps(nc, es, "pbc", [128, 512])
        PKA = _ps(nc, es, "pka", [128, 512])
        PKB = _ps(nc, es, "pkb", [128, 512])
        PU = _ps(nc, es, "pu", [128, 512])
        PT = _ps(nc, es, "pt", [128, 128], BF16)

        sch.dma("gpsimd", WPB[:, :].rearrange("p (c n) -> p c n", c=NCH),
                wp[:, :].rearrange("(c p) n -> p c n", p=128), sem="lg")
        sch.dma("gpsimd", W0B[:, :], w0b[:, :], sem="lg")
        sch.dma("gpsimd", SEL2[:, :], sel2[:, :], sem="lg")
        sch.dma("gpsimd", S2TB[0:2, :], sel2t[0:2, :], sem="lg")
        sch.dma("gpsimd", IDEN[:, :], iden[:, :], sem="lg")
        sch.dma("sync", BP[:, :], bp[:, :])
        sch.dma("sync", LNG[:, :], lngb[:, :])
        sch.dma("sync", S2TF[0:2, :], sel2t[0:2, :])
        sch.dma("sync", MASK[:, :].rearrange("p (a b) -> p a b", a=4),
                masks[:, :, :].rearrange("a p b -> p a b"))
        ldc = sch.w("ld", "lg")
        sch.op("vector", lambda e: e.memset(EPSC[:, :], EPS))
        for Tn in (D1TA, D1TB, DSTA, DSTB):
            sch.op("vector", lambda e, Tn=Tn: e.memset(Tn[:, :], 0.0))
        sch.op("vector", lambda e: e.nop(), inc=(("ve", 1),))
        init_w = sch.w("ve")

        g_col = LNG[:, 0:1]
        b_col = LNG[:, 1:2]

        def bcast2(prow, psum, dt=F32, wait=()):
            sel = S2TF if dt == F32 else S2TB
            sch.op("tensor", lambda e, prow=prow, psum=psum, sel=sel: e.matmul(
                psum[:, 0:prow.shape[1]], sel[0:2, :], prow,
                start=True, stop=True), wait=wait, inc=(("mm", 1),))

        def stats2(src_ap, psum, wait=()):
            sch.op("tensor", lambda e, src_ap=src_ap, psum=psum: e.matmul(
                psum[0:2, 0:src_ap.shape[1]], SEL2[:, :], src_ap,
                start=True, stop=True), wait=wait, inc=(("mm", 1),))

        def ln_rows_tile(zsrc, w=512):
            stats2(zsrc, PB_S, wait=sch.w("ve", "ac"))
            sch.op("vector", lambda e, w=w: e.tensor_scalar_mul(
                ROWA[0:2, 0:w], PB_S[0:2, 0:w], 1.0 / 64),
                wait=sch.w("mm"), inc=(("ve", 1),))
            sch.op("scalar", lambda e, zsrc=zsrc: e.activation(
                SQB[:, 0:w], zsrc, AF.Square),
                wait=sch.w("ve"), inc=(("ac", 1),))
            stats2(SQB[:, 0:w], PB_S, wait=sch.w("ac", "ve"))
            sch.op("vector", lambda e, w=w: e.tensor_scalar_mul(
                ROWB[0:2, 0:w], PB_S[0:2, 0:w], 1.0 / 64),
                wait=sch.w("mm"))
            sch.op("vector", lambda e, w=w: e.tensor_mul(
                ROWC[0:2, 0:w], ROWA[0:2, 0:w], ROWA[0:2, 0:w]))
            sch.op("vector", lambda e, w=w: e.tensor_sub(
                ROWB[0:2, 0:w], ROWB[0:2, 0:w], ROWC[0:2, 0:w]),
                inc=(("ve", 1),))
            sch.op("scalar", lambda e, w=w: e.activation(
                ROWB[0:2, 0:w], ROWB[0:2, 0:w], AF.Sqrt, bias=EPSC[0:2, 0:1]),
                wait=sch.w("ve"), inc=(("ac", 1),))
            sch.op("vector", lambda e, w=w: e.reciprocal(
                ROWC[0:2, 0:w], ROWB[0:2, 0:w]), wait=sch.w("ac"))
            sch.op("vector", lambda e, w=w: e.scalar_tensor_tensor(
                ROWD[0:2, 0:w], ROWA[0:2, 0:w], -1.0, ROWC[0:2, 0:w],
                ALU.mult, ALU.mult), inc=(("ve", 1),))

        def dz_chain(zsrc, dz_dst_tile, dzn_tile, t0):
            ln_rows_tile(zsrc)
            bcast2(ROWC[0:2, 0:512], PB_C, wait=sch.w("ve"))
            bcast2(ROWD[0:2, 0:512], PB_A)
            sch.op("vector", lambda e, zsrc=zsrc: e.tensor_mul(
                TS1[:, :], zsrc, PB_C[:, :]), wait=sch.w("mm"))
            sch.op("vector", lambda e: e.tensor_add(
                ZHT[:, :], TS1[:, :], PB_A[:, :]), inc=(("ve", 1),))
            sch.op("vector", lambda e: e.tensor_scalar(
                TS1[:, :], ZHT[:, :], g_col, b_col, ALU.mult, ALU.add))
            sch.op("vector", lambda e, t0=t0: e.tensor_sub(
                TS1[:, :], TS1[:, :], V2[:, t0:t0 + 512]))
            sch.op("vector", lambda e: e.tensor_scalar(
                DZHT[:, :], TS1[:, :], g_col, None, ALU.mult),
                inc=(("ve", 1),))
            stats2(DZHT[:, :], PB_S, wait=sch.w("ve"))
            sch.op("vector", lambda e: e.tensor_scalar_mul(
                ROWA[0:2, :], PB_S[0:2, :], 1.0 / 64), wait=sch.w("mm"))
            sch.op("vector", lambda e: e.tensor_mul(
                PRB[:, :], DZHT[:, :], ZHT[:, :]), inc=(("ve", 1),))
            stats2(PRB[:, :], PB_S, wait=sch.w("ve"))
            sch.op("vector", lambda e: e.tensor_scalar_mul(
                ROWB[0:2, :], PB_S[0:2, :], 1.0 / 64),
                wait=sch.w("mm"), inc=(("ve", 1),))
            bcast2(ROWA[0:2, :], PB_A, wait=sch.w("ve"))
            bcast2(ROWB[0:2, :], PB_B)
            sch.op("vector", lambda e: e.tensor_mul(
                TS1[:, :], ZHT[:, :], PB_B[:, :]), wait=sch.w("mm"))
            sch.op("vector", lambda e: e.tensor_sub(
                TS2[:, :], DZHT[:, :], PB_A[:, :]))
            sch.op("vector", lambda e: e.tensor_sub(
                TS2[:, :], TS2[:, :], TS1[:, :]))
            if dz_dst_tile is not None:
                sch.op("vector", lambda e, d=dz_dst_tile: e.tensor_mul(
                    d, TS2[:, :], PB_C[:, :]))
            sch.op("vector", lambda e: e.tensor_mul(
                TS2[:, :], TS2[:, :], PB_C[:, :]))
            sch.op("vector", lambda e, d=dzn_tile: e.tensor_scalar_mul(
                d, TS2[:, :], -1.0), inc=(("ve", 1),))

        def transposes(src_tile, dstA, dstB, i):
            for jj in range(4):
                j = 4 * i + jj
                sch.op("tensor", lambda e, jj=jj: e.transpose(
                    PT[:, 0:64], src_tile[0:64, jj * 128:(jj + 1) * 128],
                    IDEN[0:64, 0:64]),
                    wait=sch.w("ve"), inc=(("mm", 1),))
                sch.op("vector", lambda e, j=j: e.tensor_copy(
                    dstA[:, j * 128:j * 128 + 64], PT[:, 0:64]),
                    wait=sch.w("mm"), inc=(("ve", 1),))
                sch.op("tensor", lambda e, jj=jj: e.transpose(
                    PT[:, 0:64], src_tile[64:128, jj * 128:(jj + 1) * 128],
                    IDEN[64:128, 64:128]),
                    wait=sch.w("ve"), inc=(("mm", 1),))
                sch.op("vector", lambda e, j=j: e.tensor_copy(
                    dstB[:, j * 128 + 64:j * 128 + 128], PT[:, 0:64]),
                    wait=sch.w("mm"), inc=(("ve", 1),))

        def triangle(i, t0, rhs2, DTA, DTB, first_lhs_rhs, kind=0):
            nj = 4 * i + 4
            nsl = 1 if kind == 1 else 2
            tsn = {}
            _sm = getattr(sch, "serial_mode", False)
            if T4TSER or (kind == 1 and T4TSER1) or (kind == 2 and T4TSER2):
                sch.serial_mode = True
            sch.op("tensor", lambda e, t0=t0, fr=first_lhs_rhs: e.matmul(
                PU[:, :], W0B[:, :], fr[:, t0:t0 + 512],
                start=True, stop=False), wait=sch.w("ve", "ac"))
            for s in range(nj + 2):
                if s >= 2:
                    j = s - 2
                    sl = (j % nsl) * 1024
                    w = [("ve", tsn[("ev", j)][0])]
                    if kind != 1 or T4SPL1:
                        w.append(("ac", tsn[("ev", j)][1]))
                    for hh, DT in ((0, DTA), (1, DTB)):
                        so = sl + hh * 512
                        sch.op("tensor", lambda e, j=j, DT=DT, so=so: e.matmul(
                            PU[:, :], DT[:, j * 128:(j + 1) * 128],
                            KKE[:, so:so + 512],
                            start=False, stop=(j == nj - 1 and hh == 1)),
                            wait=(w if hh == 0 else ()),
                            inc=((("mm", 1),) if hh == 1 else ()))
                    tsn[("u", j)] = sch.cnt["mm"]
                if s >= 1 and s - 1 <= nj - 1:
                    j = s - 1
                    d4 = j - 4 * i
                    PKs = (PKA, PKB) if (j % nsl == 0) else (PB_A, PB_C)
                    sl = (j % nsl) * 1024
                    wv = [("mm", tsn[("kk", j)])]
                    if j >= nsl:
                        wv.append(("mm", tsn[("u", j - nsl)]))
                    wm = [("mm", max(v for s_, v in wv))]
                    dve1 = kind == 1 and not T4SPL1
                    if d4 >= 0:
                        sch.op("vector", lambda e, sl=sl, PK=PKs[0], d4=d4:
                               e.tensor_mul(KKE[:, sl:sl + 512], PK[:, :],
                                            MASK[:, d4 * 512:(d4 + 1) * 512]),
                               wait=wm)
                        sch.op("vector", lambda e, sl=sl, PK=PKs[1], d4=d4:
                               e.tensor_mul(KKE[:, sl + 512:sl + 1024], PK[:, :],
                                            MASK[:, d4 * 512:(d4 + 1) * 512]),
                               inc=(("ve", 1),))
                    elif dve1:
                        sch.op("vector", lambda e, sl=sl, PK=PKs[0]:
                               e.tensor_copy(KKE[:, sl:sl + 512], PK[:, :]),
                               wait=wm)
                        sch.op("vector", lambda e, sl=sl, PK=PKs[1]: e.tensor_copy(
                            KKE[:, sl + 512:sl + 1024], PK[:, :]),
                            inc=(("ve", 1),))
                    else:
                        sch.op("scalar", lambda e, sl=sl, PK=PKs[0]: e.copy(
                            KKE[:, sl:sl + 512], PK[:, :]),
                            wait=wm, inc=(("ac", 1),))
                        sch.op("vector", lambda e, sl=sl, PK=PKs[1]: e.tensor_copy(
                            KKE[:, sl + 512:sl + 1024], PK[:, :]),
                            wait=wm, inc=(("ve", 1),))
                    tsn[("ev", j)] = (sch.cnt["ve"], sch.cnt["ac"])
                if s <= nj - 1:
                    j = s
                    PKs = (PKA, PKB) if (j % nsl == 0) else (PB_A, PB_C)
                    w = []
                    if j >= nsl:
                        w = [("ve", tsn[("ev", j - nsl)][0])]
                        if kind != 1 or T4SPL1:
                            w.append(("ac", tsn[("ev", j - nsl)][1]))
                    for hh, PK, lo in ((0, PKs[0], 0), (1, PKs[1], 64)):
                        sch.op("tensor", lambda e, j=j, t0=t0, lo=lo, PK=PK,
                               rhs2=rhs2:
                               e.matmul(PK[:, :],
                                        K2[lo:lo + 64, j * 128:(j + 1) * 128],
                                        rhs2[lo:lo + 64, t0:t0 + 512],
                                        start=True, stop=True,
                                        tile_position=(lo, 0)),
                               wait=(w if hh == 0 else ()),
                               inc=((("mm", 1),) if hh == 1 else ()))
                    tsn[("kk", j)] = sch.cnt["mm"]
            sch.serial_mode = _sm

        for task in range(2):
            prev = sch.w("mm", "ve", "ac", "st")
            for tt in range(NTT):
                t0 = tt * 512
                for kc in range(NCH):
                    sch.dma("sync", XT[:, kc * 512:(kc + 1) * 512],
                            xs2f[task, kc, :, t0:t0 + 512],
                            wait=(prev + ldc if (tt == 0 and kc == 0)
                                  else sch.w("mm") if kc == 0 else ()))
                ldt = sch.cnt["ld"]
                for oi, DST in ((0, Q2), (1, K2), (2, V2)):
                    for kc in range(NCH):
                        sch.op("tensor", lambda e, oi=oi, kc=kc: e.matmul(
                            PU[:, :], WPB[:, kc * 384 + oi * 128:
                                          kc * 384 + (oi + 1) * 128],
                            XT[:, kc * 512:(kc + 1) * 512],
                            start=(kc == 0), stop=(kc == NCH - 1)),
                            wait=((("ld", ldt),) + (init_w if tt == 0 else ())
                                  + sch.w("ac") if kc == 0 else ()),
                            inc=((("mm", 1),) if kc == NCH - 1 else ()))
                    sch.op("scalar", lambda e, oi=oi, DST=DST, t0=t0:
                           e.activation(DST[:, t0:t0 + 512], PU[:, :],
                                        AF.Identity, bias=BP[:, oi:oi + 1]),
                           wait=sch.w("mm"), inc=(("ac", 1),))
            proj_done = sch.w("ac")
            if dbg and task == 0:
                for oi, SRC in ((0, Q2), (1, K2), (2, V2)):
                    sch.dma("sync", dbg_qkv[oi, :, :], SRC[:, :], sem="st",
                            wait=proj_done)

            for i in range(NTT):
                t0 = i * 512
                sch.op("tensor", lambda e, t0=t0: e.matmul(
                    PU[:, :], W0B[:, :], K2[:, t0:t0 + 512],
                    start=True, stop=True),
                    wait=(proj_done if i == 0 else sch.w("ac")) + sch.w("ve"),
                    inc=(("mm", 1),))
                sch.op("scalar", lambda e: e.copy(ZT[:, :], PU[:, :]),
                       wait=sch.w("mm"), inc=(("ac", 1),))
                dz_chain(ZT[:, :], DZ1[:, t0:t0 + 512], DZNT[:, :], t0)
                transposes(DZNT, D1TA, D1TB, i)
                triangle(i, t0, K2, D1TA, D1TB, K2, kind=1)
                sch.op("scalar", lambda e: e.copy(Z2T[:, :], PU[:, :]),
                       wait=sch.w("mm"), inc=(("ac", 1),))
                if dbg and task == 0 and i == 1:
                    sch.dma("sync", dbg_z2[:, :], Z2T[:, :], sem="st",
                            wait=sch.w("ac"))
                dz_chain(Z2T[:, :], None, DZNT[:, :], t0)
                sch.op("vector", lambda e, t0=t0: e.tensor_sub(
                    PRB[:, :], DZNT[:, :], DZ1[:, t0:t0 + 512]),
                    inc=(("ve", 1),))
                transposes(PRB, DSTA, DSTB, i)
                triangle(i, t0, Q2, DSTA, DSTB, Q2, kind=2)
                sch.op("scalar", lambda e, t0=t0: e.copy(
                    CTX[:, t0:t0 + 512], PU[:, :]),
                    wait=sch.w("mm"), inc=(("ac", 1),))

            if dbg and task == 0:
                sch.dma("sync", dbg_dz1[:, :], DZ1[:, :], sem="st",
                        wait=sch.w("ve", "ac"))
                sch.dma("sync", dbg_ctx[:, :], CTX[:, :], sem="st",
                        wait=sch.w("ve", "ac"))
            if T4DBG2 and task == 0:
                sch.dma("sync", dbg_dz1[:, :], DZ1[:, :], sem="st",
                        wait=sch.w("ve", "ac", "mm"))
                sch.dma("sync", dbg_ctx[:, :], CTX[:, :], sem="st",
                        wait=sch.w("ve", "ac", "mm"))
                sch.dma("sync", dbg_z2[:, :], Z2T[:, :], sem="st",
                        wait=sch.w("ve", "ac", "mm"))
            if dbg and task == 1:
                sch.dma("sync", dbg_ctx1[:, :], CTX[:, :], sem="st",
                        wait=sch.w("ve", "ac"))
            sch.serial_mode = True
            for i in range(NTT):
                t0 = i * 512
                sch.op("tensor", lambda e, t0=t0, i=i: e.matmul(
                    PB_A[0:2, 0:512], SEL2[:, :], CTX[:, t0:t0 + 512],
                    start=(i == 0), stop=(i == NTT - 1)),
                    wait=(sch.w("ac", "ve") if i == 0 else ()),
                    inc=((("mm", 1),) if i == NTT - 1 else ()))
            for i in range(NTT):
                t0 = i * 512
                sch.op("scalar", lambda e, t0=t0: e.activation(
                    SQB[:, :], CTX[:, t0:t0 + 512], AF.Square),
                    wait=sch.w("mm"), inc=(("ac", 1),))
                sch.op("tensor", lambda e, i=i: e.matmul(
                    PB_B[0:2, 0:512], SEL2[:, :], SQB[:, :],
                    start=(i == 0), stop=(i == NTT - 1)),
                    wait=sch.w("ac"), inc=(("mm", 1),))
            m = sch.w("mm")
            sch.op("vector", lambda e: e.reduce_sum(
                GSC[0:2, 0:1], PB_A[0:2, 0:512], axis=AX.X), wait=m)
            sch.op("vector", lambda e: e.reduce_sum(
                GSC[0:2, 1:2], PB_B[0:2, 0:512], axis=AX.X))
            NT64 = 1.0 / (64.0 * T)
            sch.op("vector", lambda e: e.tensor_scalar_mul(
                GSC[0:2, 0:1], GSC[0:2, 0:1], NT64))
            sch.op("vector", lambda e: e.tensor_scalar_mul(
                GSC[0:2, 1:2], GSC[0:2, 1:2], NT64))
            sch.op("vector", lambda e: e.tensor_mul(
                GSC[0:2, 2:3], GSC[0:2, 0:1], GSC[0:2, 0:1]))
            sch.op("vector", lambda e: e.tensor_sub(
                GSC[0:2, 1:2], GSC[0:2, 1:2], GSC[0:2, 2:3]),
                inc=(("ve", 1),))
            sch.op("scalar", lambda e: e.activation(
                GSC[0:2, 1:2], GSC[0:2, 1:2], AF.Sqrt, bias=EPSC[0:2, 0:1]),
                wait=sch.w("ve"), inc=(("ac", 1),))
            sch.op("vector", lambda e: e.reciprocal(
                GSC[0:2, 1:2], GSC[0:2, 1:2]), wait=sch.w("ac"),
                inc=(("ve", 1),))
            sch.op("tensor", lambda e: e.matmul(
                PB_S[:, 0:1], S2TF[0:2, :], GSC[0:2, 1:2],
                start=True, stop=True), wait=sch.w("ve"))
            sch.op("tensor", lambda e: e.matmul(
                PB_S[:, 1:2], S2TF[0:2, :], GSC[0:2, 0:1],
                start=False, stop=True), inc=(("mm", 1),))
            if dbg:
                sch.op("vector", lambda e: e.tensor_copy(
                    TS1[:, 0:2], PB_S[:, 0:2]), wait=sch.w("mm"),
                    inc=(("ve", 1),))
                sch.dma("sync", dbg_pbs[task, :, :], TS1[:, 0:2], sem="st",
                        wait=sch.w("ve"))
                if task == 0:
                    sch.dma("sync", dbg_lng[:, :], LNG[:, :], sem="st")
            sch.op("vector", lambda e: e.tensor_mul(
                GSC[:, 2:3], PB_S[:, 0:1], LNG[:, 2:3]), wait=sch.w("mm"))
            sch.op("vector", lambda e: e.tensor_mul(
                GSC[:, 3:4], PB_S[:, 1:2], GSC[:, 2:3]))
            sch.op("vector", lambda e: e.tensor_sub(
                GSC[:, 4:5], LNG[:, 3:4], GSC[:, 3:4]), inc=(("ve", 1),))
            gsc_done = sch.w("ve")
            if dbg:
                sch.dma("sync", dbg_gsc[task, :, 0:5], GSC[:, 0:5], sem="st",
                        wait=gsc_done)
            for i in range(NTT):
                t0 = i * 512
                sl = (i % 2) * 512
                stw = sch.cnt["st"] - 16 if i >= 2 else 0
                sch.op("scalar", lambda e, t0=t0, sl=sl: e.activation(
                    KKE[:, sl:sl + 512], CTX[:, t0:t0 + 512], AF.Identity,
                    bias=GSC[:, 4:5], scale=GSC[:, 2:3]),
                    wait=(gsc_done if i < 2 else (("st", stw),)),
                    inc=(("ac", 1),))
                sch.dma("sync", ctxout[task, :, t0:t0 + 512],
                        KKE[:, sl:sl + 512], sem="st", wait=sch.w("ac"))
            sch.serial_mode = False

        sch.op("gpsimd", lambda e: e.nop(), wait=(("st", sch.cnt["st"]),))
        sch.build_block()
    return nc



TE = TL + 1
T5TILES = [(0, 1), (1, 512), (513, 512), (1025, 512), (1537, 512)]


def build_launch5():
    nc = bass.Bass("TRN2", target_bir_lowering=False, debug=False)
    x2e = nc.dram_tensor("x2e", [NCH, 128, TE], F32, kind="ExternalInput").ap()
    ctxe = nc.dram_tensor("ctxe", [NCH, 128, TE], BF16, kind="ExternalInput").ap()
    wout = nc.dram_tensor("wout", [D, D], F32, kind="ExternalInput").ap()
    wrt = nc.dram_tensor("wrt", [D, D], F32, kind="ExternalInput").ap()
    wkt = nc.dram_tensor("wkt", [D, D], F32, kind="ExternalInput").ap()
    wvt = nc.dram_tensor("wvt", [D, D], F32, kind="ExternalInput").ap()
    bo_c = nc.dram_tensor("bo_c", [128, 8], F32, kind="ExternalInput").ap()
    mur_c = nc.dram_tensor("mur_c", [128, 8], F32, kind="ExternalInput").ap()
    muk_c = nc.dram_tensor("muk_c", [128, 8], F32, kind="ExternalInput").ap()
    bmask = nc.dram_tensor("bmask", [128, 1], F32, kind="ExternalInput").ap()
    idenf = nc.dram_tensor("idenf", [128, 128], F32, kind="ExternalInput").ap()
    out = nc.dram_tensor("out", [TL, D], F32, kind="ExternalOutput").ap()

    with ExitStack() as es:
        sch = Sched(nc, es)
        sch.serial_mode = True
        sch.sem("ld", "lg", "mm", "ve", "ac", "st")
        X3 = [_mk(nc, es, f"x3{c}", [128, TE], F32) for c in range(NCH)]
        WB = _mk(nc, es, "wb", [128, NCH * 128], BF16)
        BO = _mk(nc, es, "bo", [128, 8], F32)
        MUR = _mk(nc, es, "mur", [128, 8], F32)
        MUK = _mk(nc, es, "muk", [128, 8], F32)
        OMR = _mk(nc, es, "omr", [128, 8], F32)
        OMK = _mk(nc, es, "omk", [128, 8], F32)
        BM = _mk(nc, es, "bm", [128, 1], F32)
        ONESB = _mk(nc, es, "onesb", [128, 1], BF16)
        ONESR = _mk(nc, es, "onesr", [1, 512], F32)
        EPSC = _mk(nc, es, "epsc", [128, 1], F32)
        MU = _mk(nc, es, "mu", [1, TE], F32)
        Q2 = _mk(nc, es, "q2", [1, TE], F32)
        RR = _mk(nc, es, "rr", [1, TE], F32)
        NMR = _mk(nc, es, "nmr", [1, TE], F32)
        SQS = _mk(nc, es, "sqs", [128, 512], BF16)
        XBS = _mk(nc, es, "xbs", [128, 512], BF16)
        TMP = _mk(nc, es, "tmp", [128, 512], F32)
        P0 = _ps(nc, es, "p0", [128, 512])
        P1 = _ps(nc, es, "p1", [128, 512])
        P2 = _ps(nc, es, "p2", [128, 512])
        PT = _ps(nc, es, "pt", [128, 128])

        sch.op("vector", lambda e: e.memset(ONESB[:, :], 1.0))
        sch.op("vector", lambda e: e.memset(EPSC[:, :], EPS))
        sch.op("vector", lambda e: e.memset(ONESR[:, :], 1.0), inc=(("ve", 1),))
        for c in range(NCH):
            sch.dma("sync", X3[c][:, :], x2e[c, :, :])
        for name, dst, srcd in (("bo", BO, bo_c), ("mur", MUR, mur_c),
                                ("muk", MUK, muk_c), ("bm", BM, bmask)):
            sch.dma("sync", dst[:, :], srcd[:, :])
        ld0 = sch.cnt["ld"]
        sch.op("vector", lambda e: e.tensor_scalar(
            OMR[:, :], MUR[:, :], -1.0, 1.0, ALU.mult, ALU.add),
            wait=(("ld", ld0),))
        sch.op("vector", lambda e: e.tensor_scalar(
            OMK[:, :], MUK[:, :], -1.0, 1.0, ALU.mult, ALU.add),
            inc=(("ve", 1),))

        def proj(wdram, rhs_chunks, evac, nt_tiles, extra_wait=()):
            for dc in range(NCH):
                prev_mm = sch.cnt["mm"]
                for kc in range(NCH):
                    sch.dma("gpsimd", WB[:, kc * 128:(kc + 1) * 128],
                            wdram[kc * 128:(kc + 1) * 128,
                                  dc * 128:(dc + 1) * 128], sem="lg",
                            wait=((("mm", prev_mm),) if kc == 0 else ()))
                ldw = sch.cnt["lg"]
                for t0, tw in nt_tiles:
                    for kc in range(NCH):
                        first = t0 == nt_tiles[0][0] and kc == 0
                        sch.op("tensor", lambda e, t0=t0, tw=tw, kc=kc:
                               e.matmul(P0[:, 0:tw],
                                        WB[:, kc * 128:(kc + 1) * 128],
                                        rhs_chunks[kc][:, t0:t0 + tw],
                                        start=(kc == 0), stop=(kc == NCH - 1)),
                               wait=(((("lg", ldw),) + extra_wait + sch.w("ve"))
                                     if first else sch.w("ac") if kc == 0
                                     else ()),
                               inc=((("mm", 1),) if kc == NCH - 1 else ()))
                    evac(dc, t0, tw)

        with ExitStack() as es2:
            CTXS = [_mk(nc, es2, f"ctxs{c}", [128, TE], BF16) for c in range(NCH)]
            OB = [_mk(nc, es2, f"ob{c}", [128, TE], BF16) for c in range(NCH)]
            for c in range(NCH):
                sch.dma("gpsimd", CTXS[c][:, :], ctxe[c, :, :], sem="lg")
            ldc = sch.cnt["lg"]

            def evac_o(dc, t0, tw):
                sch.op("scalar", lambda e, dc=dc, t0=t0, tw=tw: e.activation(
                    OB[dc][:, t0:t0 + tw], P0[:, 0:tw], AF.Identity,
                    bias=BO[:, dc:dc + 1]),
                    wait=sch.w("mm"), inc=(("ac", 1),))

            proj(wout, CTXS, evac_o, T5TILES,
                 extra_wait=(("lg", ldc), ("ld", ld0)))
            emit_stats_1024(sch, OB, ONESB, P0, MU, Q2, TE, SQS,
                            first_wait=sch.w("ac"))
            emit_rows_rstd(sch, MU, Q2, RR, NMR, Q2, TE, double_norm=False,
                           epsc=EPSC)
            emit_neg_murr(sch, MU, RR, NMR, TE)
            emit_ln_apply(sch, OB, X3, RR, NMR, TE, P1, P2, TMP, ONESR,
                          add_chunks=X3)

        with ExitStack() as es3:
            XRK = [_mk(nc, es3, f"xrk{c}", [128, TL], BF16) for c in range(NCH)]
            R = [_mk(nc, es3, f"r{c}", [128, TL], BF16) for c in range(NCH)]
            with ExitStack() as es4:
                XS3 = [_mk(nc, es4, f"xs3{c}", [128, TE], BF16)
                       for c in range(NCH)]
                emit_stats_1024(sch, X3, ONESB, P0, MU, Q2, TE, SQS, xbs=XBS)
                emit_rows_rstd(sch, MU, Q2, RR, NMR, Q2, TE, double_norm=False,
                               epsc=EPSC)
                emit_neg_murr(sch, MU, RR, NMR, TE)
                emit_ln_apply(sch, X3, XS3, RR, NMR, TE, P1, P2, TMP, ONESR)
                for c in range(NCH):
                    sch.op("vector", lambda e, c=c: e.tensor_scalar(
                        SQS[:, 0:1], XS3[c][:, 0:1], BM[:, 0:1], None,
                        ALU.mult),
                        wait=(sch.w("ve") if c == 0 else ()))
                    sch.op("vector", lambda e, c=c: e.tensor_copy(
                        XS3[c][:, 0:1], SQS[:, 0:1]),
                        inc=((("ve", 1),) if c == NCH - 1 else ()))

                def mix(dst, mcol, ocol):
                    for c in range(NCH):
                        for t0 in range(0, TL, 512):
                            sch.op("vector", lambda e, c=c, t0=t0:
                                   e.tensor_scalar(
                                       dst[c][:, t0:t0 + 512],
                                       XS3[c][:, 1 + t0:1 + t0 + 512],
                                       mcol[:, c:c + 1], None, ALU.mult),
                                   wait=(sch.w("ve") if t0 == 0 and c == 0
                                         else ()))
                            sch.op("vector", lambda e, c=c, t0=t0:
                                   e.tensor_scalar(
                                       SQS[:, :], XS3[c][:, t0:t0 + 512],
                                       ocol[:, c:c + 1], None, ALU.mult))
                            sch.op("vector", lambda e, c=c, t0=t0:
                                   e.tensor_add(dst[c][:, t0:t0 + 512],
                                                dst[c][:, t0:t0 + 512],
                                                SQS[:, :]),
                                   inc=(("ve", 1),))

                mix(XRK, MUR, OMR)

                def evac_r(dc, t0, tw):
                    sch.op("scalar", lambda e, dc=dc, t0=t0, tw=tw:
                           e.activation(R[dc][:, t0:t0 + tw], P0[:, 0:tw],
                                        AF.Sigmoid),
                           wait=sch.w("mm"), inc=(("ac", 1),))

                tiles4 = [(i * 512, 512) for i in range(4)]
                proj(wrt, XRK, evac_r, tiles4)
                mix(XRK, MUK, OMK)
            with ExitStack() as es5:
                KA = [_mk(nc, es5, f"ka{c}", [128, TL], BF16) for c in range(NCH)]

                def evac_k(dc, t0, tw):
                    sch.op("scalar", lambda e, dc=dc, t0=t0, tw=tw:
                           e.activation(TMP[:, 0:tw], P0[:, 0:tw], AF.Relu),
                           wait=sch.w("mm"), inc=(("ac", 1),))
                    sch.op("scalar", lambda e, dc=dc, t0=t0, tw=tw:
                           e.activation(KA[dc][:, t0:t0 + tw], TMP[:, 0:tw],
                                        AF.Square),
                           inc=(("ac", 1),))

                proj(wkt, XRK, evac_k, tiles4)

                def evac_v(dc, t0, tw):
                    sch.op("vector", lambda e, dc=dc, t0=t0, tw=tw:
                           e.tensor_mul(XRK[dc][:, t0:t0 + tw],
                                        R[dc][:, t0:t0 + tw], P0[:, 0:tw]),
                           wait=sch.w("mm"), inc=(("ac", 1),))

                proj(wvt, KA, evac_v, tiles4)
                emit_stats_1024(sch, XRK, ONESB, P0, MU, Q2, TL, SQS)
                emit_rows_rstd(sch, MU, Q2, RR, NMR, Q2, TL, double_norm=False,
                               epsc=EPSC)
                emit_neg_murr(sch, MU, RR, NMR, TL)
                X3s = [X3[c][:, 1:TE] for c in range(NCH)]
                emit_ln_apply(sch, XRK, X3s, RR, NMR, TL, P1, P2, TMP, ONESR,
                              add_chunks=X3s)

        OSTG = _mk(nc, es, "ostg", [128, D], F32)
        IDF = _mk(nc, es, "idf", [128, 128], F32)
        sch.dma("sync", IDF[:, :], idenf[:, :])
        ldi = sch.cnt["ld"]
        sch.op("tensor", lambda e: e.nop(), wait=(("ld", ldi),))
        for j in range(TL // 128):
            for c in range(NCH):
                sch.op("tensor", lambda e, j=j, c=c: e.transpose(
                    PT[:, :], X3[c][:, 1 + j * 128:1 + (j + 1) * 128],
                    IDF[:, :]),
                    wait=sch.w("ve"), inc=(("mm", 1),))
                sch.op("vector", lambda e, c=c: e.tensor_copy(
                    OSTG[:, c * 128:(c + 1) * 128], PT[:, :]),
                    wait=sch.w("mm"), inc=(("ve", 1),))
            stp = sch.cnt["st"]
            sch.dma("sync", out[j * 128:(j + 1) * 128, :], OSTG[:, :],
                    sem="st", wait=sch.w("ve"))
            sch.op("vector", lambda e: e.nop(), wait=(("st", stp + 16),))

        sch.op("gpsimd", lambda e: e.nop(), wait=(("st", sch.cnt["st"]),))
        sch.build_block()
    return nc


def build_launch2_noveng():
    nc = bass.Bass("TRN2", target_bir_lowering=False, debug=False)
    q2in = nc.dram_tensor("q2in", [2, 128, T], BF16, kind="ExternalInput").ap()
    k2in = nc.dram_tensor("k2in", [2, 128, T], BF16, kind="ExternalInput").ap()
    v1in = nc.dram_tensor("v1in", [2, 2, 128, 64, 65], BF16,
                          kind="ExternalInput").ap()
    yout = nc.dram_tensor("yout", [2, 2, 64, T], BF16, kind="ExternalOutput").ap()
    NKT = T // 128
    NTT = T // 512

    with ExitStack() as es:
        sch = Sched(nc, es)
        sch.sem("ld", "lg", "mm", "ve", "ac", "st")
        Q2 = _mk(nc, es, "q2", [128, T], BF16)
        K2 = _mk(nc, es, "k2", [128, T], BF16)
        V1 = [_mk(nc, es, f"v1{h}", [128, 64 * 65], BF16) for h in range(2)]
        SQT = _mk(nc, es, "sqt", [128, T], BF16)
        PA = [_mk(nc, es, f"pa{s}", [128, 1024], BF16) for s in range(8)]
        NQ = _mk(nc, es, "nq", [1, T], F32)
        SC = _mk(nc, es, "sc", [1, 16], F32)
        NB = _mk(nc, es, "nb", [128, 2], F32)
        OE = _mk(nc, es, "oe", [128, 512], F32)
        RE = _mk(nc, es, "re", [128, 512], F32)
        YH = [_mk(nc, es, f"yh{h}", [64, T], BF16) for h in range(2)]
        ONESB = _mk(nc, es, "onesb", [128, 1], BF16)
        ONESF = _mk(nc, es, "onesf", [128, 128], F32)
        PS = [_ps(nc, es, f"ps{s}", [128, 1024]) for s in range(3)]
        PO_A = _ps(nc, es, "poa", [128, 512])
        PO_B = _ps(nc, es, "pob", [128, 512])
        PR = PS[0]

        sch.op("vector", lambda e: e.memset(ONESB[:, :], 1.0))
        sch.op("vector", lambda e: e.memset(ONESF[:, :], 1.0), inc=(("ve", 1),))

        for task in range(2):
            w_prev = sch.w("mm", "ac", "st")
            sch.dma("sync", Q2[:, :], q2in[task, :, :], wait=w_prev)
            sch.dma("sync", K2[:, :], k2in[task, :, :])
            for h in range(2):
                sch.dma("sync", V1[h][:, :],
                        v1in[task, h, :, :, :].rearrange("p a b -> p (a b)"))
            ld0 = sch.cnt["ld"]

            for which, SRC in enumerate((Q2, K2)):
                sch.op("scalar", lambda e, SRC=SRC: e.activation(
                    SQT[:, :], SRC[:, :], AF.Square),
                    wait=(("ld", ld0),) + sch.w("mm"), inc=(("ac", 1),))
                for h in range(2):
                    lo = 64 * h
                    nsn = {}
                    for tt in range(NTT):
                        bank = PS[tt % 2][:, 0:512]
                        w = sch.w("ac", "ve") if tt == 0 else (
                            ("ve", nsn[tt - 2]),) if tt >= 2 else ()
                        sch.op("tensor",
                               lambda e, lo=lo, tt=tt, bank=bank: e.matmul(
                                   bank[0:1, :], ONESB[lo:lo + 64, 0:1],
                                   SQT[lo:lo + 64, tt * 512:(tt + 1) * 512],
                                   start=True, stop=True),
                               wait=w, inc=(("mm", 1),))
                        msn = sch.cnt["mm"]
                        sch.op("vector", lambda e, tt=tt, bank=bank:
                               e.tensor_copy(NQ[0:1, tt * 512:(tt + 1) * 512],
                                             bank[0:1, :]),
                               wait=(("mm", msn),), inc=(("ve", 1),))
                        nsn[tt] = sch.cnt["ve"]
                    j = which * 4 + h
                    sch.op("vector", lambda e, j=j: e.reduce_max(
                        SC[0:1, j:j + 1], NQ[0:1, 0:T], axis=AX.X),
                        inc=(("ve", 1),))
            sch.op("scalar", lambda e: e.activation(SC[0:1, 8:14], SC[0:1, 0:6],
                                                    AF.Sqrt),
                   wait=sch.w("ve"), inc=(("ac", 1),))
            for h in range(2):
                sch.op("vector", lambda e, h=h: e.tensor_mul(
                    SC[0:1, 12 + h:13 + h], SC[0:1, 8 + h:9 + h],
                    SC[0:1, 12 + h:13 + h]), wait=sch.w("ac"),
                    inc=(("ve", 1),))
            sch.op("vector", lambda e: e.tensor_max(
                SC[0:1, 14:15], SC[0:1, 12:13], SC[0:1, 13:14]))
            sch.op("vector", lambda e: e.tensor_scalar_mul(
                SC[0:1, 15:16], SC[0:1, 14:15], -0.125), inc=(("ve", 1),))
            sch.op("tensor", lambda e: e.matmul(
                PR[:, 0:1], ONESF[0:1, 0:128], SC[0:1, 15:16],
                start=True, stop=True),
                wait=sch.w("ve"), inc=(("mm", 1),))
            sch.op("vector", lambda e: e.tensor_copy(
                NB[:, 0:1], PR[:, 0:1]),
                wait=sch.w("mm"), inc=(("ve", 1),))
            bnd = sch.w("ve")

            snap = {}
            KG = [list(range(g, min(g + 2, NKT))) for g in range(0, NKT, 2)]
            macros = [(h, kcs) for kcs in KG for h in (0, 1)]
            NM = len(macros)
            for ti in range(NTT):
                t0 = ti * 512
                for step in range(NM + 3):
                    if NM - 1 + 2 >= step >= 2:
                        mi = ti * NM + step - 2
                        sl = mi % 3
                        pl = mi % 8
                        nk = len(macros[step - 2][1]) * 512
                        v = snap[("S", mi)]
                        if mi >= 8:
                            v = max(v, snap[("PV", mi - 8)])
                        sch.op("scalar", lambda e: e.nop(),
                               wait=(("mm", v),), inc=(("ac", 1),))
                        snap[("exp", mi)] = sch.cnt["ac"]
                    if step <= NM - 1:
                        mi = ti * NM + step
                        h, kcs = macros[step]
                        lo = 64 * h
                        sl = mi % 3
                        w = []
                        if mi >= 3:
                            w.append(("ac", snap[("exp", mi - 3)]))
                        if step == 0 and ti > 0:
                            w.append(("ve", snap[("fin", ti - 1)]))
                        if ti == 0 and step == 0:
                            w = list(bnd)
                        for q, kc in enumerate(kcs):
                            sch.op("tensor",
                                   lambda e, sl=sl, kc=kc, t0=t0, q=q, lo=lo:
                                   e.matmul(PS[sl][:, q * 512:(q + 1) * 512],
                                            K2[lo:lo + 64,
                                               kc * 128:(kc + 1) * 128],
                                            Q2[lo:lo + 64, t0:t0 + 512],
                                            start=True, stop=True),
                                   wait=(w if q == 0 else ()),
                                   inc=((("mm", 1),) if q == len(kcs) - 1
                                        else ()))
                        snap[("S", mi)] = sch.cnt["mm"]
                    if step >= 3:
                        mi = ti * NM + step - 3
                        h, kcs = macros[step - 3]
                        pl = mi % 8
                        PO = PO_A if h == 0 else PO_B
                        for q, kc in enumerate(kcs):
                            sch.op("tensor", lambda e, pl=pl, kc=kc, q=q, PO=PO:
                                   e.matmul(PO[0:65, :],
                                            V1[0 if PO is PO_A else 1]
                                            [:, kc * 65:(kc + 1) * 65],
                                            PA[pl][:, q * 512:(q + 1) * 512],
                                            start=(kc == 0),
                                            stop=(kc == NKT - 1)),
                                   wait=((("ac", snap[("exp", mi)]),)
                                         if q == 0 else ()),
                                   inc=((("mm", 1),) if q == len(kcs) - 1
                                        else ()))
                        snap[("PV", mi)] = sch.cnt["mm"]
                sch.op("tensor", lambda e: e.nop(), wait=sch.w("mm"),
                       inc=(("ve", 1),))
                snap[("fin", ti)] = sch.cnt["ve"]
            for h in range(2):
                sch.dma("sync", yout[task, h, :, :], YH[h][0:64, :], sem="st",
                        wait=sch.w("ve"))

        sch.op("gpsimd", lambda e: e.nop(), wait=(("st", sch.cnt["st"]),))
        sch.build_block()
    return nc




# version 04 (final)
# speedup vs baseline: 2.3911x; 1.1057x over previous
import math
import numpy as np
import ml_dtypes
from contextlib import ExitStack

import concourse.bass as bass
import concourse.mybir as mybir
from concourse.bass_utils import run_bass_kernel_spmd

F32 = mybir.dt.float32
BF16 = mybir.dt.bfloat16
I32 = mybir.dt.int32
AF = mybir.ActivationFunctionType
ALU = mybir.AluOpType
AX = mybir.AxisListType
BF = ml_dtypes.bfloat16

NCORE = 8
B, T, D = 2, 8192, 1024
NTOK = B * T
TL = NTOK // NCORE
NCH = D // 128
H, HD = 16, 64
EPS = 1e-5
F_FFN = 4096

ENGINES = ("sync", "vector", "scalar", "tensor", "gpsimd")
SERIAL = False
T4DVE = False
T4LAG1 = False
T4TSER = False
T4TSER1 = False
T4K1DVE = False
T4SLK = False
T4DBG2 = False
T4SPL1 = False
L5SER = True
T4TSER2 = False
SER_LO = 0
SER_HI = 0
RLX_LO = 0
RLX_HI = 0
RLX_ENG = ""


class Sched:

    def __init__(self, nc, es):
        self.nc = nc
        self.es = es
        self.ops = {e: [] for e in ENGINES}
        self.cnt = {}
        self.sems = {}

    def sem(self, *names):
        for name in names:
            self.sems[name] = self.es.enter_context(self.nc.semaphore(name))
            self.cnt[name] = 0

    def op(self, eng, fn, wait=(), inc=()):
        self.opi = getattr(self, "opi", 0) + 1
        relaxed = (RLX_LO <= self.opi < RLX_HI) and (
            RLX_ENG == "" or eng == RLX_ENG)
        if SERIAL or getattr(self, "serial_mode", False) or (
                SER_LO <= self.opi < SER_HI) or (
                RLX_LO < RLX_HI and not relaxed):
            wait = tuple(wait) + tuple((n, v) for n, v in self.cnt.items())
        self.ops[eng].append((fn, tuple(wait), tuple(inc)))
        for s, a in inc:
            self.cnt[s] += a
        return self.cnt

    def dma(self, eng, out, in_, sem="ld", wait=(), **kw):
        self.op(eng, lambda e, o=out, i=in_, k=kw: e.dma_start(out=o, in_=i, **k),
                wait=wait, inc=((sem, 16),))
        return self.cnt[sem]

    def run_engine(self, name, handle):
        for fn, waits, incs in self.ops[name]:
            for s, v in waits:
                handle.wait_ge(self.sems[s], v)
            r = fn(handle)
            for s, a in incs:
                r = r.then_inc(self.sems[s], a)

    def build_block(self):
        block = self.es.enter_context(self.nc.Block())
        for name in ENGINES:
            getattr(block, name)(lambda h, n=name: self.run_engine(n, h))

    def w(self, *names):
        return tuple((n, self.cnt[n]) for n in names)


def _mk(nc, es, name, shape, dtype):
    return es.enter_context(nc.sbuf_tensor(name, shape, dtype))


def _ps(nc, es, name, shape, dtype=F32):
    return es.enter_context(nc.psum_tensor(name, shape, dtype))



def emit_stats_1024(sch, chunks, onesb, pb, mu, q2, nt, sqs, xbs=None,
                    first_wait=(), nch=None):
    if nch is None:
        nch = len(chunks)
    bf = chunks[0][:, 0:1].dtype == BF16
    tiles = [(i * 512, min(512, nt - i * 512)) for i in range((nt + 511) // 512)]
    first = True
    for t0, tw in tiles:
        for c in range(nch):
            last = c == nch - 1
            if bf:
                sch.op("tensor", lambda e, c=c, t0=t0, tw=tw, last=last:
                       e.matmul(pb[0:1, 0:tw], onesb[:, 0:1],
                                chunks[c][:, t0:t0 + tw], start=(c == 0),
                                stop=last),
                       wait=(first_wait + sch.w("ve") if first
                             else sch.w("ve") if c == 0 else ()),
                       inc=((("mm", 1),) if last else ()))
            else:
                sch.op("vector", lambda e, c=c, t0=t0, tw=tw: e.tensor_copy(
                    xbs[:, 0:tw], chunks[c][:, t0:t0 + tw]),
                    wait=(first_wait + sch.w("mm") if first else sch.w("mm")),
                    inc=(("ve", 1),))
                sch.op("tensor", lambda e, tw=tw, c=c, last=last: e.matmul(
                    pb[0:1, 0:tw], onesb[:, 0:1], xbs[:, 0:tw],
                    start=(c == 0), stop=last),
                    wait=sch.w("ve"), inc=(("mm", 1),))
            first = False
        sch.op("vector", lambda e, t0=t0, tw=tw: e.tensor_scalar_mul(
            mu[0:1, t0:t0 + tw], pb[0:1, 0:tw], 1.0 / (nch * 128)),
            wait=sch.w("mm"), inc=(("ve", 1),))
    for t0, tw in tiles:
        for c in range(nch):
            last = c == nch - 1
            sch.op("scalar", lambda e, c=c, t0=t0, tw=tw: e.activation(
                sqs[:, 0:tw], chunks[c][:, t0:t0 + tw], AF.Square),
                wait=sch.w("mm"), inc=(("ac", 1),))
            sch.op("tensor", lambda e, tw=tw, c=c, last=last: e.matmul(
                pb[0:1, 0:tw], onesb[:, 0:1], sqs[:, 0:tw],
                start=(c == 0), stop=last),
                wait=sch.w("ac", "ve"), inc=(("mm", 1),))
        sch.op("vector", lambda e, t0=t0, tw=tw: e.tensor_scalar_mul(
            q2[0:1, t0:t0 + tw], pb[0:1, 0:tw], 1.0 / (nch * 128)),
            wait=sch.w("mm"), inc=(("ve", 1),))


def emit_rows_rstd(sch, mu, q2, rr, s0, s1, nt, double_norm, epsc=None, r1_out=None):
    sch.op("vector", lambda e: e.tensor_mul(s0[0:1, 0:nt], mu[0:1, 0:nt],
                                            mu[0:1, 0:nt]))
    sch.op("vector", lambda e: e.tensor_sub(s0[0:1, 0:nt], q2[0:1, 0:nt],
                                            s0[0:1, 0:nt]), inc=(("ve", 1),))
    sch.op("scalar", lambda e: e.activation(s1[0:1, 0:nt], s0[0:1, 0:nt],
                                            AF.Sqrt, bias=epsc[0:1, 0:1]),
           wait=sch.w("ve"), inc=(("ac", 1),))
    tgt = r1_out if (double_norm and r1_out is not None) else rr
    sch.op("vector", lambda e, t=tgt: e.reciprocal(t[0:1, 0:nt], s1[0:1, 0:nt]),
           wait=sch.w("ac"), inc=(("ve", 1),))
    if double_norm:
        r1 = r1_out if r1_out is not None else rr
        sch.op("vector", lambda e, r1=r1: e.tensor_mul(
            s1[0:1, 0:nt], r1[0:1, 0:nt], r1[0:1, 0:nt]))
        sch.op("vector", lambda e: e.tensor_mul(s1[0:1, 0:nt], s0[0:1, 0:nt],
                                                s1[0:1, 0:nt]), inc=(("ve", 1),))
        sch.op("scalar", lambda e: e.activation(s1[0:1, 0:nt], s1[0:1, 0:nt],
                                                AF.Sqrt, bias=epsc[0:1, 0:1]),
               wait=sch.w("ve"), inc=(("ac", 1),))
        sch.op("vector", lambda e: e.reciprocal(s0[0:1, 0:nt], s1[0:1, 0:nt]),
               wait=sch.w("ac"))
        sch.op("vector", lambda e, r1=r1: e.tensor_mul(
            rr[0:1, 0:nt], r1[0:1, 0:nt], s0[0:1, 0:nt]), inc=(("ve", 1),))


def emit_ln_apply(sch, src_chunks, dst_chunks, rr, nmr, nt, p1, p2, tmp,
                  onesr, g_row=None, b_row=None, add_chunks=None,
                  dst_is_bf16=True):
    tiles = [(i * 512, min(512, nt - i * 512)) for i in range((nt + 511) // 512)]
    nchunks = len(src_chunks)
    for c in range(nchunks):
        for t0, tw in tiles:
            sch.op("tensor", lambda e, c=c, t0=t0, tw=tw: e.matmul(
                p1[:, 0:tw],
                (g_row[0:1, c * 128:(c + 1) * 128] if g_row is not None
                 else onesr[0:1, 0:128]),
                rr[0:1, t0:t0 + tw], start=True, stop=True),
                wait=sch.w("ve"))
            sch.op("tensor", lambda e, c=c, t0=t0, tw=tw: e.matmul(
                p2[:, 0:tw],
                (g_row[0:1, c * 128:(c + 1) * 128] if g_row is not None
                 else onesr[0:1, 0:128]),
                nmr[0:1, t0:t0 + tw], start=True, stop=(b_row is None)),
                inc=((("mm", 1),) if b_row is None else ()))
            if b_row is not None:
                sch.op("tensor", lambda e, c=c, tw=tw: e.matmul(
                    p2[:, 0:tw], b_row[0:1, c * 128:(c + 1) * 128],
                    onesr[0:1, 0:tw], start=False, stop=True),
                    inc=(("mm", 1),))
            sch.op("vector", lambda e, c=c, t0=t0, tw=tw: e.tensor_mul(
                tmp[:, 0:tw], src_chunks[c][:, t0:t0 + tw], p1[:, 0:tw]),
                wait=sch.w("mm"))
            if add_chunks is None:
                sch.op("vector", lambda e, c=c, t0=t0, tw=tw: e.tensor_add(
                    dst_chunks[c][:, t0:t0 + tw], tmp[:, 0:tw], p2[:, 0:tw]),
                    inc=(("ve", 1),))
            else:
                sch.op("vector", lambda e, c=c, t0=t0, tw=tw: e.tensor_add(
                    tmp[:, 0:tw], tmp[:, 0:tw], p2[:, 0:tw]))
                sch.op("vector", lambda e, c=c, t0=t0, tw=tw: e.tensor_add(
                    dst_chunks[c][:, t0:t0 + tw], add_chunks[c][:, t0:t0 + tw],
                    tmp[:, 0:tw]), inc=(("ve", 1),))


def emit_neg_murr(sch, mu, rr, nmr, nt):
    sch.op("vector", lambda e: e.scalar_tensor_tensor(
        nmr[0:1, 0:nt], mu[0:1, 0:nt], -1.0, rr[0:1, 0:nt],
        ALU.mult, ALU.mult), inc=(("ve", 1),))



def build_launch1():
    nc = bass.Bass("TRN2", target_bir_lowering=False, debug=False)
    xin = nc.dram_tensor("xin", [NCH, 128, TL], F32, kind="ExternalInput").ap()
    wq = nc.dram_tensor("wq", [D, D], F32, kind="ExternalInput").ap()
    wk = nc.dram_tensor("wk", [D, D], F32, kind="ExternalInput").ap()
    wv = nc.dram_tensor("wv", [D, D], F32, kind="ExternalInput").ap()
    bcols = nc.dram_tensor("bcols", [128, 24], F32, kind="ExternalInput").ap()
    g_row = nc.dram_tensor("g_row", [1, D], F32, kind="ExternalInput").ap()
    b_row = nc.dram_tensor("b_row", [1, D], F32, kind="ExternalInput").ap()
    qkv_out = nc.dram_tensor("qkv", [3, NCH, 128, TL], BF16,
                             kind="ExternalOutput").ap()

    with ExitStack() as es:
        sch = Sched(nc, es)
        sch.sem("ld", "lg", "mm", "ve", "ac", "st")
        XB = [_mk(nc, es, f"xb{c}", [128, TL], BF16) for c in range(NCH)]
        XN = [_mk(nc, es, f"xn{c}", [128, TL], BF16) for c in range(NCH)]
        SQS = _mk(nc, es, "sqs", [128, 512], BF16)
        WB = _mk(nc, es, "wb", [128, NCH * D], BF16)
        BIAS = _mk(nc, es, "bias", [128, 24], F32)
        GR = _mk(nc, es, "gr", [1, D], F32)
        BR = _mk(nc, es, "br", [1, D], F32)
        ONESB = _mk(nc, es, "onesb", [128, 1], BF16)
        EPSC = _mk(nc, es, "epsc", [128, 1], F32)
        ONESR = _mk(nc, es, "onesr", [1, 512], F32)
        MU = _mk(nc, es, "mu", [1, TL], F32)
        Q2 = _mk(nc, es, "q2", [1, TL], F32)
        RR = _mk(nc, es, "rr", [1, TL], F32)
        NMR = _mk(nc, es, "nmr", [1, TL], F32)
        RS0 = _mk(nc, es, "rs0", [1, TL], F32)
        RS1 = _mk(nc, es, "rs1", [1, TL], F32)
        TMP = _mk(nc, es, "tmp", [128, 512], F32)
        OUTS = _mk(nc, es, "outs", [128, TL], BF16)
        P0 = _ps(nc, es, "p0", [128, 512])
        P1 = _ps(nc, es, "p1", [128, 512])
        P2 = _ps(nc, es, "p2", [128, 512])

        for c in range(NCH):
            sch.dma("gpsimd", XB[c][:, :], xin[c, :, :], sem="lg")
        ld_x = sch.cnt["lg"]
        sch.dma("sync", BIAS[:, :], bcols[:, :])
        sch.dma("sync", GR[0:1, :], g_row[0:1, :])
        sch.dma("sync", BR[0:1, :], b_row[0:1, :])
        ld_r = sch.cnt["ld"]

        sch.op("vector", lambda e: e.memset(ONESB[:, :], 1.0))
        sch.op("vector", lambda e: e.memset(EPSC[:, :], EPS))
        sch.op("vector", lambda e: e.memset(ONESR[:, :], 1.0),
               inc=(("ve", 1),))
        casts_w = (("lg", ld_x),)

        emit_stats_1024(sch, XB, ONESB, P0, MU, Q2, TL, SQS,
                        first_wait=casts_w)
        emit_rows_rstd(sch, MU, Q2, RR, RS0, RS1, TL, double_norm=True, epsc=EPSC)
        emit_neg_murr(sch, MU, RR, NMR, TL)
        sch.op("tensor", lambda e: e.nop(), wait=(("ld", ld_r),))
        emit_ln_apply(sch, XB, XN, RR, NMR, TL, P1, P2, TMP, ONESR,
                      g_row=GR, b_row=BR)
        xn_done = sch.cnt["ve"]

        for wi, wdram in enumerate((wq, wk, wv)):
            prev_mm = sch.cnt["mm"]
            for kc in range(NCH):
                sch.dma("gpsimd", WB[:, kc * D:(kc + 1) * D],
                        wdram[kc * 128:(kc + 1) * 128, :], sem="lg",
                        wait=((("mm", prev_mm),) if (kc == 0 and wi > 0) else ()))
            ldw = sch.cnt["lg"]
            for dc in range(NCH):
                for ti in range(TL // 512):
                    t0 = ti * 512
                    for kc in range(NCH):
                        first = kc == 0 and dc == 0 and ti == 0
                        sch.op("tensor", lambda e, dc=dc, t0=t0, kc=kc: e.matmul(
                            P0[:, :],
                            WB[:, kc * D + dc * 128: kc * D + (dc + 1) * 128],
                            XN[kc][:, t0:t0 + 512],
                            start=(kc == 0), stop=(kc == NCH - 1)),
                            wait=((("lg", ldw), ("ve", xn_done)) if first
                                  else sch.w("ac") if kc == 0 else ()),
                            inc=((("mm", 1),) if kc == NCH - 1 else ()))
                    sch.op("scalar", lambda e, wi=wi, dc=dc, t0=t0: e.activation(
                        OUTS[:, t0:t0 + 512], P0[:, :], AF.Identity,
                        bias=BIAS[:, wi * NCH + dc: wi * NCH + dc + 1]),
                        wait=sch.w("mm"), inc=(("ac", 1),))
                st_prev = sch.cnt["st"]
                sch.dma("sync", qkv_out[wi, dc, :, :], OUTS[:, :], sem="st",
                        wait=sch.w("ac"))
                sch.op("scalar", lambda e: e.nop(),
                       wait=(("st", st_prev + 16),))

        sch.op("gpsimd", lambda e: e.nop(), wait=(("st", sch.cnt["st"]),))
        sch.build_block()
    return nc



def fm_slices(x_rows):
    out = []
    dx = x_rows.shape[1]
    for c in range(NCORE):
        s = x_rows[c * TL:(c + 1) * TL]
        out.append(np.ascontiguousarray(s.T).reshape(dx // 128, 128, TL))
    return out


def fm_join(slices):
    return np.concatenate([s.reshape(s.shape[0] * 128, TL) for s in slices],
                          axis=1)


_NC_CACHE = {}


def _get_nc(name, builder):
    if name not in _NC_CACHE:
        _NC_CACHE[name] = builder()
    return _NC_CACHE[name]


def kernel(**inputs):
    inp = {k: np.asarray(v) for k, v in inputs.items()}
    cores = list(range(NCORE))

    x = inp["L0_attn_x"].reshape(NTOK, D)
    xfm = fm_slices(x)
    bcols = np.zeros((128, 24), np.float32)
    for wi, b in enumerate((inp["L0_attn_bq"], inp["L0_attn_bk"],
                            inp["L0_attn_bv"])):
        bcols[:, wi * 8:(wi + 1) * 8] = b.reshape(8, 128).T
    nc1 = _get_nc("l1", build_launch1)
    im1 = [{"xin": xfm[c], "wq": inp["L0_attn_Wq"], "wk": inp["L0_attn_Wk"],
            "wv": inp["L0_attn_Wv"], "bcols": bcols,
            "g_row": np.ascontiguousarray(inp["L0_attn_ln_g"].reshape(1, D)),
            "b_row": np.ascontiguousarray(inp["L0_attn_ln_b"].reshape(1, D))}
           for c in cores]
    r1 = run_bass_kernel_spmd(nc1, im1, core_ids=cores)
    qf = fm_join([np.asarray(r1.results[c]["qkv"][0]) for c in cores])
    kf = fm_join([np.asarray(r1.results[c]["qkv"][1]) for c in cores])
    vf = fm_join([np.asarray(r1.results[c]["qkv"][2]) for c in cores])

    im2 = []
    for c in cores:
        q2 = np.stack([qf[c * 128:(c + 1) * 128, b * T:(b + 1) * T]
                       for b in range(2)])
        k2 = np.stack([kf[c * 128:(c + 1) * 128, b * T:(b + 1) * T]
                       for b in range(2)])
        v1 = np.zeros((2, 2, 128, 64, 65), BF)
        for b in range(2):
            for h in range(2):
                vh = vf[(2 * c + h) * 64:(2 * c + h + 1) * 64,
                        b * T:(b + 1) * T]
                vt = np.ascontiguousarray(vh.T)
                v1[b, h, :, :, 0:64] = vt.reshape(64, 128, 64).transpose(1, 0, 2)
                v1[b, h, :, :, 64] = BF(1.0)
        im2.append({"q2in": np.ascontiguousarray(q2),
                    "k2in": np.ascontiguousarray(k2), "v1in": v1})
    nc2 = _get_nc("l2", build_launch2)
    r2 = run_bass_kernel_spmd(nc2, im2, core_ids=cores)
    yfull = np.zeros((D, NTOK), BF)
    for c in cores:
        yo = np.asarray(r2.results[c]["yout"])
        for b in range(2):
            for h in range(2):
                yfull[(2 * c + h) * 64:(2 * c + h + 1) * 64,
                      b * T:(b + 1) * T] = yo[b, h]

    b1c = np.ascontiguousarray(inp["L0_ffn_fc1_bias"].reshape(32, 128).T)
    b2c = np.ascontiguousarray(inp["L0_ffn_fc2_bias"].reshape(8, 128).T)
    nc3 = _get_nc("l3", build_launch3)
    im3 = [{"xin": xfm[c],
            "yin": np.ascontiguousarray(
                yfull[:, c * TL:(c + 1) * TL].reshape(NCH, 128, TL)),
            "lg_row": np.ascontiguousarray(
                inp["L0_ffn_ln_gamma"].reshape(1, D)),
            "lb_row": np.ascontiguousarray(inp["L0_ffn_ln_beta"].reshape(1, D)),
            "fc1q": inp["L0_ffn_fc1_q"], "fc1s": inp["L0_ffn_fc1_scales"],
            "b1c": b1c, "fc2q": inp["L0_ffn_fc2_q"],
            "fc2s": inp["L0_ffn_fc2_scales"], "b2c": b2c} for c in cores]
    r3 = run_bass_kernel_spmd(nc3, im3, core_ids=cores)
    x2f = fm_join([np.asarray(r3.results[c]["x2out"]) for c in cores])
    xs2f = fm_join([np.asarray(r3.results[c]["xs2out"]) for c in cores])

    xs2in = np.stack([
        np.ascontiguousarray(xs2f[:, b * T:(b + 1) * T].reshape(NCH, 128, T))
        for b in range(2)])
    masks4 = np.zeros((4, 128, 512), np.float32)
    for dd in range(4):
        for p in range(128):
            c0 = 128 * dd + p
            if c0 < 512:
                masks4[dd, p, c0:] = 1.0
    sel2 = np.zeros((128, 2), np.float32)
    sel2[0:64, 0] = 1.0
    sel2[64:128, 1] = 1.0
    sel2t = np.ascontiguousarray(sel2.T)
    iden = np.eye(128, dtype=np.float32)
    w0blk = np.zeros((128, 128), np.float32)
    w0blk[0:64, 0:64] = inp["L1_attn_W0"]
    w0blk[64:128, 64:128] = inp["L1_attn_W0"]
    im4 = []
    for c in cores:
        hA, hB = 2 * c, 2 * c + 1
        cols = []
        for oi in range(3):
            for h in (hA, hB):
                cols.append(inp["L1_attn_Wproj"][:, oi * D + h * 64:
                                                 oi * D + (h + 1) * 64])
        wp_s = np.ascontiguousarray(np.concatenate(cols, axis=1))
        bp = np.zeros((128, 3), np.float32)
        lngb = np.zeros((128, 4), np.float32)
        for hh, h in enumerate((hA, hB)):
            lo = hh * 64
            for oi in range(3):
                bp[lo:lo + 64, oi] = inp["L1_attn_bproj"][oi * D + h * 64:
                                                          oi * D + (h + 1) * 64]
            lngb[lo:lo + 64, 0] = inp["L1_attn_ln_g"]
            lngb[lo:lo + 64, 1] = inp["L1_attn_ln_b"]
            lngb[lo:lo + 64, 2] = inp["L1_attn_gn_g"][h * 64:(h + 1) * 64]
            lngb[lo:lo + 64, 3] = inp["L1_attn_gn_b"][h * 64:(h + 1) * 64]
        im4.append({"xs2f": xs2in, "wp": wp_s, "bp": bp, "w0b": w0blk,
                    "lngb": lngb, "masks": masks4, "sel2": sel2,
                    "sel2t": sel2t, "iden": iden})
    nc4 = _get_nc("l4", build_launch4)
    r4 = run_bass_kernel_spmd(nc4, im4, core_ids=cores)
    ctxf = np.zeros((D, NTOK), BF)
    for c in cores:
        co = np.asarray(r4.results[c]["ctxn"])
        for b in range(2):
            ctxf[c * 128:(c + 1) * 128, b * T:(b + 1) * T] = co[b]

    im5 = []
    for c in cores:
        t0 = c * TL
        x2e = np.zeros((D, TE), np.float32)
        cte = np.zeros((D, TE), BF)
        x2e[:, 1:] = x2f[:, t0:t0 + TL]
        cte[:, 1:] = ctxf[:, t0:t0 + TL]
        if c not in (0, NCORE // 2):
            x2e[:, 0] = x2f[:, t0 - 1]
            cte[:, 0] = ctxf[:, t0 - 1]
        bm = (np.zeros((128, 1), np.float32) if c in (0, NCORE // 2)
              else np.ones((128, 1), np.float32))
        im5.append({
            "x2e": np.ascontiguousarray(x2e.reshape(NCH, 128, TE)),
            "ctxe": np.ascontiguousarray(cte.reshape(NCH, 128, TE)),
            "wout": inp["L1_attn_Wout"],
            "wrt": np.ascontiguousarray(inp["L1_ffn_W_r"].T),
            "wkt": np.ascontiguousarray(inp["L1_ffn_W_k"].T),
            "wvt": np.ascontiguousarray(inp["L1_ffn_W_v"].T),
            "bo_c": np.ascontiguousarray(inp["L1_attn_bout"].reshape(8, 128).T),
            "mur_c": np.ascontiguousarray(inp["L1_ffn_mu_r"].reshape(8, 128).T),
            "muk_c": np.ascontiguousarray(inp["L1_ffn_mu_k"].reshape(8, 128).T),
            "bmask": bm,
            "idenf": np.eye(128, dtype=np.float32),
        })
    nc5 = _get_nc("l5", build_launch5)
    r5 = run_bass_kernel_spmd(nc5, im5, core_ids=cores)
    out = np.concatenate([np.asarray(r5.results[c]["out"]) for c in cores],
                         axis=0)
    return out.reshape(B, T, D)



def build_launch2():
    nc = bass.Bass("TRN2", target_bir_lowering=False, debug=False)
    q2in = nc.dram_tensor("q2in", [2, 128, T], BF16, kind="ExternalInput").ap()
    k2in = nc.dram_tensor("k2in", [2, 128, T], BF16, kind="ExternalInput").ap()
    v1in = nc.dram_tensor("v1in", [2, 2, 128, 64, 65], BF16,
                          kind="ExternalInput").ap()
    yout = nc.dram_tensor("yout", [2, 2, 64, T], BF16, kind="ExternalOutput").ap()
    NKT = T // 128
    NTT = T // 512

    with ExitStack() as es:
        sch = Sched(nc, es)
        sch.sem("ld", "lg", "mm", "ve", "ac", "st")
        Q2 = _mk(nc, es, "q2", [128, T], BF16)
        K2 = _mk(nc, es, "k2", [128, T], BF16)
        V1 = [_mk(nc, es, f"v1{h}", [128, 64 * 65], BF16) for h in range(2)]
        SQT = _mk(nc, es, "sqt", [128, T], BF16)
        PA = [_mk(nc, es, f"pa{s}", [128, 1024], BF16) for s in range(8)]
        NQ = _mk(nc, es, "nq", [1, T], F32)
        SC = _mk(nc, es, "sc", [1, 16], F32)
        NB = _mk(nc, es, "nb", [128, 2], F32)
        OE = _mk(nc, es, "oe", [128, 512], F32)
        RE = _mk(nc, es, "re", [128, 512], F32)
        YH = [_mk(nc, es, f"yh{h}", [64, T], BF16) for h in range(2)]
        ONESB = _mk(nc, es, "onesb", [128, 1], BF16)
        ONESF = _mk(nc, es, "onesf", [128, 128], F32)
        PS = [_ps(nc, es, f"ps{s}", [128, 1024]) for s in range(3)]
        PO_A = _ps(nc, es, "poa", [128, 512])
        PO_B = _ps(nc, es, "pob", [128, 512])
        PR = PS[0]

        sch.op("vector", lambda e: e.memset(ONESB[:, :], 1.0))
        sch.op("vector", lambda e: e.memset(ONESF[:, :], 1.0), inc=(("ve", 1),))

        for task in range(2):
            w_prev = sch.w("mm", "ac", "st")
            sch.dma("sync", Q2[:, :], q2in[task, :, :], wait=w_prev)
            sch.dma("sync", K2[:, :], k2in[task, :, :])
            for h in range(2):
                sch.dma("sync", V1[h][:, :],
                        v1in[task, h, :, :, :].rearrange("p a b -> p (a b)"))
            ld0 = sch.cnt["ld"]

            for which, SRC in enumerate((Q2, K2)):
                sch.op("scalar", lambda e, SRC=SRC: e.activation(
                    SQT[:, :], SRC[:, :], AF.Square),
                    wait=(("ld", ld0),) + sch.w("mm"), inc=(("ac", 1),))
                for h in range(2):
                    lo = 64 * h
                    nsn = {}
                    for tt in range(NTT):
                        bank = PS[tt % 2][:, 0:512]
                        w = sch.w("ac", "ve") if tt == 0 else (
                            ("ve", nsn[tt - 2]),) if tt >= 2 else ()
                        sch.op("tensor",
                               lambda e, lo=lo, tt=tt, bank=bank: e.matmul(
                                   bank[0:1, :], ONESB[lo:lo + 64, 0:1],
                                   SQT[lo:lo + 64, tt * 512:(tt + 1) * 512],
                                   start=True, stop=True),
                               wait=w, inc=(("mm", 1),))
                        msn = sch.cnt["mm"]
                        sch.op("vector", lambda e, tt=tt, bank=bank:
                               e.tensor_copy(NQ[0:1, tt * 512:(tt + 1) * 512],
                                             bank[0:1, :]),
                               wait=(("mm", msn),), inc=(("ve", 1),))
                        nsn[tt] = sch.cnt["ve"]
                    j = which * 4 + h
                    sch.op("vector", lambda e, j=j: e.reduce_max(
                        SC[0:1, j:j + 1], NQ[0:1, 0:T], axis=AX.X),
                        inc=(("ve", 1),))
            sch.op("scalar", lambda e: e.activation(SC[0:1, 8:14], SC[0:1, 0:6],
                                                    AF.Sqrt),
                   wait=sch.w("ve"), inc=(("ac", 1),))
            for h in range(2):
                sch.op("vector", lambda e, h=h: e.tensor_mul(
                    SC[0:1, 12 + h:13 + h], SC[0:1, 8 + h:9 + h],
                    SC[0:1, 12 + h:13 + h]), wait=sch.w("ac"),
                    inc=(("ve", 1),))
            sch.op("vector", lambda e: e.tensor_max(
                SC[0:1, 14:15], SC[0:1, 12:13], SC[0:1, 13:14]))
            sch.op("vector", lambda e: e.tensor_scalar_mul(
                SC[0:1, 15:16], SC[0:1, 14:15], -0.125), inc=(("ve", 1),))
            sch.op("tensor", lambda e: e.matmul(
                PR[:, 0:1], ONESF[0:1, 0:128], SC[0:1, 15:16],
                start=True, stop=True),
                wait=sch.w("ve"), inc=(("mm", 1),))
            sch.op("vector", lambda e: e.tensor_copy(
                NB[:, 0:1], PR[:, 0:1]),
                wait=sch.w("mm"), inc=(("ve", 1),))
            bnd = sch.w("ve")

            snap = {}
            KG = [list(range(g, min(g + 2, NKT))) for g in range(0, NKT, 2)]
            macros = [(h, kcs) for kcs in KG for h in (0, 1)]
            NM = len(macros)
            for ti in range(NTT):
                t0 = ti * 512
                for step in range(NM + 3):
                    if NM - 1 + 2 >= step >= 2:
                        mi = ti * NM + step - 2
                        sl = mi % 3
                        pl = mi % 8
                        nk = len(macros[step - 2][1]) * 512
                        v = snap[("S", mi)]
                        if mi >= 8:
                            v = max(v, snap[("PV", mi - 8)])
                        sch.op("scalar", lambda e, sl=sl, pl=pl, nk=nk:
                               e.activation(PA[pl][:, 0:nk], PS[sl][:, 0:nk],
                                            AF.Exp, bias=NB[:, 0:1],
                                            scale=0.125),
                               wait=(("mm", v),), inc=(("ac", 1),))
                        snap[("exp", mi)] = sch.cnt["ac"]
                    if step <= NM - 1:
                        mi = ti * NM + step
                        h, kcs = macros[step]
                        lo = 64 * h
                        sl = mi % 3
                        w = []
                        if mi >= 3:
                            w.append(("ac", snap[("exp", mi - 3)]))
                        if step == 0 and ti > 0:
                            w.append(("ve", snap[("fin", ti - 1)]))
                        if ti == 0 and step == 0:
                            w = list(bnd)
                        for q, kc in enumerate(kcs):
                            sch.op("tensor",
                                   lambda e, sl=sl, kc=kc, t0=t0, q=q, lo=lo:
                                   e.matmul(PS[sl][:, q * 512:(q + 1) * 512],
                                            K2[lo:lo + 64,
                                               kc * 128:(kc + 1) * 128],
                                            Q2[lo:lo + 64, t0:t0 + 512],
                                            start=True, stop=True,
                                            tile_position=(lo, 0)),
                                   wait=(w if q == 0 else ()),
                                   inc=((("mm", 1),) if q == len(kcs) - 1
                                        else ()))
                        snap[("S", mi)] = sch.cnt["mm"]
                    if step >= 3:
                        mi = ti * NM + step - 3
                        h, kcs = macros[step - 3]
                        pl = mi % 8
                        PO = PO_A if h == 0 else PO_B
                        for q, kc in enumerate(kcs):
                            sch.op("tensor", lambda e, pl=pl, kc=kc, q=q, PO=PO:
                                   e.matmul(PO[0:65, :],
                                            V1[0 if PO is PO_A else 1]
                                            [:, kc * 65:(kc + 1) * 65],
                                            PA[pl][:, q * 512:(q + 1) * 512],
                                            start=(kc == 0),
                                            stop=(kc == NKT - 1)),
                                   wait=((("ac", snap[("exp", mi)]),)
                                         if q == 0 else ()),
                                   inc=((("mm", 1),) if q == len(kcs) - 1
                                        else ()))
                        snap[("PV", mi)] = sch.cnt["mm"]
                for h, PO, Y in ((0, PO_A, YH[0]), (1, PO_B, YH[1])):
                    sch.op("vector", lambda e, PO=PO: e.tensor_copy(
                        OE[0:65, :], PO[0:65, :]), wait=sch.w("mm"))
                    sch.op("vector", lambda e: e.reciprocal(
                        RE[64:65, 0:512], OE[64:65, 0:512]), inc=(("ve", 1),))
                    sch.op("tensor", lambda e, h=h: e.matmul(
                        PR[0:64, h * 512:h * 512 + 512],
                        ONESF[64:65, 0:64], RE[64:65, 0:512],
                        start=True, stop=True),
                        wait=sch.w("ve"), inc=(("mm", 1),))
                    sch.op("vector", lambda e, h=h, Y=Y, t0=t0: e.tensor_mul(
                        Y[0:64, t0:t0 + 512], OE[0:64, :],
                        PR[0:64, h * 512:h * 512 + 512]),
                        wait=sch.w("mm"), inc=(("ve", 1),))
                snap[("fin", ti)] = sch.cnt["ve"]
            for h in range(2):
                sch.dma("sync", yout[task, h, :, :], YH[h][0:64, :], sem="st",
                        wait=sch.w("ve"))

        sch.op("gpsimd", lambda e: e.nop(), wait=(("st", sch.cnt["st"]),))
        sch.build_block()
    return nc



def build_launch3():
    nc = bass.Bass("TRN2", target_bir_lowering=False, debug=False)
    xin = nc.dram_tensor("xin", [NCH, 128, TL], F32, kind="ExternalInput").ap()
    yin = nc.dram_tensor("yin", [NCH, 128, TL], BF16, kind="ExternalInput").ap()
    lg_row = nc.dram_tensor("lg_row", [1, D], F32, kind="ExternalInput").ap()
    lb_row = nc.dram_tensor("lb_row", [1, D], F32, kind="ExternalInput").ap()
    fc1q = nc.dram_tensor("fc1q", [D, F_FFN], I32, kind="ExternalInput").ap()
    fc1s = nc.dram_tensor("fc1s", [8, F_FFN], F32, kind="ExternalInput").ap()
    b1c = nc.dram_tensor("b1c", [128, 32], F32, kind="ExternalInput").ap()
    fc2q = nc.dram_tensor("fc2q", [F_FFN, D], I32, kind="ExternalInput").ap()
    fc2s = nc.dram_tensor("fc2s", [32, D], F32, kind="ExternalInput").ap()
    b2c = nc.dram_tensor("b2c", [128, 8], F32, kind="ExternalInput").ap()
    x2out = nc.dram_tensor("x2out", [NCH, 128, TL], F32, kind="ExternalOutput").ap()
    xs2out = nc.dram_tensor("xs2out", [NCH, 128, TL], BF16,
                            kind="ExternalOutput").ap()
    y1dram = nc.dram_tensor("y1dram", [32, 128, TL], BF16, kind="Internal").ap()

    with ExitStack() as es:
        sch = Sched(nc, es)
        sch.sem("ld", "lg", "mm", "ve", "ac", "st")
        X1 = [_mk(nc, es, f"x1{c}", [128, TL], F32) for c in range(NCH)]
        LG = _mk(nc, es, "lg", [1, D], F32)
        LB = _mk(nc, es, "lb", [1, D], F32)
        B1 = _mk(nc, es, "b1", [128, 32], F32)
        B2 = _mk(nc, es, "b2", [128, 8], F32)
        ONESB = _mk(nc, es, "onesb", [128, 1], BF16)
        ONESR = _mk(nc, es, "onesr", [1, 512], F32)
        EPSC = _mk(nc, es, "epsc", [128, 1], F32)
        MU = _mk(nc, es, "mu", [1, TL], F32)
        Q2 = _mk(nc, es, "q2", [1, TL], F32)
        RR = _mk(nc, es, "rr", [1, TL], F32)
        NMR = _mk(nc, es, "nmr", [1, TL], F32)
        R1 = _mk(nc, es, "r1", [1, TL], F32)
        NM1 = _mk(nc, es, "nm1", [1, TL], F32)
        RS0 = _mk(nc, es, "rs0", [1, TL], F32)
        RS1 = _mk(nc, es, "rs1", [1, TL], F32)
        SQS = _mk(nc, es, "sqs", [128, 512], BF16)
        XBS = _mk(nc, es, "xbs", [128, 512], BF16)
        TMP = _mk(nc, es, "tmp", [128, 512], F32)
        TMP2 = _mk(nc, es, "tmp2", [128, 512], F32)
        P0 = _ps(nc, es, "p0", [128, 512])
        P1 = _ps(nc, es, "p1", [128, 512])
        P2 = _ps(nc, es, "p2", [128, 512])

        sch.op("vector", lambda e: e.memset(ONESB[:, :], 1.0))
        sch.op("vector", lambda e: e.memset(EPSC[:, :], EPS))
        sch.op("vector", lambda e: e.memset(ONESR[:, :], 1.0), inc=(("ve", 1),))
        for c in range(NCH):
            sch.dma("sync", X1[c][:, :], xin[c, :, :])
        sch.dma("sync", LG[0:1, :], lg_row[0:1, :])
        sch.dma("sync", LB[0:1, :], lb_row[0:1, :])
        sch.dma("sync", B1[:, :], b1c[:, :])
        sch.dma("sync", B2[:, :], b2c[:, :])
        ld0 = sch.cnt["ld"]

        with ExitStack() as es2:
            Y = [_mk(nc, es2, f"y{c}", [128, TL], BF16) for c in range(NCH)]
            for c in range(NCH):
                sch.dma("gpsimd", Y[c][:, :], yin[c, :, :], sem="lg")
            ldy = sch.cnt["lg"]
            emit_stats_1024(sch, Y, ONESB, P0, MU, Q2, TL, SQS,
                            first_wait=(("lg", ldy), ("ld", ld0)))
            emit_rows_rstd(sch, MU, Q2, RR, RS0, RS1, TL, double_norm=False,
                           epsc=EPSC)
            emit_neg_murr(sch, MU, RR, NMR, TL)
            emit_ln_apply(sch, Y, X1, RR, NMR, TL, P1, P2, TMP, ONESR,
                          add_chunks=X1)

        emit_stats_1024(sch, X1, ONESB, P0, MU, Q2, TL, SQS, xbs=XBS)
        emit_rows_rstd(sch, MU, Q2, RR, RS0, RS1, TL, double_norm=True,
                       epsc=EPSC, r1_out=R1)
        emit_neg_murr(sch, MU, RR, NMR, TL)
        sch.op("vector", lambda e: e.tensor_mul(NM1[0:1, :], MU[0:1, :],
                                                R1[0:1, :]))
        sch.op("vector", lambda e: e.tensor_scalar_mul(NM1[0:1, :], NM1[0:1, :],
                                                       -1.0), inc=(("ve", 1),))

        with ExitStack() as es3:
            HB = [_mk(nc, es3, f"h{c}", [128, TL], BF16) for c in range(NCH)]
            W1 = _mk(nc, es3, "w1", [128, NCH * 1024], BF16)
            QB = _mk(nc, es3, "qb", [128, 1024], BF16)
            SCLF = _mk(nc, es3, "sclf", [128, 1024], F32)
            Y1T = _mk(nc, es3, "y1t", [128, TL], BF16)
            emit_ln_apply(sch, X1, HB, RR, NMR, TL, P1, P2, TMP, ONESR,
                          g_row=LG, b_row=LB)
            h_done = sch.cnt["ve"]
            for fh in range(4):
                f0 = fh * 1024
                for kc in range(NCH):
                    sch.dma("gpsimd", QB[:, :],
                            fc1q[kc * 128:(kc + 1) * 128, f0:f0 + 1024],
                            sem="lg", wait=sch.w("ve"))
                    sch.dma("sync", SCLF[:, :],
                            bass.AP(fc1s.tensor, kc * F_FFN + f0,
                                    [[0, 128], [1, 1024]]), wait=sch.w("ve"))
                    sch.op("vector", lambda e, kc=kc: e.scalar_tensor_tensor(
                        W1[:, kc * 1024:(kc + 1) * 1024], QB[:, :], -128.0,
                        SCLF[:, :], ALU.add, ALU.mult),
                        wait=(("lg", sch.cnt["lg"]), ("ld", sch.cnt["ld"]))
                        + (sch.w("mm") if kc == 0 else ()),
                        inc=(("ve", 1),))
                wdq = sch.cnt["ve"]
                for fc in range(8):
                    gfc = fh * 8 + fc
                    for ti in range(TL // 512):
                        t0 = ti * 512
                        for kc in range(NCH):
                            first = fc == 0 and ti == 0 and kc == 0
                            sch.op("tensor", lambda e, fc=fc, t0=t0, kc=kc:
                                   e.matmul(P0[:, :],
                                            W1[:, kc * 1024 + fc * 128:
                                               kc * 1024 + (fc + 1) * 128],
                                            HB[kc][:, t0:t0 + 512],
                                            start=(kc == 0), stop=(kc == NCH - 1)),
                                   wait=((("ve", max(wdq, h_done)),) if first
                                         else sch.w("ac") if kc == 0 else ()),
                                   inc=((("mm", 1),) if kc == NCH - 1 else ()))
                        sch.op("scalar", lambda e, gfc=gfc, t0=t0: e.activation(
                            Y1T[:, t0:t0 + 512], P0[:, :], AF.Silu,
                            bias=B1[:, gfc:gfc + 1]),
                            wait=sch.w("mm"), inc=(("ac", 1),))
                    stp = sch.cnt["st"]
                    sch.dma("sync", y1dram[gfc, :, :], Y1T[:, :], sem="st",
                            wait=sch.w("ac"))
                    sch.op("scalar", lambda e: e.nop(),
                           wait=(("st", stp + 16),))

        with ExitStack() as es4:
            UB = [_mk(nc, es4, f"u{c}", [128, TL], BF16) for c in range(NCH)]
            W2 = _mk(nc, es4, "w2", [128, 32 * 256], BF16)
            QB2 = _mk(nc, es4, "qb2", [128, 256], BF16)
            SCL2 = _mk(nc, es4, "scl2", [128, 256], F32)
            Y1R = _mk(nc, es4, "y1r", [128, 8 * 512], BF16)
            PZ = [_ps(nc, es4, f"pz{i}", [128, 512]) for i in range(2)]
            for dh in range(4):
                d0 = dh * 256
                for kc in range(32):
                    sch.dma("gpsimd", QB2[:, :],
                            fc2q[kc * 128:(kc + 1) * 128, d0:d0 + 256],
                            sem="lg", wait=sch.w("ve"))
                    sch.dma("sync", SCL2[:, :],
                            bass.AP(fc2s.tensor, kc * D + d0,
                                    [[0, 128], [1, 256]]), wait=sch.w("ve"))
                    sch.op("vector", lambda e, kc=kc: e.scalar_tensor_tensor(
                        W2[:, kc * 256:(kc + 1) * 256], QB2[:, :], -128.0,
                        SCL2[:, :], ALU.add, ALU.mult),
                        wait=(("lg", sch.cnt["lg"]), ("ld", sch.cnt["ld"]))
                        + (sch.w("mm") if kc == 0 else ()),
                        inc=(("ve", 1),))
                wdq2 = sch.cnt["ve"]
                for ti in range(TL // 512):
                    t0 = ti * 512
                    for sub in range(4):
                        for i in range(8):
                            kc = sub * 8 + i
                            sch.dma("sync", Y1R[:, i * 512:(i + 1) * 512],
                                    y1dram[kc, :, t0:t0 + 512], sem="ld",
                                    wait=(sch.w("mm") if i == 0 else ()))
                        ldyr = sch.cnt["ld"]
                        for dc in range(2):
                            for i in range(8):
                                kc = sub * 8 + i
                                sch.op("tensor", lambda e, dc=dc, i=i, sub=sub:
                                       e.matmul(PZ[dc][:, :],
                                                W2[:, (sub * 8 + i) * 256 + dc * 128:
                                                   (sub * 8 + i) * 256 + (dc + 1) * 128],
                                                Y1R[:, i * 512:(i + 1) * 512],
                                                start=(sub == 0 and i == 0),
                                                stop=(sub == 3 and i == 7)),
                                       wait=(((("ld", ldyr), ("ve", wdq2))
                                              + (sch.w("ac") if sub == 0 else ()))
                                             if i == 0 else ()),
                                       inc=((("mm", 1),) if i == 7 else ()))
                    for dc in range(2):
                        gdc = dh * 2 + dc
                        m = sch.w("mm", "ve")
                        sch.op("scalar", lambda e, dc=dc, gdc=gdc: e.activation(
                            TMP[:, :], PZ[dc][:, :], AF.Identity,
                            bias=B2[:, gdc:gdc + 1]),
                            wait=m, inc=(("ac", 1),))
                        sch.op("tensor", lambda e, t0=t0: e.matmul(
                            P1[:, :], ONESR[0:1, 0:128], R1[0:1, t0:t0 + 512],
                            start=True, stop=True), wait=m)
                        sch.op("tensor", lambda e, t0=t0: e.matmul(
                            P2[:, :], ONESR[0:1, 0:128], NM1[0:1, t0:t0 + 512],
                            start=True, stop=True), inc=(("mm", 1),))
                        sch.op("vector", lambda e, gdc=gdc, t0=t0: e.tensor_mul(
                            TMP2[:, :], X1[gdc][:, t0:t0 + 512], P1[:, :]),
                            wait=sch.w("mm", "ac"))
                        sch.op("vector", lambda e: e.tensor_add(
                            TMP2[:, :], TMP2[:, :], P2[:, :]))
                        sch.op("vector", lambda e, gdc=gdc, t0=t0:
                               e.scalar_tensor_tensor(
                                   UB[gdc][:, t0:t0 + 512], TMP[:, :], 0.5,
                                   TMP2[:, :], ALU.mult, ALU.add),
                               inc=(("ve", 1),))

            emit_stats_1024(sch, UB, ONESB, P0, MU, Q2, TL, SQS)
            emit_rows_rstd(sch, MU, Q2, RR, RS0, RS1, TL, double_norm=False,
                           epsc=EPSC)
            emit_neg_murr(sch, MU, RR, NMR, TL)
            emit_ln_apply(sch, UB, X1, RR, NMR, TL, P1, P2, TMP, ONESR,
                          add_chunks=X1)
        for c in range(NCH):
            sch.dma("sync", x2out[c, :, :], X1[c][:, :], sem="st",
                    wait=sch.w("ve"))
        with ExitStack() as es5:
            XS2 = [_mk(nc, es5, f"xs{c}", [128, TL], BF16) for c in range(NCH)]
            emit_stats_1024(sch, X1, ONESB, P0, MU, Q2, TL, SQS, xbs=XBS)
            emit_rows_rstd(sch, MU, Q2, RR, RS0, RS1, TL, double_norm=False,
                           epsc=EPSC)
            emit_neg_murr(sch, MU, RR, NMR, TL)
            emit_ln_apply(sch, X1, XS2, RR, NMR, TL, P1, P2, TMP, ONESR)
            for c in range(NCH):
                sch.dma("sync", xs2out[c, :, :], XS2[c][:, :], sem="st",
                        wait=(sch.w("ve") if c == 0 else ()))

        sch.op("gpsimd", lambda e: e.nop(), wait=(("st", sch.cnt["st"]),))
        sch.build_block()
    return nc



def build_launch4(dbg=False):
    nc = bass.Bass("TRN2", target_bir_lowering=False, debug=False)
    xs2f = nc.dram_tensor("xs2f", [2, NCH, 128, T], BF16, kind="ExternalInput").ap()
    wp = nc.dram_tensor("wp", [D, 384], F32, kind="ExternalInput").ap()
    bp = nc.dram_tensor("bp", [128, 3], F32, kind="ExternalInput").ap()
    w0b = nc.dram_tensor("w0b", [128, 128], F32, kind="ExternalInput").ap()
    lngb = nc.dram_tensor("lngb", [128, 4], F32, kind="ExternalInput").ap()
    masks = nc.dram_tensor("masks", [4, 128, 512], F32, kind="ExternalInput").ap()
    sel2 = nc.dram_tensor("sel2", [128, 2], F32, kind="ExternalInput").ap()
    sel2t = nc.dram_tensor("sel2t", [2, 128], F32, kind="ExternalInput").ap()
    iden = nc.dram_tensor("iden", [128, 128], F32, kind="ExternalInput").ap()
    ctxout = nc.dram_tensor("ctxn", [2, 128, T], BF16, kind="ExternalOutput").ap()
    if dbg or T4DBG2:
        dbg_qkv = nc.dram_tensor("dbg_qkv", [3, 128, T], BF16,
                                 kind="ExternalOutput").ap()
        dbg_dz1 = nc.dram_tensor("dbg_dz1", [128, T], BF16,
                                 kind="ExternalOutput").ap()
        dbg_ctx = nc.dram_tensor("dbg_ctx", [128, T], BF16,
                                 kind="ExternalOutput").ap()
        dbg_z2 = nc.dram_tensor("dbg_z2", [128, 512], BF16,
                                kind="ExternalOutput").ap()
        dbg_ctx1 = nc.dram_tensor("dbg_ctx1", [128, T], BF16,
                                  kind="ExternalOutput").ap()
        dbg_gsc = nc.dram_tensor("dbg_gsc", [2, 128, 8], F32,
                                 kind="ExternalOutput").ap()
        dbg_pbs = nc.dram_tensor("dbg_pbs", [2, 128, 2], F32,
                                 kind="ExternalOutput").ap()
        dbg_lng = nc.dram_tensor("dbg_lng", [128, 4], F32,
                                 kind="ExternalOutput").ap()
    NTT = T // 512

    with ExitStack() as es:
        sch = Sched(nc, es)
        sch.sem("ld", "lg", "mm", "ve", "ac", "st")
        Q2 = _mk(nc, es, "q2", [128, T], BF16)
        K2 = _mk(nc, es, "k2", [128, T], BF16)
        V2 = _mk(nc, es, "v2", [128, T], BF16)
        DZ1 = _mk(nc, es, "dz1", [128, T], BF16)
        D1TA = _mk(nc, es, "d1ta", [128, (T // 128) * 128], BF16)
        D1TB = _mk(nc, es, "d1tb", [128, (T // 128) * 128], BF16)
        DSTA = _mk(nc, es, "dsta", [128, (T // 128) * 128], BF16)
        DSTB = _mk(nc, es, "dstb", [128, (T // 128) * 128], BF16)
        CTX = _mk(nc, es, "ctx", [128, T], BF16)
        XT = _mk(nc, es, "xt", [128, 8 * 512], BF16)
        WPB = _mk(nc, es, "wpb", [128, 8 * 384], BF16)
        W0B = _mk(nc, es, "w0bs", [128, 128], BF16)
        BP = _mk(nc, es, "bps", [128, 3], F32)
        LNG = _mk(nc, es, "lng", [128, 4], F32)
        MASK = _mk(nc, es, "masks_sb", [128, 4 * 512], F32)
        SEL2 = _mk(nc, es, "sel2s", [128, 2], BF16)
        S2TF = _mk(nc, es, "s2tf", [2, 128], F32)
        S2TB = _mk(nc, es, "s2tb", [2, 128], BF16)
        IDEN = _mk(nc, es, "idens", [128, 128], BF16)
        EPSC = _mk(nc, es, "epsc", [128, 1], F32)
        KKE = _mk(nc, es, "kke", [128, 4 * 512], BF16)
        ZT = _mk(nc, es, "zt", [128, 512], BF16)
        Z2T = _mk(nc, es, "z2t", [128, 512], BF16)
        ZHT = _mk(nc, es, "zht", [128, 512], BF16)
        DZHT = _mk(nc, es, "dzht", [128, 512], BF16)
        PRB = _mk(nc, es, "prb", [128, 512], BF16)
        SQB = _mk(nc, es, "sqb", [128, 512], BF16)
        DZNT = _mk(nc, es, "dznt", [128, 512], BF16)
        TS1 = _mk(nc, es, "ts1", [128, 512], F32)
        TS2 = _mk(nc, es, "ts2", [128, 512], F32)
        ROWA = _mk(nc, es, "rowa", [2, 512], F32)
        ROWB = _mk(nc, es, "rowb", [2, 512], F32)
        ROWC = _mk(nc, es, "rowc", [2, 512], F32)
        ROWD = _mk(nc, es, "rowd", [2, 512], F32)
        GSC = _mk(nc, es, "gsc", [128, 8], F32)
        PB_S = _ps(nc, es, "pbs", [128, 512])
        PB_A = _ps(nc, es, "pba", [128, 512])
        PB_B = _ps(nc, es, "pbb", [128, 512])
        PB_C = _ps(nc, es, "pbc", [128, 512])
        PKA = _ps(nc, es, "pka", [128, 512])
        PKB = _ps(nc, es, "pkb", [128, 512])
        PU = _ps(nc, es, "pu", [128, 512])
        PT = _ps(nc, es, "pt", [128, 128], BF16)

        sch.dma("gpsimd", WPB[:, :].rearrange("p (c n) -> p c n", c=NCH),
                wp[:, :].rearrange("(c p) n -> p c n", p=128), sem="lg")
        sch.dma("gpsimd", W0B[:, :], w0b[:, :], sem="lg")
        sch.dma("gpsimd", SEL2[:, :], sel2[:, :], sem="lg")
        sch.dma("gpsimd", S2TB[0:2, :], sel2t[0:2, :], sem="lg")
        sch.dma("gpsimd", IDEN[:, :], iden[:, :], sem="lg")
        sch.dma("sync", BP[:, :], bp[:, :])
        sch.dma("sync", LNG[:, :], lngb[:, :])
        sch.dma("sync", S2TF[0:2, :], sel2t[0:2, :])
        sch.dma("sync", MASK[:, :].rearrange("p (a b) -> p a b", a=4),
                masks[:, :, :].rearrange("a p b -> p a b"))
        ldc = sch.w("ld", "lg")
        sch.op("vector", lambda e: e.memset(EPSC[:, :], EPS))
        for Tn in (D1TA, D1TB, DSTA, DSTB):
            sch.op("vector", lambda e, Tn=Tn: e.memset(Tn[:, :], 0.0))
        sch.op("vector", lambda e: e.nop(), inc=(("ve", 1),))
        init_w = sch.w("ve")

        g_col = LNG[:, 0:1]
        b_col = LNG[:, 1:2]

        def bcast2(prow, psum, dt=F32, wait=()):
            sel = S2TF if dt == F32 else S2TB
            sch.op("tensor", lambda e, prow=prow, psum=psum, sel=sel: e.matmul(
                psum[:, 0:prow.shape[1]], sel[0:2, :], prow,
                start=True, stop=True), wait=wait, inc=(("mm", 1),))

        def stats2(src_ap, psum, wait=()):
            sch.op("tensor", lambda e, src_ap=src_ap, psum=psum: e.matmul(
                psum[0:2, 0:src_ap.shape[1]], SEL2[:, :], src_ap,
                start=True, stop=True), wait=wait, inc=(("mm", 1),))

        def ln_rows_tile(zsrc, w=512):
            stats2(zsrc, PB_S, wait=sch.w("ve", "ac"))
            sch.op("vector", lambda e, w=w: e.tensor_scalar_mul(
                ROWA[0:2, 0:w], PB_S[0:2, 0:w], 1.0 / 64),
                wait=sch.w("mm"), inc=(("ve", 1),))
            sch.op("scalar", lambda e, zsrc=zsrc: e.activation(
                SQB[:, 0:w], zsrc, AF.Square),
                wait=sch.w("ve"), inc=(("ac", 1),))
            stats2(SQB[:, 0:w], PB_S, wait=sch.w("ac", "ve"))
            sch.op("vector", lambda e, w=w: e.tensor_scalar_mul(
                ROWB[0:2, 0:w], PB_S[0:2, 0:w], 1.0 / 64),
                wait=sch.w("mm"))
            sch.op("vector", lambda e, w=w: e.tensor_mul(
                ROWC[0:2, 0:w], ROWA[0:2, 0:w], ROWA[0:2, 0:w]))
            sch.op("vector", lambda e, w=w: e.tensor_sub(
                ROWB[0:2, 0:w], ROWB[0:2, 0:w], ROWC[0:2, 0:w]),
                inc=(("ve", 1),))
            sch.op("scalar", lambda e, w=w: e.activation(
                ROWB[0:2, 0:w], ROWB[0:2, 0:w], AF.Sqrt, bias=EPSC[0:2, 0:1]),
                wait=sch.w("ve"), inc=(("ac", 1),))
            sch.op("vector", lambda e, w=w: e.reciprocal(
                ROWC[0:2, 0:w], ROWB[0:2, 0:w]), wait=sch.w("ac"))
            sch.op("vector", lambda e, w=w: e.scalar_tensor_tensor(
                ROWD[0:2, 0:w], ROWA[0:2, 0:w], -1.0, ROWC[0:2, 0:w],
                ALU.mult, ALU.mult), inc=(("ve", 1),))

        def dz_chain(zsrc, dz_dst_tile, dzn_tile, t0):
            ln_rows_tile(zsrc)
            bcast2(ROWC[0:2, 0:512], PB_C, wait=sch.w("ve"))
            bcast2(ROWD[0:2, 0:512], PB_A)
            sch.op("vector", lambda e, zsrc=zsrc: e.tensor_mul(
                TS1[:, :], zsrc, PB_C[:, :]), wait=sch.w("mm"))
            sch.op("vector", lambda e: e.tensor_add(
                ZHT[:, :], TS1[:, :], PB_A[:, :]), inc=(("ve", 1),))
            sch.op("vector", lambda e: e.tensor_scalar(
                TS1[:, :], ZHT[:, :], g_col, b_col, ALU.mult, ALU.add))
            sch.op("vector", lambda e, t0=t0: e.tensor_sub(
                TS1[:, :], TS1[:, :], V2[:, t0:t0 + 512]))
            sch.op("vector", lambda e: e.tensor_scalar(
                DZHT[:, :], TS1[:, :], g_col, None, ALU.mult),
                inc=(("ve", 1),))
            stats2(DZHT[:, :], PB_S, wait=sch.w("ve"))
            sch.op("vector", lambda e: e.tensor_scalar_mul(
                ROWA[0:2, :], PB_S[0:2, :], 1.0 / 64), wait=sch.w("mm"))
            sch.op("vector", lambda e: e.tensor_mul(
                PRB[:, :], DZHT[:, :], ZHT[:, :]), inc=(("ve", 1),))
            stats2(PRB[:, :], PB_S, wait=sch.w("ve"))
            sch.op("vector", lambda e: e.tensor_scalar_mul(
                ROWB[0:2, :], PB_S[0:2, :], 1.0 / 64),
                wait=sch.w("mm"), inc=(("ve", 1),))
            bcast2(ROWA[0:2, :], PB_A, wait=sch.w("ve"))
            bcast2(ROWB[0:2, :], PB_B)
            sch.op("vector", lambda e: e.tensor_mul(
                TS1[:, :], ZHT[:, :], PB_B[:, :]), wait=sch.w("mm"))
            sch.op("vector", lambda e: e.tensor_sub(
                TS2[:, :], DZHT[:, :], PB_A[:, :]))
            sch.op("vector", lambda e: e.tensor_sub(
                TS2[:, :], TS2[:, :], TS1[:, :]))
            if dz_dst_tile is not None:
                sch.op("vector", lambda e, d=dz_dst_tile: e.tensor_mul(
                    d, TS2[:, :], PB_C[:, :]))
            sch.op("vector", lambda e: e.tensor_mul(
                TS2[:, :], TS2[:, :], PB_C[:, :]))
            sch.op("vector", lambda e, d=dzn_tile: e.tensor_scalar_mul(
                d, TS2[:, :], -1.0), inc=(("ve", 1),))

        def transposes(src_tile, dstA, dstB, i):
            for jj in range(4):
                j = 4 * i + jj
                sch.op("tensor", lambda e, jj=jj: e.transpose(
                    PT[:, 0:64], src_tile[0:64, jj * 128:(jj + 1) * 128],
                    IDEN[0:64, 0:64]),
                    wait=sch.w("ve"), inc=(("mm", 1),))
                sch.op("vector", lambda e, j=j: e.tensor_copy(
                    dstA[:, j * 128:j * 128 + 64], PT[:, 0:64]),
                    wait=sch.w("mm"), inc=(("ve", 1),))
                sch.op("tensor", lambda e, jj=jj: e.transpose(
                    PT[:, 0:64], src_tile[64:128, jj * 128:(jj + 1) * 128],
                    IDEN[64:128, 64:128]),
                    wait=sch.w("ve"), inc=(("mm", 1),))
                sch.op("vector", lambda e, j=j: e.tensor_copy(
                    dstB[:, j * 128 + 64:j * 128 + 128], PT[:, 0:64]),
                    wait=sch.w("mm"), inc=(("ve", 1),))

        def triangle(i, t0, rhs2, DTA, DTB, first_lhs_rhs, kind=0):
            nj = 4 * i + 4
            nsl = 2
            tsn = {}
            _sm = getattr(sch, "serial_mode", False)
            if T4TSER or (kind == 1 and T4TSER1) or (kind == 2 and T4TSER2):
                sch.serial_mode = True
            sch.op("tensor", lambda e, t0=t0, fr=first_lhs_rhs: e.matmul(
                PU[:, :], W0B[:, :], fr[:, t0:t0 + 512],
                start=True, stop=False), wait=sch.w("ve", "ac"))
            for s in range(nj + 2):
                if s >= 2:
                    j = s - 2
                    sl = (j % nsl) * 1024
                    w = [("ve", tsn[("ev", j)][0])]
                    if kind != 1 or T4SPL1:
                        w.append(("ac", tsn[("ev", j)][1]))
                    for hh, DT in ((0, DTA), (1, DTB)):
                        so = sl + hh * 512
                        sch.op("tensor", lambda e, j=j, DT=DT, so=so: e.matmul(
                            PU[:, :], DT[:, j * 128:(j + 1) * 128],
                            KKE[:, so:so + 512],
                            start=False, stop=(j == nj - 1 and hh == 1)),
                            wait=(w if hh == 0 else ()),
                            inc=((("mm", 1),) if hh == 1 else ()))
                    tsn[("u", j)] = sch.cnt["mm"]
                if s >= 1 and s - 1 <= nj - 1:
                    j = s - 1
                    d4 = j - 4 * i
                    PKs = (PKA, PKB) if (j % nsl == 0) else (PB_A, PB_C)
                    sl = (j % nsl) * 1024
                    wv = [("mm", tsn[("kk", j)])]
                    if j >= nsl:
                        wv.append(("mm", tsn[("u", j - nsl)]))
                    wm = [("mm", max(v for s_, v in wv))]
                    dve1 = kind == 1 and not T4SPL1
                    if d4 >= 0:
                        sch.op("vector", lambda e, sl=sl, PK=PKs[0], d4=d4:
                               e.tensor_mul(KKE[:, sl:sl + 512], PK[:, :],
                                            MASK[:, d4 * 512:(d4 + 1) * 512]),
                               wait=wm)
                        sch.op("vector", lambda e, sl=sl, PK=PKs[1], d4=d4:
                               e.tensor_mul(KKE[:, sl + 512:sl + 1024], PK[:, :],
                                            MASK[:, d4 * 512:(d4 + 1) * 512]),
                               inc=(("ve", 1),))
                    elif dve1:
                        sch.op("vector", lambda e, sl=sl, PK=PKs[0]:
                               e.tensor_copy(KKE[:, sl:sl + 512], PK[:, :]),
                               wait=wm)
                        sch.op("vector", lambda e, sl=sl, PK=PKs[1]: e.tensor_copy(
                            KKE[:, sl + 512:sl + 1024], PK[:, :]),
                            inc=(("ve", 1),))
                    else:
                        sch.op("scalar", lambda e, sl=sl, PK=PKs[0]: e.copy(
                            KKE[:, sl:sl + 512], PK[:, :]),
                            wait=wm, inc=(("ac", 1),))
                        sch.op("vector", lambda e, sl=sl, PK=PKs[1]: e.tensor_copy(
                            KKE[:, sl + 512:sl + 1024], PK[:, :]),
                            wait=wm, inc=(("ve", 1),))
                    tsn[("ev", j)] = (sch.cnt["ve"], sch.cnt["ac"])
                if s <= nj - 1:
                    j = s
                    PKs = (PKA, PKB) if (j % nsl == 0) else (PB_A, PB_C)
                    w = []
                    if j >= nsl:
                        w = [("ve", tsn[("ev", j - nsl)][0])]
                        if kind != 1 or T4SPL1:
                            w.append(("ac", tsn[("ev", j - nsl)][1]))
                    for hh, PK, lo in ((0, PKs[0], 0), (1, PKs[1], 64)):
                        sch.op("tensor", lambda e, j=j, t0=t0, lo=lo, PK=PK,
                               rhs2=rhs2:
                               e.matmul(PK[:, :],
                                        K2[lo:lo + 64, j * 128:(j + 1) * 128],
                                        rhs2[lo:lo + 64, t0:t0 + 512],
                                        start=True, stop=True,
                                        tile_position=(lo, 0)),
                               wait=(w if hh == 0 else ()),
                               inc=((("mm", 1),) if hh == 1 else ()))
                    tsn[("kk", j)] = sch.cnt["mm"]
            sch.serial_mode = _sm

        for task in range(2):
            prev = sch.w("mm", "ve", "ac", "st")
            for tt in range(NTT):
                t0 = tt * 512
                for kc in range(NCH):
                    sch.dma("sync", XT[:, kc * 512:(kc + 1) * 512],
                            xs2f[task, kc, :, t0:t0 + 512],
                            wait=(prev + ldc if (tt == 0 and kc == 0)
                                  else sch.w("mm") if kc == 0 else ()))
                ldt = sch.cnt["ld"]
                for oi, DST in ((0, Q2), (1, K2), (2, V2)):
                    for kc in range(NCH):
                        sch.op("tensor", lambda e, oi=oi, kc=kc: e.matmul(
                            PU[:, :], WPB[:, kc * 384 + oi * 128:
                                          kc * 384 + (oi + 1) * 128],
                            XT[:, kc * 512:(kc + 1) * 512],
                            start=(kc == 0), stop=(kc == NCH - 1)),
                            wait=((("ld", ldt),) + (init_w if tt == 0 else ())
                                  + sch.w("ac") if kc == 0 else ()),
                            inc=((("mm", 1),) if kc == NCH - 1 else ()))
                    sch.op("scalar", lambda e, oi=oi, DST=DST, t0=t0:
                           e.activation(DST[:, t0:t0 + 512], PU[:, :],
                                        AF.Identity, bias=BP[:, oi:oi + 1]),
                           wait=sch.w("mm"), inc=(("ac", 1),))
            proj_done = sch.w("ac")
            if dbg and task == 0:
                for oi, SRC in ((0, Q2), (1, K2), (2, V2)):
                    sch.dma("sync", dbg_qkv[oi, :, :], SRC[:, :], sem="st",
                            wait=proj_done)

            for i in range(NTT):
                t0 = i * 512
                sch.op("tensor", lambda e, t0=t0: e.matmul(
                    PU[:, :], W0B[:, :], K2[:, t0:t0 + 512],
                    start=True, stop=True),
                    wait=(proj_done if i == 0 else sch.w("ac")) + sch.w("ve"),
                    inc=(("mm", 1),))
                sch.op("scalar", lambda e: e.copy(ZT[:, :], PU[:, :]),
                       wait=sch.w("mm"), inc=(("ac", 1),))
                dz_chain(ZT[:, :], DZ1[:, t0:t0 + 512], DZNT[:, :], t0)
                transposes(DZNT, D1TA, D1TB, i)
                triangle(i, t0, K2, D1TA, D1TB, K2, kind=1)
                sch.op("scalar", lambda e: e.copy(Z2T[:, :], PU[:, :]),
                       wait=sch.w("mm"), inc=(("ac", 1),))
                if dbg and task == 0 and i == 1:
                    sch.dma("sync", dbg_z2[:, :], Z2T[:, :], sem="st",
                            wait=sch.w("ac"))
                dz_chain(Z2T[:, :], None, DZNT[:, :], t0)
                sch.op("vector", lambda e, t0=t0: e.tensor_sub(
                    PRB[:, :], DZNT[:, :], DZ1[:, t0:t0 + 512]),
                    inc=(("ve", 1),))
                transposes(PRB, DSTA, DSTB, i)
                triangle(i, t0, Q2, DSTA, DSTB, Q2, kind=2)
                sch.op("scalar", lambda e, t0=t0: e.copy(
                    CTX[:, t0:t0 + 512], PU[:, :]),
                    wait=sch.w("mm"), inc=(("ac", 1),))

            if dbg and task == 0:
                sch.dma("sync", dbg_dz1[:, :], DZ1[:, :], sem="st",
                        wait=sch.w("ve", "ac"))
                sch.dma("sync", dbg_ctx[:, :], CTX[:, :], sem="st",
                        wait=sch.w("ve", "ac"))
            if T4DBG2 and task == 0:
                sch.dma("sync", dbg_dz1[:, :], DZ1[:, :], sem="st",
                        wait=sch.w("ve", "ac", "mm"))
                sch.dma("sync", dbg_ctx[:, :], CTX[:, :], sem="st",
                        wait=sch.w("ve", "ac", "mm"))
                sch.dma("sync", dbg_z2[:, :], Z2T[:, :], sem="st",
                        wait=sch.w("ve", "ac", "mm"))
            if dbg and task == 1:
                sch.dma("sync", dbg_ctx1[:, :], CTX[:, :], sem="st",
                        wait=sch.w("ve", "ac"))
            sch.serial_mode = True
            for i in range(NTT):
                t0 = i * 512
                sch.op("tensor", lambda e, t0=t0, i=i: e.matmul(
                    PB_A[0:2, 0:512], SEL2[:, :], CTX[:, t0:t0 + 512],
                    start=(i == 0), stop=(i == NTT - 1)),
                    wait=(sch.w("ac", "ve") if i == 0 else ()),
                    inc=((("mm", 1),) if i == NTT - 1 else ()))
            for i in range(NTT):
                t0 = i * 512
                sch.op("scalar", lambda e, t0=t0: e.activation(
                    SQB[:, :], CTX[:, t0:t0 + 512], AF.Square),
                    wait=sch.w("mm"), inc=(("ac", 1),))
                sch.op("tensor", lambda e, i=i: e.matmul(
                    PB_B[0:2, 0:512], SEL2[:, :], SQB[:, :],
                    start=(i == 0), stop=(i == NTT - 1)),
                    wait=sch.w("ac"), inc=(("mm", 1),))
            m = sch.w("mm")
            sch.op("vector", lambda e: e.reduce_sum(
                GSC[0:2, 0:1], PB_A[0:2, 0:512], axis=AX.X), wait=m)
            sch.op("vector", lambda e: e.reduce_sum(
                GSC[0:2, 1:2], PB_B[0:2, 0:512], axis=AX.X))
            NT64 = 1.0 / (64.0 * T)
            sch.op("vector", lambda e: e.tensor_scalar_mul(
                GSC[0:2, 0:1], GSC[0:2, 0:1], NT64))
            sch.op("vector", lambda e: e.tensor_scalar_mul(
                GSC[0:2, 1:2], GSC[0:2, 1:2], NT64))
            sch.op("vector", lambda e: e.tensor_mul(
                GSC[0:2, 2:3], GSC[0:2, 0:1], GSC[0:2, 0:1]))
            sch.op("vector", lambda e: e.tensor_sub(
                GSC[0:2, 1:2], GSC[0:2, 1:2], GSC[0:2, 2:3]),
                inc=(("ve", 1),))
            sch.op("scalar", lambda e: e.activation(
                GSC[0:2, 1:2], GSC[0:2, 1:2], AF.Sqrt, bias=EPSC[0:2, 0:1]),
                wait=sch.w("ve"), inc=(("ac", 1),))
            sch.op("vector", lambda e: e.reciprocal(
                GSC[0:2, 1:2], GSC[0:2, 1:2]), wait=sch.w("ac"),
                inc=(("ve", 1),))
            sch.op("tensor", lambda e: e.matmul(
                PB_S[:, 0:1], S2TF[0:2, :], GSC[0:2, 1:2],
                start=True, stop=True), wait=sch.w("ve"))
            sch.op("tensor", lambda e: e.matmul(
                PB_S[:, 1:2], S2TF[0:2, :], GSC[0:2, 0:1],
                start=False, stop=True), inc=(("mm", 1),))
            if dbg:
                sch.op("vector", lambda e: e.tensor_copy(
                    TS1[:, 0:2], PB_S[:, 0:2]), wait=sch.w("mm"),
                    inc=(("ve", 1),))
                sch.dma("sync", dbg_pbs[task, :, :], TS1[:, 0:2], sem="st",
                        wait=sch.w("ve"))
                if task == 0:
                    sch.dma("sync", dbg_lng[:, :], LNG[:, :], sem="st")
            sch.op("vector", lambda e: e.tensor_mul(
                GSC[:, 2:3], PB_S[:, 0:1], LNG[:, 2:3]), wait=sch.w("mm"))
            sch.op("vector", lambda e: e.tensor_mul(
                GSC[:, 3:4], PB_S[:, 1:2], GSC[:, 2:3]))
            sch.op("vector", lambda e: e.tensor_sub(
                GSC[:, 4:5], LNG[:, 3:4], GSC[:, 3:4]), inc=(("ve", 1),))
            gsc_done = sch.w("ve")
            if dbg:
                sch.dma("sync", dbg_gsc[task, :, 0:5], GSC[:, 0:5], sem="st",
                        wait=gsc_done)
            for i in range(NTT):
                t0 = i * 512
                sl = (i % 2) * 512
                stw = sch.cnt["st"] - 16 if i >= 2 else 0
                sch.op("scalar", lambda e, t0=t0, sl=sl: e.activation(
                    KKE[:, sl:sl + 512], CTX[:, t0:t0 + 512], AF.Identity,
                    bias=GSC[:, 4:5], scale=GSC[:, 2:3]),
                    wait=(gsc_done if i < 2 else (("st", stw),)),
                    inc=(("ac", 1),))
                sch.dma("sync", ctxout[task, :, t0:t0 + 512],
                        KKE[:, sl:sl + 512], sem="st", wait=sch.w("ac"))
            sch.serial_mode = False

        sch.op("gpsimd", lambda e: e.nop(), wait=(("st", sch.cnt["st"]),))
        sch.build_block()
    return nc



TE = TL + 1
T5TILES = [(0, 1), (1, 512), (513, 512), (1025, 512), (1537, 512)]


def build_launch5():
    nc = bass.Bass("TRN2", target_bir_lowering=False, debug=False)
    x2e = nc.dram_tensor("x2e", [NCH, 128, TE], F32, kind="ExternalInput").ap()
    ctxe = nc.dram_tensor("ctxe", [NCH, 128, TE], BF16, kind="ExternalInput").ap()
    wout = nc.dram_tensor("wout", [D, D], F32, kind="ExternalInput").ap()
    wrt = nc.dram_tensor("wrt", [D, D], F32, kind="ExternalInput").ap()
    wkt = nc.dram_tensor("wkt", [D, D], F32, kind="ExternalInput").ap()
    wvt = nc.dram_tensor("wvt", [D, D], F32, kind="ExternalInput").ap()
    bo_c = nc.dram_tensor("bo_c", [128, 8], F32, kind="ExternalInput").ap()
    mur_c = nc.dram_tensor("mur_c", [128, 8], F32, kind="ExternalInput").ap()
    muk_c = nc.dram_tensor("muk_c", [128, 8], F32, kind="ExternalInput").ap()
    bmask = nc.dram_tensor("bmask", [128, 1], F32, kind="ExternalInput").ap()
    idenf = nc.dram_tensor("idenf", [128, 128], F32, kind="ExternalInput").ap()
    out = nc.dram_tensor("out", [TL, D], F32, kind="ExternalOutput").ap()

    with ExitStack() as es:
        sch = Sched(nc, es)
        sch.serial_mode = L5SER
        sch.sem("ld", "lg", "mm", "ve", "ac", "st")
        X3 = [_mk(nc, es, f"x3{c}", [128, TE], F32) for c in range(NCH)]
        WB = _mk(nc, es, "wb", [128, NCH * 128], BF16)
        BO = _mk(nc, es, "bo", [128, 8], F32)
        MUR = _mk(nc, es, "mur", [128, 8], F32)
        MUK = _mk(nc, es, "muk", [128, 8], F32)
        OMR = _mk(nc, es, "omr", [128, 8], F32)
        OMK = _mk(nc, es, "omk", [128, 8], F32)
        BM = _mk(nc, es, "bm", [128, 1], F32)
        ONESB = _mk(nc, es, "onesb", [128, 1], BF16)
        ONESR = _mk(nc, es, "onesr", [1, 512], F32)
        EPSC = _mk(nc, es, "epsc", [128, 1], F32)
        MU = _mk(nc, es, "mu", [1, TE], F32)
        Q2 = _mk(nc, es, "q2", [1, TE], F32)
        RR = _mk(nc, es, "rr", [1, TE], F32)
        NMR = _mk(nc, es, "nmr", [1, TE], F32)
        SQS = _mk(nc, es, "sqs", [128, 512], BF16)
        XBS = _mk(nc, es, "xbs", [128, 512], BF16)
        TMP = _mk(nc, es, "tmp", [128, 512], F32)
        P0 = _ps(nc, es, "p0", [128, 512])
        P1 = _ps(nc, es, "p1", [128, 512])
        P2 = _ps(nc, es, "p2", [128, 512])
        PT = _ps(nc, es, "pt", [128, 128])

        sch.op("vector", lambda e: e.memset(ONESB[:, :], 1.0))
        sch.op("vector", lambda e: e.memset(EPSC[:, :], EPS))
        sch.op("vector", lambda e: e.memset(ONESR[:, :], 1.0), inc=(("ve", 1),))
        for c in range(NCH):
            sch.dma("sync", X3[c][:, :], x2e[c, :, :])
        for name, dst, srcd in (("bo", BO, bo_c), ("mur", MUR, mur_c),
                                ("muk", MUK, muk_c), ("bm", BM, bmask)):
            sch.dma("sync", dst[:, :], srcd[:, :])
        ld0 = sch.cnt["ld"]
        sch.op("vector", lambda e: e.tensor_scalar(
            OMR[:, :], MUR[:, :], -1.0, 1.0, ALU.mult, ALU.add),
            wait=(("ld", ld0),))
        sch.op("vector", lambda e: e.tensor_scalar(
            OMK[:, :], MUK[:, :], -1.0, 1.0, ALU.mult, ALU.add),
            inc=(("ve", 1),))

        def proj(wdram, rhs_chunks, evac, nt_tiles, extra_wait=()):
            for dc in range(NCH):
                prev_mm = sch.cnt["mm"]
                for kc in range(NCH):
                    sch.dma("gpsimd", WB[:, kc * 128:(kc + 1) * 128],
                            wdram[kc * 128:(kc + 1) * 128,
                                  dc * 128:(dc + 1) * 128], sem="lg",
                            wait=((("mm", prev_mm),) if kc == 0 else ()))
                ldw = sch.cnt["lg"]
                for t0, tw in nt_tiles:
                    for kc in range(NCH):
                        first = t0 == nt_tiles[0][0] and kc == 0
                        sch.op("tensor", lambda e, t0=t0, tw=tw, kc=kc:
                               e.matmul(P0[:, 0:tw],
                                        WB[:, kc * 128:(kc + 1) * 128],
                                        rhs_chunks[kc][:, t0:t0 + tw],
                                        start=(kc == 0), stop=(kc == NCH - 1)),
                               wait=(((("lg", ldw),) + extra_wait + sch.w("ve"))
                                     if first else sch.w("ac") if kc == 0
                                     else ()),
                               inc=((("mm", 1),) if kc == NCH - 1 else ()))
                    evac(dc, t0, tw)

        with ExitStack() as es2:
            CTXS = [_mk(nc, es2, f"ctxs{c}", [128, TE], BF16) for c in range(NCH)]
            OB = [_mk(nc, es2, f"ob{c}", [128, TE], BF16) for c in range(NCH)]
            for c in range(NCH):
                sch.dma("gpsimd", CTXS[c][:, :], ctxe[c, :, :], sem="lg")
            ldc = sch.cnt["lg"]

            def evac_o(dc, t0, tw):
                sch.op("scalar", lambda e, dc=dc, t0=t0, tw=tw: e.activation(
                    OB[dc][:, t0:t0 + tw], P0[:, 0:tw], AF.Identity,
                    bias=BO[:, dc:dc + 1]),
                    wait=sch.w("mm"), inc=(("ac", 1),))

            proj(wout, CTXS, evac_o, T5TILES,
                 extra_wait=(("lg", ldc), ("ld", ld0)))
            emit_stats_1024(sch, OB, ONESB, P0, MU, Q2, TE, SQS,
                            first_wait=sch.w("ac"))
            emit_rows_rstd(sch, MU, Q2, RR, NMR, Q2, TE, double_norm=False,
                           epsc=EPSC)
            emit_neg_murr(sch, MU, RR, NMR, TE)
            emit_ln_apply(sch, OB, X3, RR, NMR, TE, P1, P2, TMP, ONESR,
                          add_chunks=X3)

        with ExitStack() as es3:
            XRK = [_mk(nc, es3, f"xrk{c}", [128, TL], BF16) for c in range(NCH)]
            R = [_mk(nc, es3, f"r{c}", [128, TL], BF16) for c in range(NCH)]
            with ExitStack() as es4:
                XS3 = [_mk(nc, es4, f"xs3{c}", [128, TE], BF16)
                       for c in range(NCH)]
                emit_stats_1024(sch, X3, ONESB, P0, MU, Q2, TE, SQS, xbs=XBS)
                emit_rows_rstd(sch, MU, Q2, RR, NMR, Q2, TE, double_norm=False,
                               epsc=EPSC)
                emit_neg_murr(sch, MU, RR, NMR, TE)
                emit_ln_apply(sch, X3, XS3, RR, NMR, TE, P1, P2, TMP, ONESR)
                for c in range(NCH):
                    sch.op("vector", lambda e, c=c: e.tensor_scalar(
                        SQS[:, 0:1], XS3[c][:, 0:1], BM[:, 0:1], None,
                        ALU.mult),
                        wait=(sch.w("ve") if c == 0 else ()))
                    sch.op("vector", lambda e, c=c: e.tensor_copy(
                        XS3[c][:, 0:1], SQS[:, 0:1]),
                        inc=((("ve", 1),) if c == NCH - 1 else ()))

                def mix(dst, mcol, ocol):
                    for c in range(NCH):
                        for t0 in range(0, TL, 512):
                            sch.op("vector", lambda e, c=c, t0=t0:
                                   e.tensor_scalar(
                                       dst[c][:, t0:t0 + 512],
                                       XS3[c][:, 1 + t0:1 + t0 + 512],
                                       mcol[:, c:c + 1], None, ALU.mult),
                                   wait=(sch.w("ve", "mm") if t0 == 0 and c == 0
                                         else ()))
                            sch.op("vector", lambda e, c=c, t0=t0:
                                   e.tensor_scalar(
                                       SQS[:, :], XS3[c][:, t0:t0 + 512],
                                       ocol[:, c:c + 1], None, ALU.mult))
                            sch.op("vector", lambda e, c=c, t0=t0:
                                   e.tensor_add(dst[c][:, t0:t0 + 512],
                                                dst[c][:, t0:t0 + 512],
                                                SQS[:, :]),
                                   inc=(("ve", 1),))

                mix(XRK, MUR, OMR)

                def evac_r(dc, t0, tw):
                    sch.op("scalar", lambda e, dc=dc, t0=t0, tw=tw:
                           e.activation(R[dc][:, t0:t0 + tw], P0[:, 0:tw],
                                        AF.Sigmoid),
                           wait=sch.w("mm"), inc=(("ac", 1),))

                tiles4 = [(i * 512, 512) for i in range(4)]
                proj(wrt, XRK, evac_r, tiles4)
                mix(XRK, MUK, OMK)
            with ExitStack() as es5:
                KA = [_mk(nc, es5, f"ka{c}", [128, TL], BF16) for c in range(NCH)]

                def evac_k(dc, t0, tw):
                    sch.op("scalar", lambda e, dc=dc, t0=t0, tw=tw:
                           e.activation(TMP[:, 0:tw], P0[:, 0:tw], AF.Relu),
                           wait=sch.w("mm"), inc=(("ac", 1),))
                    sch.op("scalar", lambda e, dc=dc, t0=t0, tw=tw:
                           e.activation(KA[dc][:, t0:t0 + tw], TMP[:, 0:tw],
                                        AF.Square),
                           inc=(("ac", 1),))

                proj(wkt, XRK, evac_k, tiles4)

                def evac_v(dc, t0, tw):
                    sch.op("vector", lambda e, dc=dc, t0=t0, tw=tw:
                           e.tensor_mul(XRK[dc][:, t0:t0 + tw],
                                        R[dc][:, t0:t0 + tw], P0[:, 0:tw]),
                           wait=sch.w("mm"), inc=(("ac", 1),))

                proj(wvt, KA, evac_v, tiles4)
                emit_stats_1024(sch, XRK, ONESB, P0, MU, Q2, TL, SQS,
                                first_wait=sch.w("ac"))
                emit_rows_rstd(sch, MU, Q2, RR, NMR, Q2, TL, double_norm=False,
                               epsc=EPSC)
                emit_neg_murr(sch, MU, RR, NMR, TL)
                X3s = [X3[c][:, 1:TE] for c in range(NCH)]
                emit_ln_apply(sch, XRK, X3s, RR, NMR, TL, P1, P2, TMP, ONESR,
                              add_chunks=X3s)

        OSTG = _mk(nc, es, "ostg", [128, D], F32)
        IDF = _mk(nc, es, "idf", [128, 128], F32)
        sch.dma("sync", IDF[:, :], idenf[:, :], wait=sch.w("ve", "ac", "mm"))
        ldi = sch.cnt["ld"]
        sch.op("tensor", lambda e: e.nop(), wait=(("ld", ldi),))
        for j in range(TL // 128):
            for c in range(NCH):
                sch.op("tensor", lambda e, j=j, c=c: e.transpose(
                    PT[:, :], X3[c][:, 1 + j * 128:1 + (j + 1) * 128],
                    IDF[:, :]),
                    wait=sch.w("ve"), inc=(("mm", 1),))
                sch.op("vector", lambda e, c=c: e.tensor_copy(
                    OSTG[:, c * 128:(c + 1) * 128], PT[:, :]),
                    wait=sch.w("mm"), inc=(("ve", 1),))
            stp = sch.cnt["st"]
            sch.dma("sync", out[j * 128:(j + 1) * 128, :], OSTG[:, :],
                    sem="st", wait=sch.w("ve"))
            sch.op("vector", lambda e: e.nop(), wait=(("st", stp + 16),))

        sch.op("gpsimd", lambda e: e.nop(), wait=(("st", sch.cnt["st"]),))
        sch.build_block()
    return nc


def build_launch2_noveng():
    nc = bass.Bass("TRN2", target_bir_lowering=False, debug=False)
    q2in = nc.dram_tensor("q2in", [2, 128, T], BF16, kind="ExternalInput").ap()
    k2in = nc.dram_tensor("k2in", [2, 128, T], BF16, kind="ExternalInput").ap()
    v1in = nc.dram_tensor("v1in", [2, 2, 128, 64, 65], BF16,
                          kind="ExternalInput").ap()
    yout = nc.dram_tensor("yout", [2, 2, 64, T], BF16, kind="ExternalOutput").ap()
    NKT = T // 128
    NTT = T // 512

    with ExitStack() as es:
        sch = Sched(nc, es)
        sch.sem("ld", "lg", "mm", "ve", "ac", "st")
        Q2 = _mk(nc, es, "q2", [128, T], BF16)
        K2 = _mk(nc, es, "k2", [128, T], BF16)
        V1 = [_mk(nc, es, f"v1{h}", [128, 64 * 65], BF16) for h in range(2)]
        SQT = _mk(nc, es, "sqt", [128, T], BF16)
        PA = [_mk(nc, es, f"pa{s}", [128, 1024], BF16) for s in range(8)]
        NQ = _mk(nc, es, "nq", [1, T], F32)
        SC = _mk(nc, es, "sc", [1, 16], F32)
        NB = _mk(nc, es, "nb", [128, 2], F32)
        OE = _mk(nc, es, "oe", [128, 512], F32)
        RE = _mk(nc, es, "re", [128, 512], F32)
        YH = [_mk(nc, es, f"yh{h}", [64, T], BF16) for h in range(2)]
        ONESB = _mk(nc, es, "onesb", [128, 1], BF16)
        ONESF = _mk(nc, es, "onesf", [128, 128], F32)
        PS = [_ps(nc, es, f"ps{s}", [128, 1024]) for s in range(3)]
        PO_A = _ps(nc, es, "poa", [128, 512])
        PO_B = _ps(nc, es, "pob", [128, 512])
        PR = PS[0]

        sch.op("vector", lambda e: e.memset(ONESB[:, :], 1.0))
        sch.op("vector", lambda e: e.memset(ONESF[:, :], 1.0), inc=(("ve", 1),))

        for task in range(2):
            w_prev = sch.w("mm", "ac", "st")
            sch.dma("sync", Q2[:, :], q2in[task, :, :], wait=w_prev)
            sch.dma("sync", K2[:, :], k2in[task, :, :])
            for h in range(2):
                sch.dma("sync", V1[h][:, :],
                        v1in[task, h, :, :, :].rearrange("p a b -> p (a b)"))
            ld0 = sch.cnt["ld"]

            for which, SRC in enumerate((Q2, K2)):
                sch.op("scalar", lambda e, SRC=SRC: e.activation(
                    SQT[:, :], SRC[:, :], AF.Square),
                    wait=(("ld", ld0),) + sch.w("mm"), inc=(("ac", 1),))
                for h in range(2):
                    lo = 64 * h
                    nsn = {}
                    for tt in range(NTT):
                        bank = PS[tt % 2][:, 0:512]
                        w = sch.w("ac", "ve") if tt == 0 else (
                            ("ve", nsn[tt - 2]),) if tt >= 2 else ()
                        sch.op("tensor",
                               lambda e, lo=lo, tt=tt, bank=bank: e.matmul(
                                   bank[0:1, :], ONESB[lo:lo + 64, 0:1],
                                   SQT[lo:lo + 64, tt * 512:(tt + 1) * 512],
                                   start=True, stop=True),
                               wait=w, inc=(("mm", 1),))
                        msn = sch.cnt["mm"]
                        sch.op("vector", lambda e, tt=tt, bank=bank:
                               e.tensor_copy(NQ[0:1, tt * 512:(tt + 1) * 512],
                                             bank[0:1, :]),
                               wait=(("mm", msn),), inc=(("ve", 1),))
                        nsn[tt] = sch.cnt["ve"]
                    j = which * 4 + h
                    sch.op("vector", lambda e, j=j: e.reduce_max(
                        SC[0:1, j:j + 1], NQ[0:1, 0:T], axis=AX.X),
                        inc=(("ve", 1),))
            sch.op("scalar", lambda e: e.activation(SC[0:1, 8:14], SC[0:1, 0:6],
                                                    AF.Sqrt),
                   wait=sch.w("ve"), inc=(("ac", 1),))
            for h in range(2):
                sch.op("vector", lambda e, h=h: e.tensor_mul(
                    SC[0:1, 12 + h:13 + h], SC[0:1, 8 + h:9 + h],
                    SC[0:1, 12 + h:13 + h]), wait=sch.w("ac"),
                    inc=(("ve", 1),))
            sch.op("vector", lambda e: e.tensor_max(
                SC[0:1, 14:15], SC[0:1, 12:13], SC[0:1, 13:14]))
            sch.op("vector", lambda e: e.tensor_scalar_mul(
                SC[0:1, 15:16], SC[0:1, 14:15], -0.125), inc=(("ve", 1),))
            sch.op("tensor", lambda e: e.matmul(
                PR[:, 0:1], ONESF[0:1, 0:128], SC[0:1, 15:16],
                start=True, stop=True),
                wait=sch.w("ve"), inc=(("mm", 1),))
            sch.op("vector", lambda e: e.tensor_copy(
                NB[:, 0:1], PR[:, 0:1]),
                wait=sch.w("mm"), inc=(("ve", 1),))
            bnd = sch.w("ve")

            snap = {}
            KG = [list(range(g, min(g + 2, NKT))) for g in range(0, NKT, 2)]
            macros = [(h, kcs) for kcs in KG for h in (0, 1)]
            NM = len(macros)
            for ti in range(NTT):
                t0 = ti * 512
                for step in range(NM + 3):
                    if NM - 1 + 2 >= step >= 2:
                        mi = ti * NM + step - 2
                        sl = mi % 3
                        pl = mi % 8
                        nk = len(macros[step - 2][1]) * 512
                        v = snap[("S", mi)]
                        if mi >= 8:
                            v = max(v, snap[("PV", mi - 8)])
                        sch.op("scalar", lambda e: e.nop(),
                               wait=(("mm", v),), inc=(("ac", 1),))
                        snap[("exp", mi)] = sch.cnt["ac"]
                    if step <= NM - 1:
                        mi = ti * NM + step
                        h, kcs = macros[step]
                        lo = 64 * h
                        sl = mi % 3
                        w = []
                        if mi >= 3:
                            w.append(("ac", snap[("exp", mi - 3)]))
                        if step == 0 and ti > 0:
                            w.append(("ve", snap[("fin", ti - 1)]))
                        if ti == 0 and step == 0:
                            w = list(bnd)
                        for q, kc in enumerate(kcs):
                            sch.op("tensor",
                                   lambda e, sl=sl, kc=kc, t0=t0, q=q, lo=lo:
                                   e.matmul(PS[sl][:, q * 512:(q + 1) * 512],
                                            K2[lo:lo + 64,
                                               kc * 128:(kc + 1) * 128],
                                            Q2[lo:lo + 64, t0:t0 + 512],
                                            start=True, stop=True),
                                   wait=(w if q == 0 else ()),
                                   inc=((("mm", 1),) if q == len(kcs) - 1
                                        else ()))
                        snap[("S", mi)] = sch.cnt["mm"]
                    if step >= 3:
                        mi = ti * NM + step - 3
                        h, kcs = macros[step - 3]
                        pl = mi % 8
                        PO = PO_A if h == 0 else PO_B
                        for q, kc in enumerate(kcs):
                            sch.op("tensor", lambda e, pl=pl, kc=kc, q=q, PO=PO:
                                   e.matmul(PO[0:65, :],
                                            V1[0 if PO is PO_A else 1]
                                            [:, kc * 65:(kc + 1) * 65],
                                            PA[pl][:, q * 512:(q + 1) * 512],
                                            start=(kc == 0),
                                            stop=(kc == NKT - 1)),
                                   wait=((("ac", snap[("exp", mi)]),)
                                         if q == 0 else ()),
                                   inc=((("mm", 1),) if q == len(kcs) - 1
                                        else ()))
                        snap[("PV", mi)] = sch.cnt["mm"]
                sch.op("tensor", lambda e: e.nop(), wait=sch.w("mm"),
                       inc=(("ve", 1),))
                snap[("fin", ti)] = sch.cnt["ve"]
            for h in range(2):
                sch.dma("sync", yout[task, h, :, :], YH[h][0:64, :], sem="st",
                        wait=sch.w("ve"))

        sch.op("gpsimd", lambda e: e.nop(), wait=(("st", sch.cnt["st"]),))
        sch.build_block()
    return nc


